# version 07
# baseline (speedup 1.0000x reference)
import numpy as np
import ml_dtypes
from contextlib import ExitStack

import concourse.bass as bass
import concourse.mybir as mybir
from concourse.bass_utils import run_bass_kernel_spmd

F32 = mybir.dt.float32
BF16 = mybir.dt.bfloat16
AF = mybir.ActivationFunctionType
ALU = mybir.AluOpType
AX = mybir.AxisListType

B, S, D, I, H, DH = 16, 2048, 2048, 8192, 16, 128
NCORES = 8
BC = B // NCORES
NT = BC * S
TG = 512
NTG = NT // TG
NCH = TG // 128
NIT = I // 512
NC16 = I // 128
CSC = float(1.0 / np.sqrt(DH))
MASKVAL = -1e9

ENGS = ("tensor", "vector", "scalar", "sync", "gpsimd")
DMA_SEM = {"sync": "qs", "gpsimd": "qg", "scalar": "qa"}


class Em:

    NLANE = 8

    def __init__(self, nc, stack):
        self.nc = nc
        self.streams = {e: [] for e in ENGS}
        self.sems = {}
        self.counts = {}
        names = ["pe", "dve", "act", "pool", "jn"]
        for q in ("qs", "qg", "qa"):
            names += [f"{q}{i}" for i in range(self.NLANE)]
        for name in names:
            self.sems[name] = stack.enter_context(nc.semaphore(name))
            self.counts[name] = 0
        self.qidx = {"qs": 0, "qg": 0, "qa": 0}
        self.waited = {e: {} for e in ENGS}
        self.njoin = 0

    def _emit_wait(self, eng, sem, val):
        if val is None or val <= 0:
            return
        if self.waited[eng].get(sem, 0) >= val:
            return
        self.waited[eng][sem] = val
        h = self.sems[sem]
        self.streams[eng].append(lambda e, h=h, v=val: e.wait_ge(h, v))

    def _coalesce(self, waits, skip_eng=None):
        best = {}
        for t in waits:
            if t is None:
                continue
            sem, val, prod = t
            if prod == skip_eng:
                continue
            if best.get(sem, 0) < val:
                best[sem] = val
        return best

    def op(self, eng, fn, waits=(), inc=None, fwaits=()):
        for sem, val in self._coalesce(waits, skip_eng=eng).items():
            self._emit_wait(eng, sem, val)
        for sem, val in self._coalesce(fwaits).items():
            self._emit_wait(eng, sem, val)
        tok = None
        if inc:
            self.counts[inc] += 1
            v = self.counts[inc]
            h = self.sems[inc]
            self.streams[eng].append(lambda e, fn=fn, h=h, v=v: fn(e).then_inc(h, 1))
            tok = (inc, v, eng)
        else:
            self.streams[eng].append(lambda e, fn=fn: fn(e))
        return tok

    def dma(self, eng, out, in_, waits=(), **kw):
        for sem, val in self._coalesce(waits).items():
            self._emit_wait(eng, sem, val)
        q = DMA_SEM[eng]
        lane = f"{q}{self.qidx[q] % self.NLANE}"
        self.qidx[q] += 1
        self.counts[lane] += 16
        v = self.counts[lane]
        h = self.sems[lane]
        self.streams[eng].append(
            lambda e, o=out, i=in_, h=h, v=v, kw=kw: e.dma_start(out=o, in_=i, **kw).then_inc(h, 16))
        return (lane, v, "dma")

    def join(self):
        for eng in ("sync", "gpsimd", "scalar"):
            q = DMA_SEM[eng]
            for i in range(self.NLANE):
                self._emit_wait(eng, f"{q}{i}", self.counts[f"{q}{i}"])
        self.njoin += 1
        target = 5 * self.njoin
        h = self.sems["jn"]
        for eng in ENGS:
            self.streams[eng].append(lambda e, h=h: e.sem_inc(h, 1))
            self.streams[eng].append(lambda e, h=h, t=target: e.wait_ge(h, t))
            self.waited[eng] = {}

    def finalize(self, block):
        for eng in ENGS:
            stream = self.streams[eng]

            def runner(e, stream=stream):
                for fn in stream:
                    fn(e)
            getattr(block, eng)(runner)


def ceil_div(a, b):
    return (a + b - 1) // b


def ecopy(e, dst, src):
    if hasattr(e, "tensor_copy"):
        return e.tensor_copy(out=dst, in_=src)
    return e.copy(dst, src)


def build_program(parts=("ffn1", "attn", "ffn2"), dbg=(), paranoid=0):
    nc = bass.Bass("TRN2", target_bir_lowering=False)
    stack = ExitStack()

    x_in = nc.dram_tensor("x_in", [NT, D], F32, kind="ExternalInput").ap()
    y_out = nc.dram_tensor("y_out", [NT, D], F32, kind="ExternalOutput").ap()
    wq = nc.dram_tensor("wq", [D, D], F32, kind="ExternalInput").ap()
    wk = nc.dram_tensor("wk", [D, D], F32, kind="ExternalInput").ap()
    wv = nc.dram_tensor("wv", [D, D], F32, kind="ExternalInput").ap()
    bq = nc.dram_tensor("bq", [D], F32, kind="ExternalInput").ap()
    bk = nc.dram_tensor("bk", [D], F32, kind="ExternalInput").ap()
    bv = nc.dram_tensor("bv", [D], F32, kind="ExternalInput").ap()
    wg = nc.dram_tensor("wg", [D, I], F32, kind="ExternalInput").ap()
    wu = nc.dram_tensor("wu", [D, I], F32, kind="ExternalInput").ap()
    wd = nc.dram_tensor("wd", [I, D], F32, kind="ExternalInput").ap()

    xr = nc.dram_tensor("xr", [NT, D], F32, kind="Internal").ap()
    wg_s = nc.dram_tensor("wg_s", [NIT, 128, 16, 512], BF16, kind="Internal").ap()
    wu_s = nc.dram_tensor("wu_s", [NIT, 128, 16, 512], BF16, kind="Internal").ap()
    wd_s = nc.dram_tensor("wd_s", [NC16, 128, 16, 128], BF16,
                          kind="ExternalOutput" if "wdout" in parts else "Internal").ap()
    wq_s = nc.dram_tensor("wq_s", [D, D], BF16, kind="Internal").ap()
    wk_s = nc.dram_tensor("wk_s", [D, D], BF16, kind="Internal").ap()
    wv_s = nc.dram_tensor("wv_s", [D, D], BF16, kind="Internal").ap()

    em = Em(nc, stack)
    sb = lambda name, shape, dt: stack.enter_context(nc.sbuf_tensor(name, shape, dt))
    ps = lambda name, shape, dt: stack.enter_context(nc.psum_tensor(name, shape, dt))

    dbg_t = {}
    def dbg_dump(name, shape, sbuf_ap, waits):
        if name not in dbg:
            return
        if name not in dbg_t:
            dbg_t[name] = nc.dram_tensor("dbg_" + name, shape, F32, kind="ExternalOutput").ap()
        em.dma("gpsimd", dbg_t[name], sbuf_ap, waits=waits)

    ready = {"consts": None, "bias": None, "wgu": None, "qkv": None, "wd": None}

    pSh = ps("pSh", [128, 2048], F32)
    paf = lambda i, kc=512: pSh[:, i * 512: i * 512 + kc]
    pB = [ps(f"pB{i}", [128, 512], F32) for i in range(2)]
    pT = [ps(f"pT{i}", [128, 1024], BF16) for i in range(2)]

    idb = sb("idb", [128, 128], BF16)
    idf = sb("idf", [128, 128], F32)
    mtri = sb("mtri", [128, 128], F32)
    eps_t = sb("eps_t", [128, 1], F32)
    ones_r = sb("ones_r", [65, 512], BF16)
    bias3 = sb("bias3", [65, D], BF16)
    bq_r = bias3[0:1, :]
    bk_r = bias3[32:33, :]
    bv_r = bias3[64:65, :]
    stt = sb("stt", [128, 64], F32)

    def setup_consts():
        em.op("gpsimd", lambda e: e.memset(idb[:], 0.0))
        em.op("gpsimd", lambda e: e.affine_select(
            out=idb[:], in_=idb[:], compare_op=ALU.not_equal, fill=1.0,
            base=0, pattern=[[-1, 128]], channel_multiplier=1))
        em.op("gpsimd", lambda e: e.memset(idf[:], 0.0))
        em.op("gpsimd", lambda e: e.affine_select(
            out=idf[:], in_=idf[:], compare_op=ALU.not_equal, fill=1.0,
            base=0, pattern=[[-1, 128]], channel_multiplier=1))
        em.op("gpsimd", lambda e: e.memset(mtri[:], 0.0))
        em.op("gpsimd", lambda e: e.affine_select(
            out=mtri[:], in_=mtri[:], compare_op=ALU.is_ge, fill=MASKVAL,
            base=0, pattern=[[-1, 128]], channel_multiplier=1))
        em.op("gpsimd", lambda e: e.memset(eps_t[:], 1e-5))
        t = em.op("gpsimd", lambda e: e.memset(ones_r[:], 1.0), inc="pool")
        ready["consts"] = t
        em.dma("gpsimd", bq_r, bq[None, :])
        em.dma("gpsimd", bk_r, bk[None, :])
        ready["bias"] = em.dma("gpsimd", bv_r, bv[None, :])

    def preconvert():
        t = None
        for src, dst in ((wg, wg_s), (wu, wu_s)):
            for it in range(NIT):
                for ch in range(2):
                    c0 = ch * 8
                    sap = bass.AP(src.tensor, c0 * 128 * I + it * 512,
                                  [[128 * I, 8], [I, 128], [1, 512]])
                    dap = dst[it][:, c0:c0 + 8, :].rearrange("p c n -> c p n")
                    t = em.dma("gpsimd", dap, sap)
        ready["wgu"] = t
        for src_, dst in ((wq, wq_s), (wk, wk_s), (wv, wv_s)):
            for ch in range(4):
                r0 = ch * 512
                sap = bass.AP(src_.tensor, r0 * D, [[D, 512], [1, D]])
                dap = bass.AP(dst.tensor, r0 * D, [[D, 512], [1, D]])
                t = em.dma("gpsimd", dap, sap)
        ready["qkv"] = t
        pre_s = ExitStack()
        wdbuf = [pre_s.enter_context(nc.sbuf_tensor(f"wdc{i}", [128, 2048], BF16))
                 for i in range(2)]
        free_tok = [None, None]
        for cp in range(NC16):
            slot = cp % 2
            base = (cp // 4) * 512 + (cp % 4)
            sap = bass.AP(wd.tensor, base * D, [[4 * D, 128], [1, D]])
            ld = em.dma("gpsimd", wdbuf[slot][:], sap, waits=(free_tok[slot],))
            dap = bass.AP(wd_s.tensor, cp * (128 * 16 * 128), [[2048, 128], [1, 2048]])
            st_tok = em.dma("sync", dap, wdbuf[slot][:], waits=(ld,))
            free_tok[slot] = st_tok
        ready["wd"] = st_tok
        pre_s.close()

    pt_free = [None, None]
    pA_free = [None, None, None, None]
    pB_free = [None, None]

    def ln_chunk(src_ap, row0, tc, xnT_buf, xbuf, xnbuf, xb_free, xn_free, par, sq_scr):
        sc = stt[:, par * 16: par * 16 + 16]
        ld = em.dma("sync", xbuf[:], src_ap[row0:row0 + 128, :], waits=(xb_free,))
        t0 = em.op("vector", lambda e: e.reduce_sum(sc[:, 0:1], xbuf[:], axis=AX.X),
                   waits=(ld,), inc="dve")
        t_sq = em.op("vector", lambda e: e.scalar_tensor_tensor(
                         out=sq_scr[:], in0=xbuf[:], scalar=1.0, in1=xbuf[:],
                         op0=ALU.mult, op1=ALU.mult, accum_out=sc[:, 1:2]),
                     waits=(ld,), inc="dve")
        t1 = em.op("vector", lambda e: e.tensor_scalar_mul(sc[:, 2:3], sc[:, 0:1], 1.0 / D),
                   fwaits=(t0,), inc="dve")
        t2 = em.op("vector", lambda e: e.tensor_tensor(sc[:, 3:4], sc[:, 2:3], sc[:, 2:3], op=ALU.mult),
                   fwaits=(t1,), inc="dve")
        t3 = em.op("vector", lambda e: e.tensor_scalar_mul(sc[:, 4:5], sc[:, 1:2], 1.0 / D),
                   fwaits=(t_sq,), inc="dve")
        t_var = em.op("vector", lambda e: e.tensor_tensor(sc[:, 5:6], sc[:, 4:5], sc[:, 3:4], op=ALU.subtract),
                      fwaits=(t2, t3), inc="dve")
        t_sd = em.op("scalar", lambda e: e.activation(sc[:, 6:7], sc[:, 5:6], AF.Sqrt, bias=eps_t[:]),
                     waits=(t_var, ready["consts"]), inc="act")
        t5 = em.op("vector", lambda e: e.reciprocal(sc[:, 7:8], sc[:, 6:7]),
                   waits=(t_sd,), inc="dve")
        t6 = em.op("vector", lambda e: e.tensor_tensor(sc[:, 8:9], sc[:, 2:3], sc[:, 7:8], op=ALU.mult),
                   fwaits=(t5,), inc="dve")
        t_bias = em.op("vector", lambda e: e.tensor_scalar_mul(sc[:, 9:10], sc[:, 8:9], -1.0),
                       fwaits=(t6,), inc="dve")
        t_xn = em.op("scalar", lambda e: e.activation(xnbuf[:], xbuf[:], AF.Identity,
                                                      scale=sc[:, 7:8], bias=sc[:, 9:10]),
                     waits=(t_bias, xn_free), inc="act")
        evac_toks = []
        for bnk in range(4):
            pbk = bnk % 2
            pb = pT[pbk]
            ptok = None
            for j in range(4):
                c = bnk * 4 + j
                w = [t_xn, pt_free[pbk], ready["consts"]] if j == 0 else []
                ptok = em.op("tensor",
                             lambda e, c=c, pb=pb, j=j: e.transpose(
                                 pb[:, j * 128:(j + 1) * 128],
                                 xnbuf[:, c * 128:(c + 1) * 128], idb[:]),
                             waits=w, inc="pe" if j == 3 else None)
            dst = xnT_buf[:, tc * 2048 + bnk * 512: tc * 2048 + (bnk + 1) * 512]
            ev_eng = "vector" if bnk % 2 == 0 else "scalar"
            ev = em.op(ev_eng, lambda e, pb=pb, dst=dst: ecopy(e, dst, pb[:, 0:512]),
                       waits=(ptok,), inc="dve" if ev_eng == "vector" else "act")
            pt_free[pbk] = ev
            evac_toks.append(ev)
        return evac_toks

    def ffn(src_ap, dst_ap, tag):
        fs = ExitStack()
        fsb = lambda name, shape, dt: fs.enter_context(nc.sbuf_tensor(tag + name, shape, dt))
        xnT = fsb("xnT", [128, NCH * 2048], BF16)
        hT = fsb("hT", [128, NCH * 8192], BF16)
        wsl = [fsb(f"w{i}", [128, 4096], BF16) for i in range(6)]
        xb = [fsb(f"xb{i}", [128, 2048], F32) for i in range(2)]
        xb2 = [fsb(f"xb2{i}", [128, 512], F32) for i in range(2)]
        xnb = [fsb(f"xnb{i}", [128, 2048], BF16) for i in range(2)]
        sg = [fsb(f"sg{i}", [128, 512], BF16) for i in range(2)]
        hst = [fsb(f"hst{i}", [128, 2048], BF16) for i in range(4)]
        ab = fsb("ab", [128, 2048], BF16)
        mtmp = fsb("mtmp", [128, 2048], BF16)
        hsp = [fsb(f"hsp{i}", [128, 2048], BF16) for i in range(2)]
        ytsb = fsb("ytsb", [128, 4 * 512], F32)

        xb_free = [None, None]
        xb2_free = [None, None]
        xn_free = [None, None]
        sg_free = [None, None]
        hst_free = [None, None, None, None]
        hsp_free = [None, None]
        wfree = [None] * 6
        hT3 = hT[:].rearrange("p (tc r) -> p tc r", tc=NCH)

        for tg in range(NTG):
            row_base = tg * TG
            xn_evacs = []
            for tc in range(NCH):
                slot = tc % 2
                evs = ln_chunk(src_ap, row_base + tc * 128, tc, xnT,
                               xb[slot], xnb[slot], xb_free[slot], xn_free[slot],
                               tc % 2, mtmp)
                xn_free[slot] = evs[-1]
                xb_free[slot] = evs[-1]
                xn_evacs.extend(evs)
                if paranoid & 16:
                    em.join()
            if tg == 0:
                dbg_dump(tag + "xnT", [128, NCH * 2048], xnT[:], tuple(xn_evacs))
                dbg_dump(tag + "stt", [128, 64], stt[:], tuple(xn_evacs))

            if paranoid & 2:
                em.join()
            hT_ready = []
            gtok = utok = None
            for it in range(NIT):
                gs = [(2 * it + h) % 3 for h in range(2)]
                us = [3 + (2 * it + h) % 3 for h in range(2)]
                wts = {}
                for h in range(2):
                    wts[("g", h)] = em.dma(
                        "sync", wsl[gs[h]][:],
                        bass.AP(wg_s.tensor, it * (128 * 16 * 512) + h * 4096,
                                [[8192, 128], [1, 4096]]),
                        waits=(wfree[gs[h]], ready["wgu"]))
                    wts[("u", h)] = em.dma(
                        "sync", wsl[us[h]][:],
                        bass.AP(wu_s.tensor, it * (128 * 16 * 512) + h * 4096,
                                [[8192, 128], [1, 4096]]),
                        waits=(wfree[us[h]], ready["wgu"]))
                IT, s4 = it // 4, it % 4
                gq = [None] * 2
                uq = [None] * 2
                for tc in range(NCH):
                    gb = (it * NCH + tc) % 2
                    ub = 2 + gb
                    for k in range(16):
                        h = k // 8
                        kk = k % 8
                        stat = xnT[:, tc * 2048 + k * 128: tc * 2048 + (k + 1) * 128]
                        w = []
                        if k == 0:
                            w = [pA_free[gb], pA_free[ub]]
                            w.extend(xn_evacs[tc * 4: tc * 4 + 4])
                        if kk == 0:
                            w = w + [wts[("g", h)], wts[("u", h)]]
                        ginc = "pe" if kk == 7 else None
                        gtok = em.op("tensor",
                                     lambda e, stat=stat, gb=gb, sl=gs[h], kk=kk, k=k: e.matmul(
                                         paf(gb), stat, wsl[sl][:, kk * 512:(kk + 1) * 512],
                                         start=(k == 0), stop=(k == 15)),
                                     waits=w, inc=ginc)
                        utok = em.op("tensor",
                                     lambda e, stat=stat, ub=ub, sl=us[h], kk=kk, k=k: e.matmul(
                                         paf(ub), stat, wsl[sl][:, kk * 512:(kk + 1) * 512],
                                         start=(k == 0), stop=(k == 15)),
                                     inc=ginc)
                        if kk == 7:
                            gq[h] = gtok
                            uq[h] = utok
                    if tc == NCH - 1:
                        for h in range(2):
                            wfree[gs[h]] = gq[h]
                            wfree[us[h]] = uq[h]
                    sslot = (it * NCH + tc) % 2
                    t_silu = em.op("scalar",
                                   lambda e, gb=gb, sslot=sslot: e.activation(
                                       sg[sslot][:], paf(gb), AF.Silu),
                                   waits=(gtok, sg_free[sslot]), inc="act")
                    pA_free[gb] = t_silu
                    hdst = hst[tc][:].rearrange("p (q r) -> p q r", q=4)[:, :, s4 * 128:(s4 + 1) * 128]
                    t_h = em.op("vector",
                                lambda e, sslot=sslot, ub=ub, hdst=hdst: e.tensor_tensor(
                                    hdst,
                                    sg[sslot][:].rearrange("p (j q) -> p q j", q=4),
                                    paf(ub).rearrange("p (j q) -> p q j", q=4),
                                    op=ALU.mult),
                                waits=(t_silu, utok,
                                       hst_free[tc] if s4 == 0 else None),
                                inc="dve")
                    pA_free[ub] = t_h
                    sg_free[sslot] = t_h
                    if s4 < 3:
                        continue
                    hs = hst[tc]
                    t_abs = em.op("scalar",
                                  lambda e, hs=hs: e.activation(ab[:], hs[:], AF.Abs),
                                  waits=(t_h,), inc="act")
                    q0, q1 = ab[:, 0:512], ab[:, 512:1024]
                    q2, q3 = ab[:, 1024:1536], ab[:, 1536:2048]
                    m1, n1 = mtmp[:, 0:512], mtmp[:, 512:1024]
                    m2, n2 = mtmp[:, 1024:1536], mtmp[:, 1536:2048]
                    em.op("vector", lambda e: e.tensor_tensor(m1, q0, q1, op=ALU.max), waits=(t_abs,))
                    em.op("vector", lambda e: e.tensor_tensor(n1, q0, q1, op=ALU.min))
                    em.op("vector", lambda e: e.tensor_tensor(m2, q2, q3, op=ALU.max))
                    em.op("vector", lambda e: e.tensor_tensor(n2, q2, q3, op=ALU.min))
                    em.op("vector", lambda e: e.tensor_tensor(m2, m1, m2, op=ALU.min))
                    em.op("vector", lambda e: e.tensor_tensor(n2, n1, n2, op=ALU.max))
                    em.op("vector", lambda e: e.tensor_tensor(m1, m2, n2, op=ALU.max))
                    hslot = (it * NCH + tc) % 2
                    happly = None
                    for q in range(4):
                        aq = ab[:, q * 512:(q + 1) * 512]
                        em.op("vector", lambda e, aq=aq: e.tensor_tensor(aq, aq, m1, op=ALU.is_ge))
                        happly = em.op("vector",
                                       lambda e, q=q, aq=aq, hslot=hslot, hs=hs: e.tensor_tensor(
                                           hsp[hslot][:, q * 512:(q + 1) * 512],
                                           hs[:, q * 512:(q + 1) * 512], aq, op=ALU.mult),
                                       waits=(hsp_free[hslot] if q == 0 else None,),
                                       inc="dve" if q == 3 else None)
                    hst_free[tc] = happly
                    if tg == 0 and it == 3 and tc == 0:
                        dbg_dump(tag + "hsp", [128, 2048], hsp[hslot][:], (happly,))
                        dbg_dump(tag + "hstg", [128, 2048], hst[tc][:], (t_h,))
                        dbg_dump(tag + "thr", [128, 2048], mtmp[:], (happly,))
                    for s in range(4):
                        pbk = (IT * 4 + s) % 2
                        pb = pT[pbk]
                        ptok = None
                        for q in range(4):
                            w = [happly, pt_free[pbk]] if q == 0 else []
                            ptok = em.op("tensor",
                                         lambda e, pb=pb, q=q, s=s, hslot=hslot: e.transpose(
                                             pb[:, q * 128:(q + 1) * 128],
                                             hsp[hslot][:, q * 512 + s * 128: q * 512 + s * 128 + 128],
                                             idb[:]),
                                         waits=w, inc="pe" if q == 3 else None)
                        if s == 3:
                            hsp_free[hslot] = ptok
                        cbase = IT * 16 + s * 4
                        dst = hT[:, tc * 8192 + cbase * 128: tc * 8192 + (cbase + 4) * 128]
                        ev_eng = "vector" if s % 2 == 0 else "scalar"
                        ev = em.op(ev_eng, lambda e, pb=pb, dst=dst: ecopy(e, dst, pb[:, 0:512]),
                                   waits=(ptok,), inc="dve" if ev_eng == "vector" else "act")
                        pt_free[pbk] = ev
                        hT_ready.append(ev)
                if paranoid & 4:
                    em.join()

            if tg == 0:
                dbg_dump(tag + "hT", [128, NCH * 8192], hT[:], tuple(hT_ready))

            if paranoid & 2:
                em.join()
            for dq in range(4):
                yevs = []
                for d4 in range(4):
                    gd = dq * 4 + d4
                    dsl = [(2 * gd + j) % 6 for j in range(2)]
                    wdts = {}
                    for j in range(2):
                        for q4 in range(4):
                            cp0 = j * 32 + q4 * 8
                            wdts[j] = em.dma(
                                "sync",
                                wsl[dsl[j]][:].rearrange("p (cp n) -> p cp n", cp=32)[:, q4 * 8:(q4 + 1) * 8, :],
                                bass.AP(wd_s.tensor, cp0 * (128 * 16 * 128) + gd * 128,
                                        [[2048, 128], [128 * 16 * 128, 8], [1, 128]]),
                                waits=(wfree[dsl[j]], ready["wd"]) if q4 == 0 else ())
                    ybk = gd % 2
                    ytok = None
                    ytoks = {}
                    for i in range(NC16):
                        j = i // 32
                        ii = i % 32
                        w = []
                        if i == 0:
                            w = [pA_free[ybk]]
                            if gd == 0:
                                w += hT_ready
                        if ii == 0:
                            w = w + [wdts[j]]
                        yinc = "pe" if ii == 31 else None
                        rhs = hT3[:, :, i * 128:(i + 1) * 128]
                        ytok = em.op("tensor",
                                     lambda e, i=i, ybk=ybk, sl=dsl[j], ii=ii, rhs=rhs: e.matmul(
                                         paf(ybk), wsl[sl][:, ii * 128:(ii + 1) * 128], rhs,
                                         start=(i == 0), stop=(i == NC16 - 1)),
                                     waits=w, inc=yinc)
                        if ii == 31:
                            ytoks[j] = ytok
                    for j in range(2):
                        wfree[dsl[j]] = ytoks[j]
                    ev = em.op("vector",
                               lambda e, d4=d4, ybk=ybk: e.tensor_copy(
                                   out=ytsb[:, d4 * 512:(d4 + 1) * 512], in_=paf(ybk)),
                               waits=(ytok,), inc="dve")
                    pA_free[ybk] = ev
                    yevs.append(ev)
                    if paranoid & 8:
                        em.join()
                for tc in range(NCH):
                    xslot = tc % 2
                    rows = src_ap[row_base + tc * 128: row_base + tc * 128 + 128,
                                  dq * 512:(dq + 1) * 512]
                    xld = em.dma("sync", xb2[xslot][:, 0:512], rows, waits=(xb2_free[xslot],))
                    pbk = tc % 2
                    ptok = None
                    for j in range(4):
                        d4 = j
                        w = [yevs[-1], pB_free[pbk], ready["consts"]] if j == 0 else []
                        ptok = em.op("tensor",
                                     lambda e, pbk=pbk, j=j, d4=d4, tc=tc: e.transpose(
                                         pB[pbk][:, j * 128:(j + 1) * 128],
                                         ytsb[:, d4 * 512 + tc * 128: d4 * 512 + tc * 128 + 128],
                                         idf[:]),
                                     waits=w, inc="pe" if j == 3 else None)
                    t_res = em.op("vector",
                                  lambda e, pbk=pbk, xslot=xslot: e.scalar_tensor_tensor(
                                      out=xb2[xslot][:, 0:512],
                                      in0=pB[pbk][:], scalar=0.5,
                                      in1=xb2[xslot][:, 0:512],
                                      op0=ALU.mult, op1=ALU.add),
                                  waits=(ptok, xld), inc="dve")
                    pB_free[pbk] = t_res
                    st = em.dma("gpsimd",
                                dst_ap[row_base + tc * 128: row_base + tc * 128 + 128,
                                       dq * 512:(dq + 1) * 512],
                                xb2[xslot][:, 0:512], waits=(t_res,))
                    xb2_free[xslot] = st
            if paranoid & 1:
                em.join()
        fs.close()

    def attn(src_ap, dst_ap):
        As = ExitStack()
        asb = lambda name, shape, dt: As.enter_context(nc.sbuf_tensor(name, shape, dt))
        xnT = asb("a_xnT", [128, 16 * 2048], BF16)
        qt_sb = asb("a_qt", [128, 4 * 2048], BF16)
        kt_sb = asb("a_kt", [128, 4 * 2048], BF16)
        v_sb = asb("a_v", [128, 16 * 512], BF16)
        wqsl = [asb(f"a_wq{i}", [128, 2048], BF16) for i in range(2)]
        wksl = [asb(f"a_wk{i}", [128, 2048], BF16) for i in range(2)]
        wvsl = asb("a_wv", [128, 16 * 512], BF16)
        pbuf = [asb(f"a_p{i}", [128, 2048], BF16) for i in range(2)]
        ptb = [asb(f"a_pt{i}", [128, 512], BF16) for i in range(2)]
        og = [asb(f"a_og{i}", [128, 512], F32) for i in range(2)]
        xb = [asb(f"a_xb{i}", [128, 2048], F32) for i in range(2)]
        xnb = [asb(f"a_xnb{i}", [128, 2048], BF16) for i in range(2)]
        rres = asb("a_r", [128, 4], F32)
        a_sq = asb("a_sq", [128, 2048], BF16)
        xrm = [asb(f"a_xrm{i}", [128, 512], F32) for i in range(2)]

        xnT3 = xnT[:].rearrange("p (tc r) -> p tc r", tc=16)
        xb_free = [None, None]
        xn_free = [None, None]
        pbuf_free = [None, None]
        ptb_free = [None, None]
        og_free = [None, None]
        r_free = [None] * 4
        xrm_free = [None, None]
        w_free = [None]
        proj_free = [None]

        for b in range(BC):
            rb = b * S
            xn_evacs_all = []
            for tc in range(16):
                slot = tc % 2
                evs = ln_chunk(src_ap, rb + tc * 128, tc, xnT,
                               xb[slot], xnb[slot], xb_free[slot], xn_free[slot],
                               tc % 2, a_sq)
                xn_free[slot] = evs[-1]
                xb_free[slot] = evs[-1]
                xn_evacs_all.extend(evs)
            qk_free = [None, None, None, None]
            for g in range(4):
                wts = []
                for chh in range(2):
                    c0_ = chh * 8
                    wts.append(em.dma(
                        "sync",
                        wvsl[:].rearrange("p (c n) -> p c n", c=16)[:, c0_:c0_ + 8, :],
                        bass.AP(wv_s.tensor, c0_ * 128 * D + g * 512,
                                [[D, 128], [128 * D, 8], [1, 512]]),
                        waits=(w_free[0], ready["qkv"]) if chh == 0 else ()))
                ev_flip = 0
                proj_evs = []
                lastmm = None
                for mi, (mat, wslr, wsrc, br, dstb) in enumerate(
                        (("q", wqsl, wq_s, bq_r, qt_sb), ("k", wksl, wk_s, bk_r, kt_sb))):
                    for oc in range(4):
                        o = g * 4 + oc
                        slot = oc % 2
                        for chh in range(2):
                            c0_ = chh * 8
                            wt = em.dma(
                                "sync",
                                wslr[slot][:].rearrange("p (c n) -> p c n", c=16)[:, c0_:c0_ + 8, :],
                                bass.AP(wsrc.tensor, c0_ * 128 * D + o * 128,
                                        [[D, 128], [128 * D, 8], [1, 128]]),
                                waits=(qk_free[mi * 2 + slot], ready["qkv"], ready["bias"]) if chh == 0 else ())
                        lastproj = None
                        for w4 in range(4):
                            bk_ = (oc * 4 + w4) % 2
                            for c in range(16):
                                w = ([pA_free[bk_], wt] + xn_evacs_all) if c == 0 else []
                                rhs = xnT3[:, w4 * 4:(w4 + 1) * 4, c * 128:(c + 1) * 128]
                                lastproj = em.op("tensor",
                                      lambda e, wslr=wslr, slot=slot, c=c, bk_=bk_, rhs=rhs: e.matmul(
                                          paf(bk_), wslr[slot][:, c * 128:(c + 1) * 128],
                                          rhs, start=(c == 0), stop=False),
                                      waits=w)
                            mtok = em.op("tensor",
                                         lambda e, br=br, oc=oc, g=g, bk_=bk_, mi=mi: e.matmul(
                                             paf(bk_), br[0:1, (g * 4 + oc) * 128:(g * 4 + oc + 1) * 128],
                                             ones_r[mi * 32: mi * 32 + 1, :], start=False, stop=True),
                                         inc="pe")
                            lastmm = mtok
                            qk_free[mi * 2 + slot] = mtok
                            ev_eng = "vector" if ev_flip % 2 == 0 else "scalar"
                            ev_flip += 1
                            dst = dstb[:, oc * 2048 + w4 * 512: oc * 2048 + (w4 + 1) * 512]
                            scl = CSC if mat == "q" else 1.0
                            if ev_eng == "vector":
                                ev = em.op(ev_eng,
                                           lambda e, bk_=bk_, dst=dst, scl=scl: e.tensor_scalar_mul(
                                               dst, paf(bk_), scl),
                                           waits=(mtok, proj_free[0]), inc="dve")
                            else:
                                ev = em.op(ev_eng,
                                           lambda e, bk_=bk_, dst=dst, scl=scl: e.activation(
                                               dst, paf(bk_), AF.Identity, scale=scl),
                                           waits=(mtok, proj_free[0]), inc="act")
                            pA_free[bk_] = ev
                            proj_evs.append(ev)
                for tc in range(16):
                    bk_ = tc % 2
                    for c in range(16):
                        w = [pA_free[bk_]] if c == 0 else []
                        if tc == 0 and c == 0:
                            w = w + wts
                        stat = xnT[:, tc * 2048 + c * 128: tc * 2048 + (c + 1) * 128]
                        em.op("tensor",
                              lambda e, stat=stat, c=c, bk_=bk_: e.matmul(
                                  paf(bk_), stat, wvsl[:, c * 512:(c + 1) * 512],
                                  start=(c == 0), stop=False),
                              waits=w)
                    mtok = em.op("tensor",
                                 lambda e, g=g, bk_=bk_: e.matmul(
                                     paf(bk_), ones_r[64:65, 0:128], bv_r[0:1, g * 512:(g + 1) * 512],
                                     start=False, stop=True),
                                 inc="pe")
                    lastmm = mtok
                    ev_eng = "vector" if tc % 2 == 0 else "scalar"
                    ev = em.op(ev_eng,
                               lambda e, tc=tc, bk_=bk_: ecopy(
                                   e, v_sb[:, tc * 512:(tc + 1) * 512], paf(bk_)),
                               waits=(mtok, proj_free[0]),
                               inc="dve" if ev_eng == "vector" else "act")
                    pA_free[bk_] = ev
                    proj_evs.append(ev)
                w_free[0] = lastmm

                otok = None
                for Q in range(16):
                    oQev = None
                    for oc in range(4):
                        KW = (Q + 4) // 4
                        stoks = []
                        for kw in range(KW):
                            kc = min(512, (Q + 1) * 128 - kw * 512)
                            w = [pA_free[kw]]
                            if Q == 0 and oc == 0 and kw == 0:
                                w += proj_evs
                            stok = em.op("tensor",
                                         lambda e, oc=oc, Q=Q, kw=kw, kc=kc: e.matmul(
                                             paf(kw, kc),
                                             qt_sb[:, oc * 2048 + Q * 128: oc * 2048 + (Q + 1) * 128],
                                             kt_sb[:, oc * 2048 + kw * 512: oc * 2048 + kw * 512 + kc],
                                             start=True, stop=True),
                                         waits=w, inc="pe")
                            stoks.append(stok)
                        tmask = em.op("vector",
                                      lambda e, Q=Q: e.tensor_tensor(
                                          pSh[:, Q * 128: (Q + 1) * 128],
                                          pSh[:, Q * 128: (Q + 1) * 128], mtri[:], op=ALU.add),
                                      waits=tuple(stoks), inc="dve")
                        par = (Q * 4 + oc) % 2
                        mc = stt[:, 32 + par * 16: 32 + (par + 1) * 16]
                        tnm = em.op("vector",
                                    lambda e, Q=Q, mc=mc: e.reduce_max(
                                        mc[:, 0:1], pSh[:, 0:(Q + 1) * 128], axis=AX.X, negate=True),
                                    waits=(tmask,), inc="dve")
                        pslot = (Q * 4 + oc) % 2
                        etoks = []
                        for kw in range(KW):
                            kc = min(512, (Q + 1) * 128 - kw * 512)
                            w = [tnm] + ([pbuf_free[pslot]] if kw == 0 else [])
                            et = em.op("scalar",
                                       lambda e, kw=kw, kc=kc, pslot=pslot, mc=mc: e.activation(
                                           pbuf[pslot][:, kw * 512: kw * 512 + kc],
                                           paf(kw, kc), AF.Exp,
                                           bias=mc[:, 0:1],
                                           accum_out=mc[:, 9 + kw:10 + kw]),
                                       waits=tuple(w), inc="act")
                            pA_free[kw] = et
                            etoks.append(et)
                        tsum = em.op("vector",
                                     lambda e, mc=mc, KW=KW: e.reduce_sum(
                                         mc[:, 13:14], mc[:, 9:9 + KW], axis=AX.X),
                                     waits=tuple(etoks), inc="dve")
                        rslot = (Q * 4 + oc) % 4
                        trec = em.op("vector",
                                     lambda e, mc=mc, rslot=rslot: e.reciprocal(
                                         rres[:, rslot:rslot + 1], mc[:, 13:14]),
                                     fwaits=(tsum, r_free[rslot]), inc="dve")
                        obk = (Q * 4 + oc) % 2
                        NKT = Q + 1
                        for ktg in range(ceil_div(NKT, 4)):
                            nk = min(4, NKT - ktg * 4)
                            pbk = ktg % 2
                            ptok = None
                            for j in range(nk):
                                kt = ktg * 4 + j
                                w = [etoks[ktg], pt_free[pbk]] if j == 0 else []
                                ptok = em.op("tensor",
                                             lambda e, pbk=pbk, j=j, kt=kt, pslot=pslot: e.transpose(
                                                 pT[pbk][:, j * 128:(j + 1) * 128],
                                                 pbuf[pslot][:, kt * 128:(kt + 1) * 128], idb[:]),
                                             waits=w, inc="pe" if j == nk - 1 else None)
                            ptev = em.op("vector",
                                         lambda e, pbk=pbk, nk=nk: e.tensor_copy(
                                             out=ptb[pbk][:, 0:nk * 128], in_=pT[pbk][:, 0:nk * 128]),
                                         waits=(ptok, ptb_free[pbk]), inc="dve")
                            pt_free[pbk] = ptev
                            for j in range(nk):
                                kt = ktg * 4 + j
                                w = [ptev]
                                if ktg == 0 and j == 0:
                                    w.append(pB_free[obk])
                                otok = em.op("tensor",
                                             lambda e, pbk=pbk, j=j, kt=kt, oc=oc, obk=obk: e.matmul(
                                                 pB[obk][:, 0:128],
                                                 ptb[pbk][:, j * 128:(j + 1) * 128],
                                                 v_sb[:, kt * 512 + oc * 128: kt * 512 + (oc + 1) * 128],
                                                 start=(kt == 0), stop=(kt == NKT - 1)),
                                             waits=w, inc="pe" if kt == NKT - 1 else None)
                            ptb_free[pbk] = otok
                        pbuf_free[pslot] = otok
                        oslot = Q % 2
                        oev = em.op("vector",
                                    lambda e, oc=oc, obk=obk, rslot=rslot, oslot=oslot: e.tensor_scalar(
                                        out=og[oslot][:, oc * 128:(oc + 1) * 128],
                                        in0=pB[obk][:, 0:128], scalar1=rres[:, rslot:rslot + 1],
                                        scalar2=None, op0=ALU.mult),
                                    waits=(otok, og_free[oslot] if oc == 0 else None),
                                    fwaits=(trec,), inc="dve")
                        pB_free[obk] = oev
                        r_free[rslot] = oev
                        oQev = oev
                    xs = Q % 2
                    ld = em.dma("sync", xrm[xs][:],
                                src_ap[rb + Q * 128: rb + (Q + 1) * 128, g * 512:(g + 1) * 512],
                                waits=(xrm_free[xs],))
                    oslot = Q % 2
                    tadd = em.op("vector",
                                 lambda e, xs=xs, oslot=oslot: e.tensor_tensor(
                                     xrm[xs][:], xrm[xs][:], og[oslot][:], op=ALU.add),
                                 waits=(ld, oQev), inc="dve")
                    st = em.dma("gpsimd",
                                dst_ap[rb + Q * 128: rb + (Q + 1) * 128, g * 512:(g + 1) * 512],
                                xrm[xs][:], waits=(tadd,))
                    xrm_free[xs] = st
                    og_free[oslot] = tadd
                proj_free[0] = otok
            em.join()
        As.close()

    setup_consts()
    preconvert()
    if "dumpw" in parts:
        em.join()
        dump_out = nc.dram_tensor("dump_out", [16 * 128, NC16 * 128], BF16,
                                  kind="ExternalOutput").ap()
        rb = sb("dump_rb", [128, NC16 * 128], BF16)
        for blk in range(16):
            ldd = em.dma("sync", rb[:], bass.AP(wd_s.tensor, blk * (128 * NC16 * 128),
                                                [[NC16 * 128, 128], [1, NC16 * 128]]))
            std = em.dma("gpsimd", dump_out[blk * 128:(blk + 1) * 128, :], rb[:], waits=(ldd,))
            em.join()
    if "ffn1" in parts:
        ffn(x_in, xr if ("attn" in parts or "ffn2" in parts) else y_out, "f1")
        em.join()
    if "attn" in parts:
        src = xr if "ffn1" in parts else x_in
        dst = xr if "ffn2" in parts else y_out
        attn(src, dst)
        em.join()
    if "ffn2" in parts:
        src = xr if ("ffn1" in parts or "attn" in parts) else x_in
        ffn(src, y_out, "f2")
        em.join()

    for eng in ENGS:
        for q in ("qs", "qg", "qa"):
            for i in range(Em.NLANE):
                em._emit_wait(eng, f"{q}{i}", em.counts[f"{q}{i}"])

    with stack:
        with nc.Block() as block:
            em.finalize(block)
    return nc


_PROG_CACHE = {}


def _get_prog(parts):
    key = tuple(parts)
    if key not in _PROG_CACHE:
        _PROG_CACHE[key] = build_program(parts)
    return _PROG_CACHE[key]


def kernel(L0_attn_x, L0_attn_Wq, L0_attn_bq, L0_attn_Wk, L0_attn_bk,
           L0_attn_Wv, L0_attn_bv, L0_ffn_w_gate, L0_ffn_w_up, L0_ffn_w_down,
           parts=("ffn1", "attn", "ffn2"), core_ids=None):
    x = np.ascontiguousarray(np.asarray(L0_attn_x, dtype=np.float32))
    weights = {
        "wq": np.ascontiguousarray(np.asarray(L0_attn_Wq, np.float32)),
        "wk": np.ascontiguousarray(np.asarray(L0_attn_Wk, np.float32)),
        "wv": np.ascontiguousarray(np.asarray(L0_attn_Wv, np.float32)),
        "bq": np.ascontiguousarray(np.asarray(L0_attn_bq, np.float32)),
        "bk": np.ascontiguousarray(np.asarray(L0_attn_bk, np.float32)),
        "bv": np.ascontiguousarray(np.asarray(L0_attn_bv, np.float32)),
        "wg": np.ascontiguousarray(np.asarray(L0_ffn_w_gate, np.float32)),
        "wu": np.ascontiguousarray(np.asarray(L0_ffn_w_up, np.float32)),
        "wd": np.ascontiguousarray(np.asarray(L0_ffn_w_down, np.float32)),
    }
    nc = _get_prog(parts)
    ncore = NCORES
    in_maps = []
    for core in range(ncore):
        m = dict(weights)
        m["x_in"] = x[core * BC:(core + 1) * BC].reshape(NT, D)
        in_maps.append(m)
    res = run_bass_kernel_spmd(nc, in_maps, core_ids=list(range(ncore)))
    out = np.empty((B, S, D), np.float32)
    for core in range(ncore):
        out[core * BC:(core + 1) * BC] = res.results[core]["y_out"].reshape(BC, S, D)
    return out


# version 08
# speedup vs baseline: 1.0148x; 1.0148x over previous
import numpy as np
import ml_dtypes
from contextlib import ExitStack

import concourse.bass as bass
import concourse.mybir as mybir
from concourse.bass_utils import run_bass_kernel_spmd

F32 = mybir.dt.float32
BF16 = mybir.dt.bfloat16
AF = mybir.ActivationFunctionType
ALU = mybir.AluOpType
AX = mybir.AxisListType

B, S, D, I, H, DH = 16, 2048, 2048, 8192, 16, 128
NCORES = 8
BC = B // NCORES
NT = BC * S
TG = 512
NTG = NT // TG
NCH = TG // 128
NIT = I // 512
NC16 = I // 128
CSC = float(1.0 / np.sqrt(DH))
MASKVAL = -1e9

ENGS = ("tensor", "vector", "scalar", "sync", "gpsimd")
DMA_SEM = {"sync": "qs", "gpsimd": "qg", "scalar": "qa"}


class Em:

    NLANE = 8

    def __init__(self, nc, stack):
        self.nc = nc
        self.streams = {e: [] for e in ENGS}
        self.sems = {}
        self.counts = {}
        names = ["pe", "dve", "act", "pool", "jn"]
        for q in ("qs", "qg", "qa"):
            names += [f"{q}{i}" for i in range(self.NLANE)]
        for name in names:
            self.sems[name] = stack.enter_context(nc.semaphore(name))
            self.counts[name] = 0
        self.qidx = {"qs": 0, "qg": 0, "qa": 0}
        self.waited = {e: {} for e in ENGS}
        self.njoin = 0

    def _emit_wait(self, eng, sem, val):
        if val is None or val <= 0:
            return
        if self.waited[eng].get(sem, 0) >= val:
            return
        self.waited[eng][sem] = val
        h = self.sems[sem]
        self.streams[eng].append(lambda e, h=h, v=val: e.wait_ge(h, v))

    def _coalesce(self, waits, skip_eng=None):
        best = {}
        for t in waits:
            if t is None:
                continue
            sem, val, prod = t
            if prod == skip_eng:
                continue
            if best.get(sem, 0) < val:
                best[sem] = val
        return best

    def op(self, eng, fn, waits=(), inc=None, fwaits=()):
        for sem, val in self._coalesce(waits, skip_eng=eng).items():
            self._emit_wait(eng, sem, val)
        for sem, val in self._coalesce(fwaits).items():
            self._emit_wait(eng, sem, val)
        tok = None
        if inc:
            self.counts[inc] += 1
            v = self.counts[inc]
            h = self.sems[inc]
            self.streams[eng].append(lambda e, fn=fn, h=h, v=v: fn(e).then_inc(h, 1))
            tok = (inc, v, eng)
        else:
            self.streams[eng].append(lambda e, fn=fn: fn(e))
        return tok

    def dma(self, eng, out, in_, waits=(), **kw):
        for sem, val in self._coalesce(waits).items():
            self._emit_wait(eng, sem, val)
        q = DMA_SEM[eng]
        lane = f"{q}{self.qidx[q] % self.NLANE}"
        self.qidx[q] += 1
        self.counts[lane] += 16
        v = self.counts[lane]
        h = self.sems[lane]
        self.streams[eng].append(
            lambda e, o=out, i=in_, h=h, v=v, kw=kw: e.dma_start(out=o, in_=i, **kw).then_inc(h, 16))
        return (lane, v, "dma")

    def join(self):
        for eng in ("sync", "gpsimd", "scalar"):
            q = DMA_SEM[eng]
            for i in range(self.NLANE):
                self._emit_wait(eng, f"{q}{i}", self.counts[f"{q}{i}"])
        self.njoin += 1
        target = 5 * self.njoin
        h = self.sems["jn"]
        for eng in ENGS:
            self.streams[eng].append(lambda e, h=h: e.sem_inc(h, 1))
            self.streams[eng].append(lambda e, h=h, t=target: e.wait_ge(h, t))
            self.waited[eng] = {}

    def finalize(self, block):
        for eng in ENGS:
            stream = self.streams[eng]

            def runner(e, stream=stream):
                for fn in stream:
                    fn(e)
            getattr(block, eng)(runner)


def ceil_div(a, b):
    return (a + b - 1) // b


def ecopy(e, dst, src):
    if hasattr(e, "tensor_copy"):
        return e.tensor_copy(out=dst, in_=src)
    return e.copy(dst, src)


def build_program(parts=("ffn1", "attn", "ffn2"), dbg=(), paranoid=0):
    nc = bass.Bass("TRN2", target_bir_lowering=False)
    stack = ExitStack()

    x_in = nc.dram_tensor("x_in", [NT, D], F32, kind="ExternalInput").ap()
    y_out = nc.dram_tensor("y_out", [NT, D], F32, kind="ExternalOutput").ap()
    wq = nc.dram_tensor("wq", [D, D], F32, kind="ExternalInput").ap()
    wk = nc.dram_tensor("wk", [D, D], F32, kind="ExternalInput").ap()
    wv = nc.dram_tensor("wv", [D, D], F32, kind="ExternalInput").ap()
    bq = nc.dram_tensor("bq", [D], F32, kind="ExternalInput").ap()
    bk = nc.dram_tensor("bk", [D], F32, kind="ExternalInput").ap()
    bv = nc.dram_tensor("bv", [D], F32, kind="ExternalInput").ap()
    wg = nc.dram_tensor("wg", [D, I], F32, kind="ExternalInput").ap()
    wu = nc.dram_tensor("wu", [D, I], F32, kind="ExternalInput").ap()
    wd = nc.dram_tensor("wd", [I, D], F32, kind="ExternalInput").ap()

    xr = nc.dram_tensor("xr", [NT, D], F32, kind="Internal").ap()
    wg_s = nc.dram_tensor("wg_s", [NIT, 128, 16, 512], BF16, kind="Internal").ap()
    wu_s = nc.dram_tensor("wu_s", [NIT, 128, 16, 512], BF16, kind="Internal").ap()
    wd_s = nc.dram_tensor("wd_s", [NC16, 128, 16, 128], BF16,
                          kind="ExternalOutput" if "wdout" in parts else "Internal").ap()
    wq_s = nc.dram_tensor("wq_s", [D, D], BF16, kind="Internal").ap()
    wk_s = nc.dram_tensor("wk_s", [D, D], BF16, kind="Internal").ap()
    wv_s = nc.dram_tensor("wv_s", [D, D], BF16, kind="Internal").ap()

    em = Em(nc, stack)
    sb = lambda name, shape, dt: stack.enter_context(nc.sbuf_tensor(name, shape, dt))
    ps = lambda name, shape, dt: stack.enter_context(nc.psum_tensor(name, shape, dt))

    dbg_t = {}
    def dbg_dump(name, shape, sbuf_ap, waits):
        if name not in dbg:
            return
        if name not in dbg_t:
            dbg_t[name] = nc.dram_tensor("dbg_" + name, shape, F32, kind="ExternalOutput").ap()
        em.dma("gpsimd", dbg_t[name], sbuf_ap, waits=waits)

    ready = {"consts": None, "bias": None, "wgu": None, "qkv": None, "wd": None}

    pSh = ps("pSh", [128, 2048], F32)
    paf = lambda i, kc=512: pSh[:, i * 512: i * 512 + kc]
    pB = [ps(f"pB{i}", [128, 512], F32) for i in range(2)]
    pT = [ps(f"pT{i}", [128, 1024], BF16) for i in range(2)]

    idb = sb("idb", [128, 128], BF16)
    idf = sb("idf", [128, 128], F32)
    mtri = sb("mtri", [128, 128], F32)
    mtriT = sb("mtriT", [128, 128], BF16)
    eps_t = sb("eps_t", [128, 1], F32)
    ones_r = sb("ones_r", [65, 512], BF16)
    bias3 = sb("bias3", [65, D], BF16)
    bq_r = bias3[0:1, :]
    bk_r = bias3[32:33, :]
    bv_r = bias3[64:65, :]
    stt = sb("stt", [128, 64], F32)

    def setup_consts():
        em.op("gpsimd", lambda e: e.memset(idb[:], 0.0))
        em.op("gpsimd", lambda e: e.affine_select(
            out=idb[:], in_=idb[:], compare_op=ALU.not_equal, fill=1.0,
            base=0, pattern=[[-1, 128]], channel_multiplier=1))
        em.op("gpsimd", lambda e: e.memset(idf[:], 0.0))
        em.op("gpsimd", lambda e: e.affine_select(
            out=idf[:], in_=idf[:], compare_op=ALU.not_equal, fill=1.0,
            base=0, pattern=[[-1, 128]], channel_multiplier=1))
        em.op("gpsimd", lambda e: e.memset(mtri[:], 0.0))
        em.op("gpsimd", lambda e: e.affine_select(
            out=mtri[:], in_=mtri[:], compare_op=ALU.is_ge, fill=MASKVAL,
            base=0, pattern=[[-1, 128]], channel_multiplier=1))
        em.op("gpsimd", lambda e: e.memset(mtriT[:], 0.0))
        em.op("gpsimd", lambda e: e.affine_select(
            out=mtriT[:], in_=mtriT[:], compare_op=ALU.is_ge, fill=MASKVAL,
            base=0, pattern=[[1, 128]], channel_multiplier=-1))
        em.op("gpsimd", lambda e: e.memset(eps_t[:], 1e-5))
        t = em.op("gpsimd", lambda e: e.memset(ones_r[:], 1.0), inc="pool")
        ready["consts"] = t
        em.dma("gpsimd", bq_r, bq[None, :])
        em.dma("gpsimd", bk_r, bk[None, :])
        ready["bias"] = em.dma("gpsimd", bv_r, bv[None, :])

    def preconvert():
        t = None
        for src, dst in ((wg, wg_s), (wu, wu_s)):
            for it in range(NIT):
                for ch in range(2):
                    c0 = ch * 8
                    sap = bass.AP(src.tensor, c0 * 128 * I + it * 512,
                                  [[128 * I, 8], [I, 128], [1, 512]])
                    dap = dst[it][:, c0:c0 + 8, :].rearrange("p c n -> c p n")
                    t = em.dma("gpsimd", dap, sap)
        ready["wgu"] = t
        for src_, dst in ((wq, wq_s), (wk, wk_s), (wv, wv_s)):
            for ch in range(4):
                r0 = ch * 512
                sap = bass.AP(src_.tensor, r0 * D, [[D, 512], [1, D]])
                dap = bass.AP(dst.tensor, r0 * D, [[D, 512], [1, D]])
                t = em.dma("gpsimd", dap, sap)
        ready["qkv"] = t
        pre_s = ExitStack()
        wdbuf = [pre_s.enter_context(nc.sbuf_tensor(f"wdc{i}", [128, 2048], BF16))
                 for i in range(2)]
        free_tok = [None, None]
        for cp in range(NC16):
            slot = cp % 2
            base = (cp // 4) * 512 + (cp % 4)
            sap = bass.AP(wd.tensor, base * D, [[4 * D, 128], [1, D]])
            ld = em.dma("gpsimd", wdbuf[slot][:], sap, waits=(free_tok[slot],))
            dap = bass.AP(wd_s.tensor, cp * (128 * 16 * 128), [[2048, 128], [1, 2048]])
            st_tok = em.dma("sync", dap, wdbuf[slot][:], waits=(ld,))
            free_tok[slot] = st_tok
        ready["wd"] = st_tok
        pre_s.close()

    pt_free = [None, None]
    pA_free = [None, None, None, None]
    pB_free = [None, None]

    def ln_chunk(src_ap, row0, tc, xnT_buf, xbuf, xnbuf, xb_free, xn_free, par, sq_scr):
        sc = stt[:, par * 16: par * 16 + 16]
        ld = em.dma("sync", xbuf[:], src_ap[row0:row0 + 128, :], waits=(xb_free,))
        t0 = em.op("vector", lambda e: e.reduce_sum(sc[:, 0:1], xbuf[:], axis=AX.X),
                   waits=(ld,), inc="dve")
        t_sq = em.op("vector", lambda e: e.scalar_tensor_tensor(
                         out=sq_scr[:], in0=xbuf[:], scalar=1.0, in1=xbuf[:],
                         op0=ALU.mult, op1=ALU.mult, accum_out=sc[:, 1:2]),
                     waits=(ld,), inc="dve")
        t1 = em.op("vector", lambda e: e.tensor_scalar_mul(sc[:, 2:3], sc[:, 0:1], 1.0 / D),
                   fwaits=(t0,), inc="dve")
        t2 = em.op("vector", lambda e: e.tensor_tensor(sc[:, 3:4], sc[:, 2:3], sc[:, 2:3], op=ALU.mult),
                   fwaits=(t1,), inc="dve")
        t3 = em.op("vector", lambda e: e.tensor_scalar_mul(sc[:, 4:5], sc[:, 1:2], 1.0 / D),
                   fwaits=(t_sq,), inc="dve")
        t_var = em.op("vector", lambda e: e.tensor_tensor(sc[:, 5:6], sc[:, 4:5], sc[:, 3:4], op=ALU.subtract),
                      fwaits=(t2, t3), inc="dve")
        t_sd = em.op("scalar", lambda e: e.activation(sc[:, 6:7], sc[:, 5:6], AF.Sqrt, bias=eps_t[:]),
                     waits=(t_var, ready["consts"]), inc="act")
        t5 = em.op("vector", lambda e: e.reciprocal(sc[:, 7:8], sc[:, 6:7]),
                   waits=(t_sd,), inc="dve")
        t6 = em.op("vector", lambda e: e.tensor_tensor(sc[:, 8:9], sc[:, 2:3], sc[:, 7:8], op=ALU.mult),
                   fwaits=(t5,), inc="dve")
        t_bias = em.op("vector", lambda e: e.tensor_scalar_mul(sc[:, 9:10], sc[:, 8:9], -1.0),
                       fwaits=(t6,), inc="dve")
        t_xn = em.op("scalar", lambda e: e.activation(xnbuf[:], xbuf[:], AF.Identity,
                                                      scale=sc[:, 7:8], bias=sc[:, 9:10]),
                     waits=(t_bias, xn_free), inc="act")
        evac_toks = []
        for bnk in range(4):
            pbk = bnk % 2
            pb = pT[pbk]
            ptok = None
            for j in range(4):
                c = bnk * 4 + j
                w = [t_xn, pt_free[pbk], ready["consts"]] if j == 0 else []
                ptok = em.op("tensor",
                             lambda e, c=c, pb=pb, j=j: e.transpose(
                                 pb[:, j * 128:(j + 1) * 128],
                                 xnbuf[:, c * 128:(c + 1) * 128], idb[:]),
                             waits=w, inc="pe" if j == 3 else None)
            dst = xnT_buf[:, tc * 2048 + bnk * 512: tc * 2048 + (bnk + 1) * 512]
            ev_eng = "vector" if bnk % 2 == 0 else "scalar"
            ev = em.op(ev_eng, lambda e, pb=pb, dst=dst: ecopy(e, dst, pb[:, 0:512]),
                       waits=(ptok,), inc="dve" if ev_eng == "vector" else "act")
            pt_free[pbk] = ev
            evac_toks.append(ev)
        return evac_toks

    def ffn(src_ap, dst_ap, tag):
        fs = ExitStack()
        fsb = lambda name, shape, dt: fs.enter_context(nc.sbuf_tensor(tag + name, shape, dt))
        xnT = fsb("xnT", [128, NCH * 2048], BF16)
        hT = fsb("hT", [128, NCH * 8192], BF16)
        wsl = [fsb(f"w{i}", [128, 4096], BF16) for i in range(6)]
        xb = [fsb(f"xb{i}", [128, 2048], F32) for i in range(2)]
        xb2 = [fsb(f"xb2{i}", [128, 512], F32) for i in range(2)]
        xnb = [fsb(f"xnb{i}", [128, 2048], BF16) for i in range(2)]
        sg = [fsb(f"sg{i}", [128, 512], BF16) for i in range(2)]
        hst = [fsb(f"hst{i}", [128, 2048], BF16) for i in range(4)]
        ab = fsb("ab", [128, 2048], BF16)
        mtmp = fsb("mtmp", [128, 2048], BF16)
        hsp = [fsb(f"hsp{i}", [128, 2048], BF16) for i in range(2)]
        ytsb = fsb("ytsb", [128, 4 * 512], F32)

        xb_free = [None, None]
        xb2_free = [None, None]
        xn_free = [None, None]
        sg_free = [None, None]
        hst_free = [None, None, None, None]
        hsp_free = [None, None]
        wfree = [None] * 6
        hT3 = hT[:].rearrange("p (tc r) -> p tc r", tc=NCH)

        for tg in range(NTG):
            row_base = tg * TG
            xn_evacs = []
            for tc in range(NCH):
                slot = tc % 2
                evs = ln_chunk(src_ap, row_base + tc * 128, tc, xnT,
                               xb[slot], xnb[slot], xb_free[slot], xn_free[slot],
                               tc % 2, mtmp)
                xn_free[slot] = evs[-1]
                xb_free[slot] = evs[-1]
                xn_evacs.extend(evs)
                if paranoid & 16:
                    em.join()
            if tg == 0:
                dbg_dump(tag + "xnT", [128, NCH * 2048], xnT[:], tuple(xn_evacs))
                dbg_dump(tag + "stt", [128, 64], stt[:], tuple(xn_evacs))

            if paranoid & 2:
                em.join()
            hT_ready = []
            gtok = utok = None
            for it in range(NIT):
                gs = [(2 * it + h) % 3 for h in range(2)]
                us = [3 + (2 * it + h) % 3 for h in range(2)]
                wts = {}
                for h in range(2):
                    wts[("g", h)] = em.dma(
                        "sync", wsl[gs[h]][:],
                        bass.AP(wg_s.tensor, it * (128 * 16 * 512) + h * 4096,
                                [[8192, 128], [1, 4096]]),
                        waits=(wfree[gs[h]], ready["wgu"]))
                    wts[("u", h)] = em.dma(
                        "sync", wsl[us[h]][:],
                        bass.AP(wu_s.tensor, it * (128 * 16 * 512) + h * 4096,
                                [[8192, 128], [1, 4096]]),
                        waits=(wfree[us[h]], ready["wgu"]))
                IT, s4 = it // 4, it % 4
                gq = [None] * 2
                uq = [None] * 2
                for tc in range(NCH):
                    gb = (it * NCH + tc) % 2
                    ub = 2 + gb
                    for k in range(16):
                        h = k // 8
                        kk = k % 8
                        stat = xnT[:, tc * 2048 + k * 128: tc * 2048 + (k + 1) * 128]
                        w = []
                        if k == 0:
                            w = [pA_free[gb], pA_free[ub]]
                            w.extend(xn_evacs[tc * 4: tc * 4 + 4])
                        if kk == 0:
                            w = w + [wts[("g", h)], wts[("u", h)]]
                        ginc = "pe" if kk == 7 else None
                        gtok = em.op("tensor",
                                     lambda e, stat=stat, gb=gb, sl=gs[h], kk=kk, k=k: e.matmul(
                                         paf(gb), stat, wsl[sl][:, kk * 512:(kk + 1) * 512],
                                         start=(k == 0), stop=(k == 15)),
                                     waits=w, inc=ginc)
                        utok = em.op("tensor",
                                     lambda e, stat=stat, ub=ub, sl=us[h], kk=kk, k=k: e.matmul(
                                         paf(ub), stat, wsl[sl][:, kk * 512:(kk + 1) * 512],
                                         start=(k == 0), stop=(k == 15)),
                                     inc=ginc)
                        if kk == 7:
                            gq[h] = gtok
                            uq[h] = utok
                    if tc == NCH - 1:
                        for h in range(2):
                            wfree[gs[h]] = gq[h]
                            wfree[us[h]] = uq[h]
                    sslot = (it * NCH + tc) % 2
                    t_silu = em.op("scalar",
                                   lambda e, gb=gb, sslot=sslot: e.activation(
                                       sg[sslot][:], paf(gb), AF.Silu),
                                   waits=(gtok, sg_free[sslot]), inc="act")
                    pA_free[gb] = t_silu
                    hdst = hst[tc][:].rearrange("p (q r) -> p q r", q=4)[:, :, s4 * 128:(s4 + 1) * 128]
                    t_h = em.op("vector",
                                lambda e, sslot=sslot, ub=ub, hdst=hdst: e.tensor_tensor(
                                    hdst,
                                    sg[sslot][:].rearrange("p (j q) -> p q j", q=4),
                                    paf(ub).rearrange("p (j q) -> p q j", q=4),
                                    op=ALU.mult),
                                waits=(t_silu, utok,
                                       hst_free[tc] if s4 == 0 else None),
                                inc="dve")
                    pA_free[ub] = t_h
                    sg_free[sslot] = t_h
                    if s4 < 3:
                        continue
                    hs = hst[tc]
                    t_abs = em.op("scalar",
                                  lambda e, hs=hs: e.activation(ab[:], hs[:], AF.Abs),
                                  waits=(t_h,), inc="act")
                    q0, q1 = ab[:, 0:512], ab[:, 512:1024]
                    q2, q3 = ab[:, 1024:1536], ab[:, 1536:2048]
                    m1, n1 = mtmp[:, 0:512], mtmp[:, 512:1024]
                    m2, n2 = mtmp[:, 1024:1536], mtmp[:, 1536:2048]
                    em.op("vector", lambda e: e.tensor_tensor(m1, q0, q1, op=ALU.max), waits=(t_abs,))
                    em.op("vector", lambda e: e.tensor_tensor(n1, q0, q1, op=ALU.min))
                    em.op("vector", lambda e: e.tensor_tensor(m2, q2, q3, op=ALU.max))
                    em.op("vector", lambda e: e.tensor_tensor(n2, q2, q3, op=ALU.min))
                    em.op("vector", lambda e: e.tensor_tensor(m2, m1, m2, op=ALU.min))
                    em.op("vector", lambda e: e.tensor_tensor(n2, n1, n2, op=ALU.max))
                    em.op("vector", lambda e: e.tensor_tensor(m1, m2, n2, op=ALU.max))
                    hslot = (it * NCH + tc) % 2
                    happly = None
                    for q in range(4):
                        aq = ab[:, q * 512:(q + 1) * 512]
                        em.op("vector", lambda e, aq=aq: e.tensor_tensor(aq, aq, m1, op=ALU.is_ge))
                        happly = em.op("vector",
                                       lambda e, q=q, aq=aq, hslot=hslot, hs=hs: e.tensor_tensor(
                                           hsp[hslot][:, q * 512:(q + 1) * 512],
                                           hs[:, q * 512:(q + 1) * 512], aq, op=ALU.mult),
                                       waits=(hsp_free[hslot] if q == 0 else None,),
                                       inc="dve" if q == 3 else None)
                    hst_free[tc] = happly
                    if tg == 0 and it == 3 and tc == 0:
                        dbg_dump(tag + "hsp", [128, 2048], hsp[hslot][:], (happly,))
                        dbg_dump(tag + "hstg", [128, 2048], hst[tc][:], (t_h,))
                        dbg_dump(tag + "thr", [128, 2048], mtmp[:], (happly,))
                    for s in range(4):
                        pbk = (IT * 4 + s) % 2
                        pb = pT[pbk]
                        ptok = None
                        for q in range(4):
                            w = [happly, pt_free[pbk]] if q == 0 else []
                            ptok = em.op("tensor",
                                         lambda e, pb=pb, q=q, s=s, hslot=hslot: e.transpose(
                                             pb[:, q * 128:(q + 1) * 128],
                                             hsp[hslot][:, q * 512 + s * 128: q * 512 + s * 128 + 128],
                                             idb[:]),
                                         waits=w, inc="pe" if q == 3 else None)
                        if s == 3:
                            hsp_free[hslot] = ptok
                        cbase = IT * 16 + s * 4
                        dst = hT[:, tc * 8192 + cbase * 128: tc * 8192 + (cbase + 4) * 128]
                        ev_eng = "vector" if s % 2 == 0 else "scalar"
                        ev = em.op(ev_eng, lambda e, pb=pb, dst=dst: ecopy(e, dst, pb[:, 0:512]),
                                   waits=(ptok,), inc="dve" if ev_eng == "vector" else "act")
                        pt_free[pbk] = ev
                        hT_ready.append(ev)
                if paranoid & 4:
                    em.join()

            if tg == 0:
                dbg_dump(tag + "hT", [128, NCH * 8192], hT[:], tuple(hT_ready))

            if paranoid & 2:
                em.join()
            for dq in range(4):
                yevs = []
                for d4 in range(4):
                    gd = dq * 4 + d4
                    dsl = [(2 * gd + j) % 6 for j in range(2)]
                    wdts = {}
                    for j in range(2):
                        for q4 in range(4):
                            cp0 = j * 32 + q4 * 8
                            wdts[j] = em.dma(
                                "sync",
                                wsl[dsl[j]][:].rearrange("p (cp n) -> p cp n", cp=32)[:, q4 * 8:(q4 + 1) * 8, :],
                                bass.AP(wd_s.tensor, cp0 * (128 * 16 * 128) + gd * 128,
                                        [[2048, 128], [128 * 16 * 128, 8], [1, 128]]),
                                waits=(wfree[dsl[j]], ready["wd"]) if q4 == 0 else ())
                    ybk = gd % 2
                    ytok = None
                    ytoks = {}
                    for i in range(NC16):
                        j = i // 32
                        ii = i % 32
                        w = []
                        if i == 0:
                            w = [pA_free[ybk]]
                            if gd == 0:
                                w += hT_ready
                        if ii == 0:
                            w = w + [wdts[j]]
                        yinc = "pe" if ii == 31 else None
                        rhs = hT3[:, :, i * 128:(i + 1) * 128]
                        ytok = em.op("tensor",
                                     lambda e, i=i, ybk=ybk, sl=dsl[j], ii=ii, rhs=rhs: e.matmul(
                                         paf(ybk), wsl[sl][:, ii * 128:(ii + 1) * 128], rhs,
                                         start=(i == 0), stop=(i == NC16 - 1)),
                                     waits=w, inc=yinc)
                        if ii == 31:
                            ytoks[j] = ytok
                    for j in range(2):
                        wfree[dsl[j]] = ytoks[j]
                    ev = em.op("vector",
                               lambda e, d4=d4, ybk=ybk: e.tensor_copy(
                                   out=ytsb[:, d4 * 512:(d4 + 1) * 512], in_=paf(ybk)),
                               waits=(ytok,), inc="dve")
                    pA_free[ybk] = ev
                    yevs.append(ev)
                    if paranoid & 8:
                        em.join()
                for tc in range(NCH):
                    xslot = tc % 2
                    rows = src_ap[row_base + tc * 128: row_base + tc * 128 + 128,
                                  dq * 512:(dq + 1) * 512]
                    xld = em.dma("sync", xb2[xslot][:, 0:512], rows, waits=(xb2_free[xslot],))
                    pbk = tc % 2
                    ptok = None
                    for j in range(4):
                        d4 = j
                        w = [yevs[-1], pB_free[pbk], ready["consts"]] if j == 0 else []
                        ptok = em.op("tensor",
                                     lambda e, pbk=pbk, j=j, d4=d4, tc=tc: e.transpose(
                                         pB[pbk][:, j * 128:(j + 1) * 128],
                                         ytsb[:, d4 * 512 + tc * 128: d4 * 512 + tc * 128 + 128],
                                         idf[:]),
                                     waits=w, inc="pe" if j == 3 else None)
                    t_res = em.op("vector",
                                  lambda e, pbk=pbk, xslot=xslot: e.scalar_tensor_tensor(
                                      out=xb2[xslot][:, 0:512],
                                      in0=pB[pbk][:], scalar=0.5,
                                      in1=xb2[xslot][:, 0:512],
                                      op0=ALU.mult, op1=ALU.add),
                                  waits=(ptok, xld), inc="dve")
                    pB_free[pbk] = t_res
                    st = em.dma("gpsimd",
                                dst_ap[row_base + tc * 128: row_base + tc * 128 + 128,
                                       dq * 512:(dq + 1) * 512],
                                xb2[xslot][:, 0:512], waits=(t_res,))
                    xb2_free[xslot] = st
            if paranoid & 1:
                em.join()
        fs.close()

    def attn(src_ap, dst_ap):
        As = ExitStack()
        asb = lambda name, shape, dt: As.enter_context(nc.sbuf_tensor(name, shape, dt))
        xnT = asb("a_xnT", [128, 16 * 2048], BF16)
        qt_sb = asb("a_qt", [128, 4 * 2048], BF16)
        kt_sb = asb("a_kt", [128, 4 * 2048], BF16)
        v_sb = asb("a_v", [128, 16 * 512], BF16)
        wqsl = [asb(f"a_wq{i}", [128, 2048], BF16) for i in range(2)]
        wksl = [asb(f"a_wk{i}", [128, 2048], BF16) for i in range(2)]
        wvsl = asb("a_wv", [128, 16 * 512], BF16)
        pbuf = [asb(f"a_p{i}", [128, 2048], BF16) for i in range(2)]
        ptb = [asb(f"a_pt{i}", [128, 512], BF16) for i in range(2)]
        og = [asb(f"a_og{i}", [128, 512], F32) for i in range(2)]
        xb = [asb(f"a_xb{i}", [128, 2048], F32) for i in range(2)]
        xnb = [asb(f"a_xnb{i}", [128, 2048], BF16) for i in range(2)]
        rres = asb("a_r", [128, 4], F32)
        a_sq = asb("a_sq", [128, 2048], BF16)
        xrm = [asb(f"a_xrm{i}", [128, 512], F32) for i in range(2)]

        xnT3 = xnT[:].rearrange("p (tc r) -> p tc r", tc=16)
        xb_free = [None, None]
        xn_free = [None, None]
        pbuf_free = [None, None]
        ptb_free = [None, None]
        og_free = [None, None]
        r_free = [None] * 4
        xrm_free = [None, None]
        w_free = [None]
        proj_free = [None]

        for b in range(BC):
            rb = b * S
            xn_evacs_all = []
            for tc in range(16):
                slot = tc % 2
                evs = ln_chunk(src_ap, rb + tc * 128, tc, xnT,
                               xb[slot], xnb[slot], xb_free[slot], xn_free[slot],
                               tc % 2, a_sq)
                xn_free[slot] = evs[-1]
                xb_free[slot] = evs[-1]
                xn_evacs_all.extend(evs)
            qk_free = [None, None, None, None]
            for g in range(4):
                wts = []
                for chh in range(2):
                    c0_ = chh * 8
                    wts.append(em.dma(
                        "sync",
                        wvsl[:].rearrange("p (c n) -> p c n", c=16)[:, c0_:c0_ + 8, :],
                        bass.AP(wv_s.tensor, c0_ * 128 * D + g * 512,
                                [[D, 128], [128 * D, 8], [1, 512]]),
                        waits=(w_free[0], ready["qkv"]) if chh == 0 else ()))
                ev_flip = 0
                proj_evs = []
                lastmm = None
                for mi, (mat, wslr, wsrc, br, dstb) in enumerate(
                        (("q", wqsl, wq_s, bq_r, qt_sb), ("k", wksl, wk_s, bk_r, kt_sb))):
                    for oc in range(4):
                        o = g * 4 + oc
                        slot = oc % 2
                        for chh in range(2):
                            c0_ = chh * 8
                            wt = em.dma(
                                "sync",
                                wslr[slot][:].rearrange("p (c n) -> p c n", c=16)[:, c0_:c0_ + 8, :],
                                bass.AP(wsrc.tensor, c0_ * 128 * D + o * 128,
                                        [[D, 128], [128 * D, 8], [1, 128]]),
                                waits=(qk_free[mi * 2 + slot], ready["qkv"], ready["bias"]) if chh == 0 else ())
                        lastproj = None
                        for w4 in range(4):
                            bk_ = (oc * 4 + w4) % 2
                            for c in range(16):
                                w = ([pA_free[bk_], wt] + xn_evacs_all) if c == 0 else []
                                rhs = xnT3[:, w4 * 4:(w4 + 1) * 4, c * 128:(c + 1) * 128]
                                lastproj = em.op("tensor",
                                      lambda e, wslr=wslr, slot=slot, c=c, bk_=bk_, rhs=rhs: e.matmul(
                                          paf(bk_), wslr[slot][:, c * 128:(c + 1) * 128],
                                          rhs, start=(c == 0), stop=False),
                                      waits=w)
                            mtok = em.op("tensor",
                                         lambda e, br=br, oc=oc, g=g, bk_=bk_, mi=mi: e.matmul(
                                             paf(bk_), br[0:1, (g * 4 + oc) * 128:(g * 4 + oc + 1) * 128],
                                             ones_r[mi * 32: mi * 32 + 1, :], start=False, stop=True),
                                         inc="pe")
                            lastmm = mtok
                            qk_free[mi * 2 + slot] = mtok
                            ev_eng = "vector" if ev_flip % 2 == 0 else "scalar"
                            ev_flip += 1
                            dst = dstb[:, oc * 2048 + w4 * 512: oc * 2048 + (w4 + 1) * 512]
                            scl = CSC if mat == "q" else 1.0
                            if ev_eng == "vector":
                                ev = em.op(ev_eng,
                                           lambda e, bk_=bk_, dst=dst, scl=scl: e.tensor_scalar_mul(
                                               dst, paf(bk_), scl),
                                           waits=(mtok, proj_free[0]), inc="dve")
                            else:
                                ev = em.op(ev_eng,
                                           lambda e, bk_=bk_, dst=dst, scl=scl: e.activation(
                                               dst, paf(bk_), AF.Identity, scale=scl),
                                           waits=(mtok, proj_free[0]), inc="act")
                            pA_free[bk_] = ev
                            proj_evs.append(ev)
                for tc in range(16):
                    bk_ = tc % 2
                    for c in range(16):
                        w = [pA_free[bk_]] if c == 0 else []
                        if tc == 0 and c == 0:
                            w = w + wts
                        stat = xnT[:, tc * 2048 + c * 128: tc * 2048 + (c + 1) * 128]
                        em.op("tensor",
                              lambda e, stat=stat, c=c, bk_=bk_: e.matmul(
                                  paf(bk_), stat, wvsl[:, c * 512:(c + 1) * 512],
                                  start=(c == 0), stop=False),
                              waits=w)
                    mtok = em.op("tensor",
                                 lambda e, g=g, bk_=bk_: e.matmul(
                                     paf(bk_), ones_r[64:65, 0:128], bv_r[0:1, g * 512:(g + 1) * 512],
                                     start=False, stop=True),
                                 inc="pe")
                    lastmm = mtok
                    ev_eng = "vector" if tc % 2 == 0 else "scalar"
                    ev = em.op(ev_eng,
                               lambda e, tc=tc, bk_=bk_: ecopy(
                                   e, v_sb[:, tc * 512:(tc + 1) * 512], paf(bk_)),
                               waits=(mtok, proj_free[0]),
                               inc="dve" if ev_eng == "vector" else "act")
                    pA_free[bk_] = ev
                    proj_evs.append(ev)
                w_free[0] = lastmm

                otok = None
                for Q in range(16):
                    oQev = None
                    for oc in range(4):
                        KW = (Q + 4) // 4
                        span = (Q + 1) * 128
                        for kw in range(KW):
                            kc = min(512, span - kw * 512)
                            w = [pA_free[kw]]
                            if Q == 0 and oc == 0 and kw == 0:
                                w += proj_evs
                            em.op("tensor",
                                  lambda e, oc=oc, Q=Q, kw=kw, kc=kc: e.matmul(
                                      paf(kw, kc),
                                      qt_sb[:, oc * 2048 + Q * 128: oc * 2048 + (Q + 1) * 128],
                                      kt_sb[:, oc * 2048 + kw * 512: oc * 2048 + kw * 512 + kc],
                                      start=True, stop=(kw != KW - 1)),
                                  waits=w)
                        smtok = em.op("tensor",
                                      lambda e, Q=Q: e.matmul(
                                          pSh[:, Q * 128:(Q + 1) * 128],
                                          mtriT[:], idb[:], start=False, stop=True),
                                      inc="pe")
                        par = (Q * 4 + oc) % 2
                        mc = stt[:, 32 + par * 16: 32 + (par + 1) * 16]
                        tnm = em.op("vector",
                                    lambda e, span=span, mc=mc: e.reduce_max(
                                        mc[:, 0:1], pSh[:, 0:span], axis=AX.X, negate=True),
                                    waits=(smtok,), inc="dve")
                        pslot = (Q * 4 + oc) % 2
                        et = em.op("scalar",
                                   lambda e, span=span, pslot=pslot, mc=mc: e.activation(
                                       pbuf[pslot][:, 0:span], pSh[:, 0:span], AF.Exp,
                                       bias=mc[:, 0:1], accum_out=mc[:, 9:10]),
                                   waits=(tnm, pbuf_free[pslot]), inc="act")
                        etoks = [et] * KW
                        for kw in range(KW):
                            pA_free[kw] = et
                        rslot = (Q * 4 + oc) % 4
                        trec = em.op("vector",
                                     lambda e, mc=mc, rslot=rslot: e.reciprocal(
                                         rres[:, rslot:rslot + 1], mc[:, 9:10]),
                                     waits=(et,), fwaits=(r_free[rslot],), inc="dve")
                        obk = (Q * 4 + oc) % 2
                        NKT = Q + 1
                        for ktg in range(ceil_div(NKT, 4)):
                            nk = min(4, NKT - ktg * 4)
                            pbk = ktg % 2
                            ptok = None
                            for j in range(nk):
                                kt = ktg * 4 + j
                                w = [etoks[ktg], pt_free[pbk]] if j == 0 else []
                                ptok = em.op("tensor",
                                             lambda e, pbk=pbk, j=j, kt=kt, pslot=pslot: e.transpose(
                                                 pT[pbk][:, j * 128:(j + 1) * 128],
                                                 pbuf[pslot][:, kt * 128:(kt + 1) * 128], idb[:]),
                                             waits=w, inc="pe" if j == nk - 1 else None)
                            ptev = em.op("vector",
                                         lambda e, pbk=pbk, nk=nk: e.tensor_copy(
                                             out=ptb[pbk][:, 0:nk * 128], in_=pT[pbk][:, 0:nk * 128]),
                                         waits=(ptok, ptb_free[pbk]), inc="dve")
                            pt_free[pbk] = ptev
                            for j in range(nk):
                                kt = ktg * 4 + j
                                w = [ptev]
                                if ktg == 0 and j == 0:
                                    w.append(pB_free[obk])
                                otok = em.op("tensor",
                                             lambda e, pbk=pbk, j=j, kt=kt, oc=oc, obk=obk: e.matmul(
                                                 pB[obk][:, 0:128],
                                                 ptb[pbk][:, j * 128:(j + 1) * 128],
                                                 v_sb[:, kt * 512 + oc * 128: kt * 512 + (oc + 1) * 128],
                                                 start=(kt == 0), stop=(kt == NKT - 1)),
                                             waits=w, inc="pe" if kt == NKT - 1 else None)
                            ptb_free[pbk] = otok
                        pbuf_free[pslot] = otok
                        oslot = Q % 2
                        oev = em.op("vector",
                                    lambda e, oc=oc, obk=obk, rslot=rslot, oslot=oslot: e.tensor_scalar(
                                        out=og[oslot][:, oc * 128:(oc + 1) * 128],
                                        in0=pB[obk][:, 0:128], scalar1=rres[:, rslot:rslot + 1],
                                        scalar2=None, op0=ALU.mult),
                                    waits=(otok, og_free[oslot] if oc == 0 else None),
                                    fwaits=(trec,), inc="dve")
                        pB_free[obk] = oev
                        r_free[rslot] = oev
                        oQev = oev
                    xs = Q % 2
                    ld = em.dma("sync", xrm[xs][:],
                                src_ap[rb + Q * 128: rb + (Q + 1) * 128, g * 512:(g + 1) * 512],
                                waits=(xrm_free[xs],))
                    oslot = Q % 2
                    tadd = em.op("vector",
                                 lambda e, xs=xs, oslot=oslot: e.tensor_tensor(
                                     xrm[xs][:], xrm[xs][:], og[oslot][:], op=ALU.add),
                                 waits=(ld, oQev), inc="dve")
                    st = em.dma("gpsimd",
                                dst_ap[rb + Q * 128: rb + (Q + 1) * 128, g * 512:(g + 1) * 512],
                                xrm[xs][:], waits=(tadd,))
                    xrm_free[xs] = st
                    og_free[oslot] = tadd
                proj_free[0] = otok
        As.close()

    setup_consts()
    preconvert()
    if "dumpw" in parts:
        em.join()
        dump_out = nc.dram_tensor("dump_out", [16 * 128, NC16 * 128], BF16,
                                  kind="ExternalOutput").ap()
        rb = sb("dump_rb", [128, NC16 * 128], BF16)
        for blk in range(16):
            ldd = em.dma("sync", rb[:], bass.AP(wd_s.tensor, blk * (128 * NC16 * 128),
                                                [[NC16 * 128, 128], [1, NC16 * 128]]))
            std = em.dma("gpsimd", dump_out[blk * 128:(blk + 1) * 128, :], rb[:], waits=(ldd,))
            em.join()
    if "ffn1" in parts:
        ffn(x_in, xr if ("attn" in parts or "ffn2" in parts) else y_out, "f1")
        em.join()
    if "attn" in parts:
        src = xr if "ffn1" in parts else x_in
        dst = xr if "ffn2" in parts else y_out
        attn(src, dst)
        em.join()
    if "ffn2" in parts:
        src = xr if ("ffn1" in parts or "attn" in parts) else x_in
        ffn(src, y_out, "f2")
        em.join()

    for eng in ENGS:
        for q in ("qs", "qg", "qa"):
            for i in range(Em.NLANE):
                em._emit_wait(eng, f"{q}{i}", em.counts[f"{q}{i}"])

    with stack:
        with nc.Block() as block:
            em.finalize(block)
    return nc


_PROG_CACHE = {}


def _get_prog(parts):
    key = tuple(parts)
    if key not in _PROG_CACHE:
        _PROG_CACHE[key] = build_program(parts)
    return _PROG_CACHE[key]


def kernel(L0_attn_x, L0_attn_Wq, L0_attn_bq, L0_attn_Wk, L0_attn_bk,
           L0_attn_Wv, L0_attn_bv, L0_ffn_w_gate, L0_ffn_w_up, L0_ffn_w_down,
           parts=("ffn1", "attn", "ffn2"), core_ids=None):
    x = np.ascontiguousarray(np.asarray(L0_attn_x, dtype=np.float32))
    weights = {
        "wq": np.ascontiguousarray(np.asarray(L0_attn_Wq, np.float32)),
        "wk": np.ascontiguousarray(np.asarray(L0_attn_Wk, np.float32)),
        "wv": np.ascontiguousarray(np.asarray(L0_attn_Wv, np.float32)),
        "bq": np.ascontiguousarray(np.asarray(L0_attn_bq, np.float32)),
        "bk": np.ascontiguousarray(np.asarray(L0_attn_bk, np.float32)),
        "bv": np.ascontiguousarray(np.asarray(L0_attn_bv, np.float32)),
        "wg": np.ascontiguousarray(np.asarray(L0_ffn_w_gate, np.float32)),
        "wu": np.ascontiguousarray(np.asarray(L0_ffn_w_up, np.float32)),
        "wd": np.ascontiguousarray(np.asarray(L0_ffn_w_down, np.float32)),
    }
    nc = _get_prog(parts)
    ncore = NCORES
    in_maps = []
    for core in range(ncore):
        m = dict(weights)
        m["x_in"] = x[core * BC:(core + 1) * BC].reshape(NT, D)
        in_maps.append(m)
    res = run_bass_kernel_spmd(nc, in_maps, core_ids=list(range(ncore)))
    out = np.empty((B, S, D), np.float32)
    for core in range(ncore):
        out[core * BC:(core + 1) * BC] = res.results[core]["y_out"].reshape(BC, S, D)
    return out


# version 09
# speedup vs baseline: 1.1231x; 1.1068x over previous
import numpy as np
import ml_dtypes
from contextlib import ExitStack

import concourse.bass as bass
import concourse.mybir as mybir
from concourse.bass_utils import run_bass_kernel_spmd

F32 = mybir.dt.float32
BF16 = mybir.dt.bfloat16
AF = mybir.ActivationFunctionType
ALU = mybir.AluOpType
AX = mybir.AxisListType

B, S, D, I, H, DH = 16, 2048, 2048, 8192, 16, 128
NCORES = 8
BC = B // NCORES
NT = BC * S
TG = 512
NTG = NT // TG
NCH = TG // 128
NIT = I // 512
NC16 = I // 128
CSC = float(1.0 / np.sqrt(DH))
MASKVAL = -1e9

ENGS = ("tensor", "vector", "scalar", "sync", "gpsimd")
DMA_SEM = {"sync": "qs", "gpsimd": "qg", "scalar": "qa"}


class Em:

    NLANE = 8

    def __init__(self, nc, stack):
        self.nc = nc
        self.streams = {e: [] for e in ENGS}
        self.sems = {}
        self.counts = {}
        names = ["pe", "dve", "act", "pool", "jn"]
        for q in ("qs", "qg", "qa"):
            names += [f"{q}{i}" for i in range(self.NLANE)]
        for name in names:
            self.sems[name] = stack.enter_context(nc.semaphore(name))
            self.counts[name] = 0
        self.qidx = {"qs": 0, "qg": 0, "qa": 0}
        self.waited = {e: {} for e in ENGS}
        self.njoin = 0

    def _emit_wait(self, eng, sem, val):
        if val is None or val <= 0:
            return
        if self.waited[eng].get(sem, 0) >= val:
            return
        self.waited[eng][sem] = val
        h = self.sems[sem]
        self.streams[eng].append(lambda e, h=h, v=val: e.wait_ge(h, v))

    def _coalesce(self, waits, skip_eng=None):
        best = {}
        for t in waits:
            if t is None:
                continue
            sem, val, prod = t
            if prod == skip_eng:
                continue
            if best.get(sem, 0) < val:
                best[sem] = val
        return best

    def op(self, eng, fn, waits=(), inc=None, fwaits=()):
        for sem, val in self._coalesce(waits, skip_eng=eng).items():
            self._emit_wait(eng, sem, val)
        for sem, val in self._coalesce(fwaits).items():
            self._emit_wait(eng, sem, val)
        tok = None
        if inc:
            self.counts[inc] += 1
            v = self.counts[inc]
            h = self.sems[inc]
            self.streams[eng].append(lambda e, fn=fn, h=h, v=v: fn(e).then_inc(h, 1))
            tok = (inc, v, eng)
        else:
            self.streams[eng].append(lambda e, fn=fn: fn(e))
        return tok

    def dma(self, eng, out, in_, waits=(), **kw):
        for sem, val in self._coalesce(waits).items():
            self._emit_wait(eng, sem, val)
        q = DMA_SEM[eng]
        lane = f"{q}{self.qidx[q] % self.NLANE}"
        self.qidx[q] += 1
        self.counts[lane] += 16
        v = self.counts[lane]
        h = self.sems[lane]
        self.streams[eng].append(
            lambda e, o=out, i=in_, h=h, v=v, kw=kw: e.dma_start(out=o, in_=i, **kw).then_inc(h, 16))
        return (lane, v, "dma")

    def join(self):
        for eng in ("sync", "gpsimd", "scalar"):
            q = DMA_SEM[eng]
            for i in range(self.NLANE):
                self._emit_wait(eng, f"{q}{i}", self.counts[f"{q}{i}"])
        self.njoin += 1
        target = 5 * self.njoin
        h = self.sems["jn"]
        for eng in ENGS:
            self.streams[eng].append(lambda e, h=h: e.sem_inc(h, 1))
            self.streams[eng].append(lambda e, h=h, t=target: e.wait_ge(h, t))
            self.waited[eng] = {}

    def finalize(self, block):
        for eng in ENGS:
            stream = self.streams[eng]

            def runner(e, stream=stream):
                for fn in stream:
                    fn(e)
            getattr(block, eng)(runner)


def ceil_div(a, b):
    return (a + b - 1) // b


def ecopy(e, dst, src):
    if hasattr(e, "tensor_copy"):
        return e.tensor_copy(out=dst, in_=src)
    return e.copy(dst, src)


def build_program(parts=("ffn1", "attn", "ffn2"), dbg=(), paranoid=0):
    nc = bass.Bass("TRN2", target_bir_lowering=False)
    stack = ExitStack()

    x_in = nc.dram_tensor("x_in", [NT, D], F32, kind="ExternalInput").ap()
    y_out = nc.dram_tensor("y_out", [NT, D], F32, kind="ExternalOutput").ap()
    wq = nc.dram_tensor("wq", [D, D], F32, kind="ExternalInput").ap()
    wk = nc.dram_tensor("wk", [D, D], F32, kind="ExternalInput").ap()
    wv = nc.dram_tensor("wv", [D, D], F32, kind="ExternalInput").ap()
    bq = nc.dram_tensor("bq", [D], F32, kind="ExternalInput").ap()
    bk = nc.dram_tensor("bk", [D], F32, kind="ExternalInput").ap()
    bv = nc.dram_tensor("bv", [D], F32, kind="ExternalInput").ap()
    wg = nc.dram_tensor("wg", [D, I], F32, kind="ExternalInput").ap()
    wu = nc.dram_tensor("wu", [D, I], F32, kind="ExternalInput").ap()
    wd = nc.dram_tensor("wd", [I, D], F32, kind="ExternalInput").ap()

    xr = nc.dram_tensor("xr", [NT, D], F32, kind="Internal").ap()
    wg_s = nc.dram_tensor("wg_s", [NIT, 128, 16, 512], BF16, kind="Internal").ap()
    wu_s = nc.dram_tensor("wu_s", [NIT, 128, 16, 512], BF16, kind="Internal").ap()
    wd_s = nc.dram_tensor("wd_s", [NC16, 128, 16, 128], BF16,
                          kind="ExternalOutput" if "wdout" in parts else "Internal").ap()
    wq_s = nc.dram_tensor("wq_s", [D, D], BF16, kind="Internal").ap()
    wk_s = nc.dram_tensor("wk_s", [D, D], BF16, kind="Internal").ap()
    wv_s = nc.dram_tensor("wv_s", [D, D], BF16, kind="Internal").ap()

    em = Em(nc, stack)
    sb = lambda name, shape, dt: stack.enter_context(nc.sbuf_tensor(name, shape, dt))
    ps = lambda name, shape, dt: stack.enter_context(nc.psum_tensor(name, shape, dt))

    dbg_t = {}
    def dbg_dump(name, shape, sbuf_ap, waits):
        if name not in dbg:
            return
        if name not in dbg_t:
            dbg_t[name] = nc.dram_tensor("dbg_" + name, shape, F32, kind="ExternalOutput").ap()
        em.dma("gpsimd", dbg_t[name], sbuf_ap, waits=waits)

    ready = {"consts": None, "bias": None, "wgu": [None] * NIT, "qkv": None, "wd": None}

    pSh = ps("pSh", [128, 2048], F32)
    paf = lambda i, kc=512: pSh[:, i * 512: i * 512 + kc]
    pB = [ps(f"pB{i}", [128, 512], F32) for i in range(2)]
    pT = [ps(f"pT{i}", [128, 1024], BF16) for i in range(2)]

    idb = sb("idb", [128, 128], BF16)
    idf = sb("idf", [128, 128], F32)
    mtri = sb("mtri", [128, 128], F32)
    mtriT = sb("mtriT", [128, 128], BF16)
    eps_t = sb("eps_t", [128, 1], F32)
    ones_r = sb("ones_r", [65, 512], BF16)
    bias3 = sb("bias3", [65, D], BF16)
    bq_r = bias3[0:1, :]
    bk_r = bias3[32:33, :]
    bv_r = bias3[64:65, :]
    stt = sb("stt", [128, 64], F32)

    def setup_consts():
        em.op("gpsimd", lambda e: e.memset(idb[:], 0.0))
        em.op("gpsimd", lambda e: e.affine_select(
            out=idb[:], in_=idb[:], compare_op=ALU.not_equal, fill=1.0,
            base=0, pattern=[[-1, 128]], channel_multiplier=1))
        em.op("gpsimd", lambda e: e.memset(idf[:], 0.0))
        em.op("gpsimd", lambda e: e.affine_select(
            out=idf[:], in_=idf[:], compare_op=ALU.not_equal, fill=1.0,
            base=0, pattern=[[-1, 128]], channel_multiplier=1))
        em.op("gpsimd", lambda e: e.memset(mtri[:], 0.0))
        em.op("gpsimd", lambda e: e.affine_select(
            out=mtri[:], in_=mtri[:], compare_op=ALU.is_ge, fill=MASKVAL,
            base=0, pattern=[[-1, 128]], channel_multiplier=1))
        em.op("gpsimd", lambda e: e.memset(mtriT[:], 0.0))
        em.op("gpsimd", lambda e: e.affine_select(
            out=mtriT[:], in_=mtriT[:], compare_op=ALU.is_ge, fill=MASKVAL,
            base=0, pattern=[[1, 128]], channel_multiplier=-1))
        em.op("gpsimd", lambda e: e.memset(eps_t[:], 1e-5))
        t = em.op("gpsimd", lambda e: e.memset(ones_r[:], 1.0), inc="pool")
        ready["consts"] = t
        em.dma("gpsimd", bq_r, bq[None, :])
        em.dma("gpsimd", bk_r, bk[None, :])
        ready["bias"] = em.dma("gpsimd", bv_r, bv[None, :])

    def preconvert():
        t = None
        for it in range(NIT):
            for src, dst in ((wg, wg_s), (wu, wu_s)):
                for ch in range(2):
                    c0 = ch * 8
                    sap = bass.AP(src.tensor, c0 * 128 * I + it * 512,
                                  [[128 * I, 8], [I, 128], [1, 512]])
                    dap = dst[it][:, c0:c0 + 8, :].rearrange("p c n -> c p n")
                    t = em.dma("gpsimd", dap, sap)
            ready["wgu"][it] = t
        pre_s = ExitStack()
        wdbuf = [pre_s.enter_context(nc.sbuf_tensor(f"wdc{i}", [128, 2048], BF16))
                 for i in range(2)]
        free_tok = [None, None]
        for cp in range(NC16):
            slot = cp % 2
            base = (cp // 4) * 512 + (cp % 4)
            sap = bass.AP(wd.tensor, base * D, [[4 * D, 128], [1, D]])
            ld = em.dma("gpsimd", wdbuf[slot][:], sap, waits=(free_tok[slot],))
            dap = bass.AP(wd_s.tensor, cp * (128 * 16 * 128), [[2048, 128], [1, 2048]])
            st_tok = em.dma("sync", dap, wdbuf[slot][:], waits=(ld,))
            free_tok[slot] = st_tok
        ready["wd"] = st_tok
        pre_s.close()
        for src_, dst in ((wq, wq_s), (wk, wk_s), (wv, wv_s)):
            for ch in range(4):
                r0 = ch * 512
                sap = bass.AP(src_.tensor, r0 * D, [[D, 512], [1, D]])
                dap = bass.AP(dst.tensor, r0 * D, [[D, 512], [1, D]])
                t = em.dma("gpsimd", dap, sap)
        ready["qkv"] = t

    pt_free = [None, None]
    pA_free = [None, None, None, None]
    pB_free = [None, None]

    def ln_chunk(src_ap, row0, tc, xnT_buf, xbuf, xnbuf, xb_free, xn_free, par, sq_scr):
        sc = stt[:, par * 16: par * 16 + 16]
        ld = em.dma("sync", xbuf[:], src_ap[row0:row0 + 128, :], waits=(xb_free,))
        t0 = em.op("vector", lambda e: e.reduce_sum(sc[:, 0:1], xbuf[:], axis=AX.X),
                   waits=(ld,), inc="dve")
        t_sq = em.op("vector", lambda e: e.scalar_tensor_tensor(
                         out=sq_scr[:], in0=xbuf[:], scalar=1.0, in1=xbuf[:],
                         op0=ALU.mult, op1=ALU.mult, accum_out=sc[:, 1:2]),
                     waits=(ld,), inc="dve")
        t1 = em.op("vector", lambda e: e.tensor_scalar_mul(sc[:, 2:3], sc[:, 0:1], 1.0 / D),
                   fwaits=(t0,), inc="dve")
        t2 = em.op("vector", lambda e: e.tensor_tensor(sc[:, 3:4], sc[:, 2:3], sc[:, 2:3], op=ALU.mult),
                   fwaits=(t1,), inc="dve")
        t3 = em.op("vector", lambda e: e.tensor_scalar_mul(sc[:, 4:5], sc[:, 1:2], 1.0 / D),
                   fwaits=(t_sq,), inc="dve")
        t_var = em.op("vector", lambda e: e.tensor_tensor(sc[:, 5:6], sc[:, 4:5], sc[:, 3:4], op=ALU.subtract),
                      fwaits=(t2, t3), inc="dve")
        t_sd = em.op("scalar", lambda e: e.activation(sc[:, 6:7], sc[:, 5:6], AF.Sqrt, bias=eps_t[:]),
                     waits=(t_var, ready["consts"]), inc="act")
        t5 = em.op("vector", lambda e: e.reciprocal(sc[:, 7:8], sc[:, 6:7]),
                   waits=(t_sd,), inc="dve")
        t6 = em.op("vector", lambda e: e.tensor_tensor(sc[:, 8:9], sc[:, 2:3], sc[:, 7:8], op=ALU.mult),
                   fwaits=(t5,), inc="dve")
        t_bias = em.op("vector", lambda e: e.tensor_scalar_mul(sc[:, 9:10], sc[:, 8:9], -1.0),
                       fwaits=(t6,), inc="dve")
        t_xn = em.op("scalar", lambda e: e.activation(xnbuf[:], xbuf[:], AF.Identity,
                                                      scale=sc[:, 7:8], bias=sc[:, 9:10]),
                     waits=(t_bias, xn_free), inc="act")
        evac_toks = []
        for bnk in range(4):
            pbk = bnk % 2
            pb = pT[pbk]
            ptok = None
            for j in range(4):
                c = bnk * 4 + j
                w = [t_xn, pt_free[pbk], ready["consts"]] if j == 0 else []
                ptok = em.op("tensor",
                             lambda e, c=c, pb=pb, j=j: e.transpose(
                                 pb[:, j * 128:(j + 1) * 128],
                                 xnbuf[:, c * 128:(c + 1) * 128], idb[:]),
                             waits=w, inc="pe" if j == 3 else None)
            dst = xnT_buf[:, tc * 2048 + bnk * 512: tc * 2048 + (bnk + 1) * 512]
            ev_eng = "vector" if bnk % 2 == 0 else "scalar"
            ev = em.op(ev_eng, lambda e, pb=pb, dst=dst: ecopy(e, dst, pb[:, 0:512]),
                       waits=(ptok,), inc="dve" if ev_eng == "vector" else "act")
            pt_free[pbk] = ev
            evac_toks.append(ev)
        return evac_toks

    def ffn(src_ap, dst_ap, tag):
        fs = ExitStack()
        fsb = lambda name, shape, dt: fs.enter_context(nc.sbuf_tensor(tag + name, shape, dt))
        xnT = fsb("xnT", [128, NCH * 2048], BF16)
        hT = fsb("hT", [128, NCH * 8192], BF16)
        wsl = [fsb(f"w{i}", [128, 4096], BF16) for i in range(6)]
        xb = [fsb(f"xb{i}", [128, 2048], F32) for i in range(2)]
        xb2 = [fsb(f"xb2{i}", [128, 512], F32) for i in range(2)]
        xnb = [fsb(f"xnb{i}", [128, 2048], BF16) for i in range(2)]
        sg = [fsb(f"sg{i}", [128, 512], BF16) for i in range(2)]
        hst = [fsb(f"hst{i}", [128, 2048], BF16) for i in range(4)]
        ab = fsb("ab", [128, 2048], BF16)
        mtmp = fsb("mtmp", [128, 2048], BF16)
        hsp = [fsb(f"hsp{i}", [128, 2048], BF16) for i in range(2)]
        ytsb = fsb("ytsb", [128, 4 * 512], F32)

        xb_free = [None, None]
        xb2_free = [None, None]
        xn_free = [None, None]
        sg_free = [None, None]
        hst_free = [None, None, None, None]
        hsp_free = [None, None]
        wfree = [None] * 6
        hT3 = hT[:].rearrange("p (tc r) -> p tc r", tc=NCH)

        def emit_ln_group(tg):
            evs_all = []
            for tc in range(NCH):
                slot = tc % 2
                evs = ln_chunk(src_ap, tg * TG + tc * 128, tc, xnT,
                               xb[slot], xnb[slot], xb_free[slot], xn_free[slot],
                               tc % 2, mtmp)
                xn_free[slot] = evs[-1]
                xb_free[slot] = evs[-1]
                evs_all.extend(evs)
            return evs_all

        xn_evacs = emit_ln_group(0)
        if "f1xnT" in [tag + "xnT"] and tag + "xnT" in dbg:
            dbg_dump(tag + "xnT", [128, NCH * 2048], xnT[:], tuple(xn_evacs))
            dbg_dump(tag + "stt", [128, 64], stt[:], tuple(xn_evacs))

        for tg in range(NTG):
            row_base = tg * TG
            hT_ready = []
            gtok = utok = None
            pending_tr = [None]

            def flush_tr():
                if pending_tr[0] is not None:
                    f = pending_tr[0]
                    pending_tr[0] = None
                    f()

            for it in range(NIT):
                gs = [(2 * it + h) % 3 for h in range(2)]
                us = [3 + (2 * it + h) % 3 for h in range(2)]
                wts = {}
                for h in range(2):
                    wts[("g", h)] = em.dma(
                        "sync", wsl[gs[h]][:],
                        bass.AP(wg_s.tensor, it * (128 * 16 * 512) + h * 4096,
                                [[8192, 128], [1, 4096]]),
                        waits=(wfree[gs[h]], ready["wgu"][it]))
                    wts[("u", h)] = em.dma(
                        "sync", wsl[us[h]][:],
                        bass.AP(wu_s.tensor, it * (128 * 16 * 512) + h * 4096,
                                [[8192, 128], [1, 4096]]),
                        waits=(wfree[us[h]], ready["wgu"][it]))
                IT, s4 = it // 4, it % 4
                gq = [None] * 2
                uq = [None] * 2
                for tc in range(NCH):
                    gb = (it * NCH + tc) % 2
                    ub = 2 + gb
                    for k in range(16):
                        h = k // 8
                        kk = k % 8
                        stat = xnT[:, tc * 2048 + k * 128: tc * 2048 + (k + 1) * 128]
                        w = []
                        if k == 0:
                            w = [pA_free[gb], pA_free[ub]]
                            w.extend(xn_evacs[tc * 4: tc * 4 + 4])
                        if kk == 0:
                            w = w + [wts[("g", h)], wts[("u", h)]]
                        ginc = "pe" if kk == 7 else None
                        gtok = em.op("tensor",
                                     lambda e, stat=stat, gb=gb, sl=gs[h], kk=kk, k=k: e.matmul(
                                         paf(gb), stat, wsl[sl][:, kk * 512:(kk + 1) * 512],
                                         start=(k == 0), stop=(k == 15)),
                                     waits=w, inc=ginc)
                        utok = em.op("tensor",
                                     lambda e, stat=stat, ub=ub, sl=us[h], kk=kk, k=k: e.matmul(
                                         paf(ub), stat, wsl[sl][:, kk * 512:(kk + 1) * 512],
                                         start=(k == 0), stop=(k == 15)),
                                     inc=ginc)
                        if kk == 7:
                            gq[h] = gtok
                            uq[h] = utok
                    if tc == NCH - 1:
                        for h in range(2):
                            wfree[gs[h]] = gq[h]
                            wfree[us[h]] = uq[h]
                    flush_tr()
                    sslot = (it * NCH + tc) % 2
                    t_silu = em.op("scalar",
                                   lambda e, gb=gb, sslot=sslot: e.activation(
                                       sg[sslot][:], paf(gb), AF.Silu),
                                   waits=(gtok, sg_free[sslot]), inc="act")
                    pA_free[gb] = t_silu
                    hdst = hst[tc][:].rearrange("p (q r) -> p q r", q=4)[:, :, s4 * 128:(s4 + 1) * 128]
                    t_h = em.op("vector",
                                lambda e, sslot=sslot, ub=ub, hdst=hdst: e.tensor_tensor(
                                    hdst,
                                    sg[sslot][:].rearrange("p (j q) -> p q j", q=4),
                                    paf(ub).rearrange("p (j q) -> p q j", q=4),
                                    op=ALU.mult),
                                waits=(t_silu, utok,
                                       hst_free[tc] if s4 == 0 else None),
                                inc="dve")
                    pA_free[ub] = t_h
                    sg_free[sslot] = t_h
                    if s4 < 3:
                        continue
                    hs = hst[tc]
                    t_abs = em.op("scalar",
                                  lambda e, hs=hs: e.activation(ab[:], hs[:], AF.Abs),
                                  waits=(t_h,), inc="act")
                    q0, q1 = ab[:, 0:512], ab[:, 512:1024]
                    q2, q3 = ab[:, 1024:1536], ab[:, 1536:2048]
                    m1, n1 = mtmp[:, 0:512], mtmp[:, 512:1024]
                    m2, n2 = mtmp[:, 1024:1536], mtmp[:, 1536:2048]
                    em.op("vector", lambda e: e.tensor_tensor(m1, q0, q1, op=ALU.max), waits=(t_abs,))
                    em.op("vector", lambda e: e.tensor_tensor(n1, q0, q1, op=ALU.min))
                    em.op("vector", lambda e: e.tensor_tensor(m2, q2, q3, op=ALU.max))
                    em.op("vector", lambda e: e.tensor_tensor(n2, q2, q3, op=ALU.min))
                    em.op("vector", lambda e: e.tensor_tensor(m2, m1, m2, op=ALU.min))
                    em.op("vector", lambda e: e.tensor_tensor(n2, n1, n2, op=ALU.max))
                    em.op("vector", lambda e: e.tensor_tensor(m1, m2, n2, op=ALU.max))
                    hslot = (it * NCH + tc) % 2
                    happly = None
                    for q in range(4):
                        aq = ab[:, q * 512:(q + 1) * 512]
                        em.op("vector", lambda e, aq=aq: e.tensor_tensor(aq, aq, m1, op=ALU.is_ge))
                        happly = em.op("vector",
                                       lambda e, q=q, aq=aq, hslot=hslot, hs=hs: e.tensor_tensor(
                                           hsp[hslot][:, q * 512:(q + 1) * 512],
                                           hs[:, q * 512:(q + 1) * 512], aq, op=ALU.mult),
                                       waits=(hsp_free[hslot] if q == 0 else None,),
                                       inc="dve" if q == 3 else None)
                    hst_free[tc] = happly

                    def do_tr(IT=IT, tc=tc, hslot=hslot, happly=happly):
                        for s in range(4):
                            pbk = (IT * 4 + s) % 2
                            pb = pT[pbk]
                            ptok = None
                            for q in range(4):
                                w = [happly, pt_free[pbk]] if q == 0 else []
                                ptok = em.op("tensor",
                                             lambda e, pb=pb, q=q, s=s, hslot=hslot: e.transpose(
                                                 pb[:, q * 128:(q + 1) * 128],
                                                 hsp[hslot][:, q * 512 + s * 128: q * 512 + s * 128 + 128],
                                                 idb[:]),
                                             waits=w, inc="pe" if q == 3 else None)
                            if s == 3:
                                hsp_free[hslot] = ptok
                            cbase = IT * 16 + s * 4
                            dst = hT[:, tc * 8192 + cbase * 128: tc * 8192 + (cbase + 4) * 128]
                            ev_eng = "vector" if s % 2 == 0 else "scalar"
                            ev = em.op(ev_eng, lambda e, pb=pb, dst=dst: ecopy(e, dst, pb[:, 0:512]),
                                       waits=(ptok,), inc="dve" if ev_eng == "vector" else "act")
                            pt_free[pbk] = ev
                            hT_ready.append(ev)
                    pending_tr[0] = do_tr
            flush_tr()

            if tg + 1 < NTG:
                next_evacs = emit_ln_group(tg + 1)
            else:
                next_evacs = None

            for dq in range(4):
                yevs = []
                for d4 in range(4):
                    gd = dq * 4 + d4
                    dsl = [(2 * gd + j) % 6 for j in range(2)]
                    wdts = {}
                    for j in range(2):
                        for q4 in range(4):
                            cp0 = j * 32 + q4 * 8
                            wdts[j] = em.dma(
                                "sync",
                                wsl[dsl[j]][:].rearrange("p (cp n) -> p cp n", cp=32)[:, q4 * 8:(q4 + 1) * 8, :],
                                bass.AP(wd_s.tensor, cp0 * (128 * 16 * 128) + gd * 128,
                                        [[2048, 128], [128 * 16 * 128, 8], [1, 128]]),
                                waits=(wfree[dsl[j]], ready["wd"]) if q4 == 0 else ())
                    ybk = gd % 2
                    ytok = None
                    ytoks = {}
                    for i in range(NC16):
                        j = i // 32
                        ii = i % 32
                        w = []
                        if i == 0:
                            w = [pA_free[ybk]]
                            if gd == 0:
                                w += hT_ready
                        if ii == 0:
                            w = w + [wdts[j]]
                        yinc = "pe" if ii == 31 else None
                        rhs = hT3[:, :, i * 128:(i + 1) * 128]
                        ytok = em.op("tensor",
                                     lambda e, i=i, ybk=ybk, sl=dsl[j], ii=ii, rhs=rhs: e.matmul(
                                         paf(ybk), wsl[sl][:, ii * 128:(ii + 1) * 128], rhs,
                                         start=(i == 0), stop=(i == NC16 - 1)),
                                     waits=w, inc=yinc)
                        if ii == 31:
                            ytoks[j] = ytok
                    for j in range(2):
                        wfree[dsl[j]] = ytoks[j]
                    ev = em.op("vector",
                               lambda e, d4=d4, ybk=ybk: e.tensor_copy(
                                   out=ytsb[:, d4 * 512:(d4 + 1) * 512], in_=paf(ybk)),
                               waits=(ytok,), inc="dve")
                    pA_free[ybk] = ev
                    yevs.append(ev)
                for tc in range(NCH):
                    xslot = tc % 2
                    rows = src_ap[row_base + tc * 128: row_base + tc * 128 + 128,
                                  dq * 512:(dq + 1) * 512]
                    xld = em.dma("scalar", xb2[xslot][:, 0:512], rows, waits=(xb2_free[xslot],))
                    pbk = tc % 2
                    ptok = None
                    for j in range(4):
                        d4 = j
                        w = [yevs[-1], pB_free[pbk], ready["consts"]] if j == 0 else []
                        ptok = em.op("tensor",
                                     lambda e, pbk=pbk, j=j, d4=d4, tc=tc: e.transpose(
                                         pB[pbk][:, j * 128:(j + 1) * 128],
                                         ytsb[:, d4 * 512 + tc * 128: d4 * 512 + tc * 128 + 128],
                                         idf[:]),
                                     waits=w, inc="pe" if j == 3 else None)
                    t_res = em.op("vector",
                                  lambda e, pbk=pbk, xslot=xslot: e.scalar_tensor_tensor(
                                      out=xb2[xslot][:, 0:512],
                                      in0=pB[pbk][:], scalar=0.5,
                                      in1=xb2[xslot][:, 0:512],
                                      op0=ALU.mult, op1=ALU.add),
                                  waits=(ptok, xld), inc="dve")
                    pB_free[pbk] = t_res
                    st = em.dma("gpsimd",
                                dst_ap[row_base + tc * 128: row_base + tc * 128 + 128,
                                       dq * 512:(dq + 1) * 512],
                                xb2[xslot][:, 0:512], waits=(t_res,))
                    xb2_free[xslot] = st
            if next_evacs is not None:
                xn_evacs = next_evacs
        fs.close()

    def attn(src_ap, dst_ap):
        As = ExitStack()
        asb = lambda name, shape, dt: As.enter_context(nc.sbuf_tensor(name, shape, dt))
        xnT = asb("a_xnT", [128, 16 * 2048], BF16)
        qt_sb = asb("a_qt", [128, 4 * 2048], BF16)
        kt_sb = asb("a_kt", [128, 4 * 2048], BF16)
        v_sb = asb("a_v", [128, 16 * 512], BF16)
        wqsl = [asb(f"a_wq{i}", [128, 2048], BF16) for i in range(2)]
        wksl = [asb(f"a_wk{i}", [128, 2048], BF16) for i in range(2)]
        wvsl = asb("a_wv", [128, 16 * 512], BF16)
        pbuf = [asb(f"a_p{i}", [128, 2048], BF16) for i in range(2)]
        ptb = [asb(f"a_pt{i}", [128, 512], BF16) for i in range(2)]
        og = [asb(f"a_og{i}", [128, 512], F32) for i in range(2)]
        xb = [asb(f"a_xb{i}", [128, 2048], F32) for i in range(2)]
        xnb = [asb(f"a_xnb{i}", [128, 2048], BF16) for i in range(2)]
        rres = asb("a_r", [128, 4], F32)
        a_sq = asb("a_sq", [128, 2048], BF16)
        xrm = [asb(f"a_xrm{i}", [128, 512], F32) for i in range(2)]

        xnT3 = xnT[:].rearrange("p (tc r) -> p tc r", tc=16)
        xb_free = [None, None]
        xn_free = [None, None]
        pbuf_free = [None, None]
        ptb_free = [None, None]
        og_free = [None, None]
        r_free = [None] * 4
        xrm_free = [None, None]
        w_free = [None]
        proj_free = [None]

        for b in range(BC):
            rb = b * S
            xn_evacs_all = []
            for tc in range(16):
                slot = tc % 2
                evs = ln_chunk(src_ap, rb + tc * 128, tc, xnT,
                               xb[slot], xnb[slot], xb_free[slot], xn_free[slot],
                               tc % 2, a_sq)
                xn_free[slot] = evs[-1]
                xb_free[slot] = evs[-1]
                xn_evacs_all.extend(evs)
            qk_free = [None, None, None, None]
            for g in range(4):
                wts = []
                for chh in range(2):
                    c0_ = chh * 8
                    wts.append(em.dma(
                        "sync",
                        wvsl[:].rearrange("p (c n) -> p c n", c=16)[:, c0_:c0_ + 8, :],
                        bass.AP(wv_s.tensor, c0_ * 128 * D + g * 512,
                                [[D, 128], [128 * D, 8], [1, 512]]),
                        waits=(w_free[0], ready["qkv"]) if chh == 0 else ()))
                ev_flip = 0
                proj_evs = []
                lastmm = None
                for mi, (mat, wslr, wsrc, br, dstb) in enumerate(
                        (("q", wqsl, wq_s, bq_r, qt_sb), ("k", wksl, wk_s, bk_r, kt_sb))):
                    for oc in range(4):
                        o = g * 4 + oc
                        slot = oc % 2
                        for chh in range(2):
                            c0_ = chh * 8
                            wt = em.dma(
                                "sync",
                                wslr[slot][:].rearrange("p (c n) -> p c n", c=16)[:, c0_:c0_ + 8, :],
                                bass.AP(wsrc.tensor, c0_ * 128 * D + o * 128,
                                        [[D, 128], [128 * D, 8], [1, 128]]),
                                waits=(qk_free[mi * 2 + slot], ready["qkv"], ready["bias"]) if chh == 0 else ())
                        lastproj = None
                        for w4 in range(4):
                            bk_ = (oc * 4 + w4) % 2
                            for c in range(16):
                                w = ([pA_free[bk_], wt] + xn_evacs_all) if c == 0 else []
                                rhs = xnT3[:, w4 * 4:(w4 + 1) * 4, c * 128:(c + 1) * 128]
                                lastproj = em.op("tensor",
                                      lambda e, wslr=wslr, slot=slot, c=c, bk_=bk_, rhs=rhs: e.matmul(
                                          paf(bk_), wslr[slot][:, c * 128:(c + 1) * 128],
                                          rhs, start=(c == 0), stop=False),
                                      waits=w)
                            mtok = em.op("tensor",
                                         lambda e, br=br, oc=oc, g=g, bk_=bk_, mi=mi: e.matmul(
                                             paf(bk_), br[0:1, (g * 4 + oc) * 128:(g * 4 + oc + 1) * 128],
                                             ones_r[mi * 32: mi * 32 + 1, :], start=False, stop=True),
                                         inc="pe")
                            lastmm = mtok
                            qk_free[mi * 2 + slot] = mtok
                            ev_eng = "vector" if ev_flip % 2 == 0 else "scalar"
                            ev_flip += 1
                            dst = dstb[:, oc * 2048 + w4 * 512: oc * 2048 + (w4 + 1) * 512]
                            scl = CSC if mat == "q" else 1.0
                            if ev_eng == "vector":
                                ev = em.op(ev_eng,
                                           lambda e, bk_=bk_, dst=dst, scl=scl: e.tensor_scalar_mul(
                                               dst, paf(bk_), scl),
                                           waits=(mtok, proj_free[0]), inc="dve")
                            else:
                                ev = em.op(ev_eng,
                                           lambda e, bk_=bk_, dst=dst, scl=scl: e.activation(
                                               dst, paf(bk_), AF.Identity, scale=scl),
                                           waits=(mtok, proj_free[0]), inc="act")
                            pA_free[bk_] = ev
                            proj_evs.append(ev)
                for tc in range(16):
                    bk_ = tc % 2
                    for c in range(16):
                        w = [pA_free[bk_]] if c == 0 else []
                        if tc == 0 and c == 0:
                            w = w + wts
                        stat = xnT[:, tc * 2048 + c * 128: tc * 2048 + (c + 1) * 128]
                        em.op("tensor",
                              lambda e, stat=stat, c=c, bk_=bk_: e.matmul(
                                  paf(bk_), stat, wvsl[:, c * 512:(c + 1) * 512],
                                  start=(c == 0), stop=False),
                              waits=w)
                    mtok = em.op("tensor",
                                 lambda e, g=g, bk_=bk_: e.matmul(
                                     paf(bk_), ones_r[64:65, 0:128], bv_r[0:1, g * 512:(g + 1) * 512],
                                     start=False, stop=True),
                                 inc="pe")
                    lastmm = mtok
                    ev_eng = "vector" if tc % 2 == 0 else "scalar"
                    ev = em.op(ev_eng,
                               lambda e, tc=tc, bk_=bk_: ecopy(
                                   e, v_sb[:, tc * 512:(tc + 1) * 512], paf(bk_)),
                               waits=(mtok, proj_free[0]),
                               inc="dve" if ev_eng == "vector" else "act")
                    pA_free[bk_] = ev
                    proj_evs.append(ev)
                w_free[0] = lastmm

                otok = None
                for Q in range(16):
                    oQev = None
                    for oc in range(4):
                        KW = (Q + 4) // 4
                        span = (Q + 1) * 128
                        for kw in range(KW):
                            kc = min(512, span - kw * 512)
                            w = [pA_free[kw]]
                            if Q == 0 and oc == 0 and kw == 0:
                                w += proj_evs
                            em.op("tensor",
                                  lambda e, oc=oc, Q=Q, kw=kw, kc=kc: e.matmul(
                                      paf(kw, kc),
                                      qt_sb[:, oc * 2048 + Q * 128: oc * 2048 + (Q + 1) * 128],
                                      kt_sb[:, oc * 2048 + kw * 512: oc * 2048 + kw * 512 + kc],
                                      start=True, stop=(kw != KW - 1)),
                                  waits=w)
                        smtok = em.op("tensor",
                                      lambda e, Q=Q: e.matmul(
                                          pSh[:, Q * 128:(Q + 1) * 128],
                                          mtriT[:], idb[:], start=False, stop=True),
                                      inc="pe")
                        par = (Q * 4 + oc) % 2
                        mc = stt[:, 32 + par * 16: 32 + (par + 1) * 16]
                        tnm = em.op("vector",
                                    lambda e, span=span, mc=mc: e.reduce_max(
                                        mc[:, 0:1], pSh[:, 0:span], axis=AX.X, negate=True),
                                    waits=(smtok,), inc="dve")
                        pslot = (Q * 4 + oc) % 2
                        et = em.op("scalar",
                                   lambda e, span=span, pslot=pslot, mc=mc: e.activation(
                                       pbuf[pslot][:, 0:span], pSh[:, 0:span], AF.Exp,
                                       bias=mc[:, 0:1], accum_out=mc[:, 9:10]),
                                   waits=(tnm, pbuf_free[pslot]), inc="act")
                        etoks = [et] * KW
                        for kw in range(KW):
                            pA_free[kw] = et
                        rslot = (Q * 4 + oc) % 4
                        trec = em.op("vector",
                                     lambda e, mc=mc, rslot=rslot: e.reciprocal(
                                         rres[:, rslot:rslot + 1], mc[:, 9:10]),
                                     waits=(et,), fwaits=(r_free[rslot],), inc="dve")
                        obk = (Q * 4 + oc) % 2
                        NKT = Q + 1
                        for ktg in range(ceil_div(NKT, 4)):
                            nk = min(4, NKT - ktg * 4)
                            pbk = ktg % 2
                            ptok = None
                            for j in range(nk):
                                kt = ktg * 4 + j
                                w = [etoks[ktg], pt_free[pbk]] if j == 0 else []
                                ptok = em.op("tensor",
                                             lambda e, pbk=pbk, j=j, kt=kt, pslot=pslot: e.transpose(
                                                 pT[pbk][:, j * 128:(j + 1) * 128],
                                                 pbuf[pslot][:, kt * 128:(kt + 1) * 128], idb[:]),
                                             waits=w, inc="pe" if j == nk - 1 else None)
                            ptev = em.op("vector",
                                         lambda e, pbk=pbk, nk=nk: e.tensor_copy(
                                             out=ptb[pbk][:, 0:nk * 128], in_=pT[pbk][:, 0:nk * 128]),
                                         waits=(ptok, ptb_free[pbk]), inc="dve")
                            pt_free[pbk] = ptev
                            for j in range(nk):
                                kt = ktg * 4 + j
                                w = [ptev]
                                if ktg == 0 and j == 0:
                                    w.append(pB_free[obk])
                                otok = em.op("tensor",
                                             lambda e, pbk=pbk, j=j, kt=kt, oc=oc, obk=obk: e.matmul(
                                                 pB[obk][:, 0:128],
                                                 ptb[pbk][:, j * 128:(j + 1) * 128],
                                                 v_sb[:, kt * 512 + oc * 128: kt * 512 + (oc + 1) * 128],
                                                 start=(kt == 0), stop=(kt == NKT - 1)),
                                             waits=w, inc="pe" if kt == NKT - 1 else None)
                            ptb_free[pbk] = otok
                        pbuf_free[pslot] = otok
                        oslot = Q % 2
                        oev = em.op("vector",
                                    lambda e, oc=oc, obk=obk, rslot=rslot, oslot=oslot: e.tensor_scalar(
                                        out=og[oslot][:, oc * 128:(oc + 1) * 128],
                                        in0=pB[obk][:, 0:128], scalar1=rres[:, rslot:rslot + 1],
                                        scalar2=None, op0=ALU.mult),
                                    waits=(otok, og_free[oslot] if oc == 0 else None),
                                    fwaits=(trec,), inc="dve")
                        pB_free[obk] = oev
                        r_free[rslot] = oev
                        oQev = oev
                    xs = Q % 2
                    ld = em.dma("scalar", xrm[xs][:],
                                src_ap[rb + Q * 128: rb + (Q + 1) * 128, g * 512:(g + 1) * 512],
                                waits=(xrm_free[xs],))
                    oslot = Q % 2
                    tadd = em.op("vector",
                                 lambda e, xs=xs, oslot=oslot: e.tensor_tensor(
                                     xrm[xs][:], xrm[xs][:], og[oslot][:], op=ALU.add),
                                 waits=(ld, oQev), inc="dve")
                    st = em.dma("gpsimd",
                                dst_ap[rb + Q * 128: rb + (Q + 1) * 128, g * 512:(g + 1) * 512],
                                xrm[xs][:], waits=(tadd,))
                    xrm_free[xs] = st
                    og_free[oslot] = tadd
                proj_free[0] = otok
        As.close()

    setup_consts()
    preconvert()
    if "dumpw" in parts:
        em.join()
        dump_out = nc.dram_tensor("dump_out", [16 * 128, NC16 * 128], BF16,
                                  kind="ExternalOutput").ap()
        rb = sb("dump_rb", [128, NC16 * 128], BF16)
        for blk in range(16):
            ldd = em.dma("sync", rb[:], bass.AP(wd_s.tensor, blk * (128 * NC16 * 128),
                                                [[NC16 * 128, 128], [1, NC16 * 128]]))
            std = em.dma("gpsimd", dump_out[blk * 128:(blk + 1) * 128, :], rb[:], waits=(ldd,))
            em.join()
    if "ffn1" in parts:
        ffn(x_in, xr if ("attn" in parts or "ffn2" in parts) else y_out, "f1")
        em.join()
    if "attn" in parts:
        src = xr if "ffn1" in parts else x_in
        dst = xr if "ffn2" in parts else y_out
        attn(src, dst)
        em.join()
    if "ffn2" in parts:
        src = xr if ("ffn1" in parts or "attn" in parts) else x_in
        ffn(src, y_out, "f2")
        em.join()

    for eng in ENGS:
        for q in ("qs", "qg", "qa"):
            for i in range(Em.NLANE):
                em._emit_wait(eng, f"{q}{i}", em.counts[f"{q}{i}"])

    with stack:
        with nc.Block() as block:
            em.finalize(block)
    return nc


_PROG_CACHE = {}


def _get_prog(parts):
    key = tuple(parts)
    if key not in _PROG_CACHE:
        _PROG_CACHE[key] = build_program(parts)
    return _PROG_CACHE[key]


def kernel(L0_attn_x, L0_attn_Wq, L0_attn_bq, L0_attn_Wk, L0_attn_bk,
           L0_attn_Wv, L0_attn_bv, L0_ffn_w_gate, L0_ffn_w_up, L0_ffn_w_down,
           parts=("ffn1", "attn", "ffn2"), core_ids=None):
    x = np.ascontiguousarray(np.asarray(L0_attn_x, dtype=np.float32))
    weights = {
        "wq": np.ascontiguousarray(np.asarray(L0_attn_Wq, np.float32)),
        "wk": np.ascontiguousarray(np.asarray(L0_attn_Wk, np.float32)),
        "wv": np.ascontiguousarray(np.asarray(L0_attn_Wv, np.float32)),
        "bq": np.ascontiguousarray(np.asarray(L0_attn_bq, np.float32)),
        "bk": np.ascontiguousarray(np.asarray(L0_attn_bk, np.float32)),
        "bv": np.ascontiguousarray(np.asarray(L0_attn_bv, np.float32)),
        "wg": np.ascontiguousarray(np.asarray(L0_ffn_w_gate, np.float32)),
        "wu": np.ascontiguousarray(np.asarray(L0_ffn_w_up, np.float32)),
        "wd": np.ascontiguousarray(np.asarray(L0_ffn_w_down, np.float32)),
    }
    nc = _get_prog(parts)
    ncore = NCORES
    in_maps = []
    for core in range(ncore):
        m = dict(weights)
        m["x_in"] = x[core * BC:(core + 1) * BC].reshape(NT, D)
        in_maps.append(m)
    res = run_bass_kernel_spmd(nc, in_maps, core_ids=list(range(ncore)))
    out = np.empty((B, S, D), np.float32)
    for core in range(ncore):
        out[core * BC:(core + 1) * BC] = res.results[core]["y_out"].reshape(BC, S, D)
    return out


# version 10
# speedup vs baseline: 1.1252x; 1.0019x over previous
import numpy as np
import ml_dtypes
from contextlib import ExitStack

import concourse.bass as bass
import concourse.mybir as mybir
from concourse.bass_utils import run_bass_kernel_spmd

F32 = mybir.dt.float32
BF16 = mybir.dt.bfloat16
AF = mybir.ActivationFunctionType
ALU = mybir.AluOpType
AX = mybir.AxisListType

B, S, D, I, H, DH = 16, 2048, 2048, 8192, 16, 128
NCORES = 8
BC = B // NCORES
NT = BC * S
TG = 512
NTG = NT // TG
NCH = TG // 128
NIT = I // 512
NC16 = I // 128
CSC = float(1.0 / np.sqrt(DH))
MASKVAL = -1e9

ENGS = ("tensor", "vector", "scalar", "sync", "gpsimd")
DMA_SEM = {"sync": "qs", "gpsimd": "qg", "scalar": "qa"}


class Em:

    NLANE = 8

    def __init__(self, nc, stack):
        self.nc = nc
        self.streams = {e: [] for e in ENGS}
        self.sems = {}
        self.counts = {}
        names = ["pe", "dve", "act", "pool", "jn"]
        for q in ("qs", "qg", "qa"):
            names += [f"{q}{i}" for i in range(self.NLANE)]
        for name in names:
            self.sems[name] = stack.enter_context(nc.semaphore(name))
            self.counts[name] = 0
        self.qidx = {"qs": 0, "qg": 0, "qa": 0}
        self.waited = {e: {} for e in ENGS}
        self.njoin = 0

    def _emit_wait(self, eng, sem, val):
        if val is None or val <= 0:
            return
        if self.waited[eng].get(sem, 0) >= val:
            return
        self.waited[eng][sem] = val
        h = self.sems[sem]
        self.streams[eng].append(lambda e, h=h, v=val: e.wait_ge(h, v))

    def _coalesce(self, waits, skip_eng=None):
        best = {}
        for t in waits:
            if t is None:
                continue
            sem, val, prod = t
            if prod == skip_eng:
                continue
            if best.get(sem, 0) < val:
                best[sem] = val
        return best

    def op(self, eng, fn, waits=(), inc=None, fwaits=()):
        for sem, val in self._coalesce(waits, skip_eng=eng).items():
            self._emit_wait(eng, sem, val)
        for sem, val in self._coalesce(fwaits).items():
            self._emit_wait(eng, sem, val)
        tok = None
        if inc:
            self.counts[inc] += 1
            v = self.counts[inc]
            h = self.sems[inc]
            self.streams[eng].append(lambda e, fn=fn, h=h, v=v: fn(e).then_inc(h, 1))
            tok = (inc, v, eng)
        else:
            self.streams[eng].append(lambda e, fn=fn: fn(e))
        return tok

    def dma(self, eng, out, in_, waits=(), **kw):
        for sem, val in self._coalesce(waits).items():
            self._emit_wait(eng, sem, val)
        q = DMA_SEM[eng]
        lane = f"{q}{self.qidx[q] % self.NLANE}"
        self.qidx[q] += 1
        self.counts[lane] += 16
        v = self.counts[lane]
        h = self.sems[lane]
        self.streams[eng].append(
            lambda e, o=out, i=in_, h=h, v=v, kw=kw: e.dma_start(out=o, in_=i, **kw).then_inc(h, 16))
        return (lane, v, "dma")

    def join(self):
        for eng in ("sync", "gpsimd", "scalar"):
            q = DMA_SEM[eng]
            for i in range(self.NLANE):
                self._emit_wait(eng, f"{q}{i}", self.counts[f"{q}{i}"])
        self.njoin += 1
        target = 5 * self.njoin
        h = self.sems["jn"]
        for eng in ENGS:
            self.streams[eng].append(lambda e, h=h: e.sem_inc(h, 1))
            self.streams[eng].append(lambda e, h=h, t=target: e.wait_ge(h, t))
            self.waited[eng] = {}

    def finalize(self, block):
        for eng in ENGS:
            stream = self.streams[eng]

            def runner(e, stream=stream):
                for fn in stream:
                    fn(e)
            getattr(block, eng)(runner)


def ceil_div(a, b):
    return (a + b - 1) // b


def ecopy(e, dst, src):
    if hasattr(e, "tensor_copy"):
        return e.tensor_copy(out=dst, in_=src)
    return e.copy(dst, src)


def build_program(parts=("ffn1", "attn", "ffn2"), dbg=(), paranoid=0):
    nc = bass.Bass("TRN2", target_bir_lowering=False)
    stack = ExitStack()

    x_in = nc.dram_tensor("x_in", [NT, D], F32, kind="ExternalInput").ap()
    y_out = nc.dram_tensor("y_out", [NT, D], F32, kind="ExternalOutput").ap()
    wq = nc.dram_tensor("wq", [D, D], F32, kind="ExternalInput").ap()
    wk = nc.dram_tensor("wk", [D, D], F32, kind="ExternalInput").ap()
    wv = nc.dram_tensor("wv", [D, D], F32, kind="ExternalInput").ap()
    bq = nc.dram_tensor("bq", [D], F32, kind="ExternalInput").ap()
    bk = nc.dram_tensor("bk", [D], F32, kind="ExternalInput").ap()
    bv = nc.dram_tensor("bv", [D], F32, kind="ExternalInput").ap()
    wg = nc.dram_tensor("wg", [D, I], F32, kind="ExternalInput").ap()
    wu = nc.dram_tensor("wu", [D, I], F32, kind="ExternalInput").ap()
    wd = nc.dram_tensor("wd", [I, D], F32, kind="ExternalInput").ap()

    xr = nc.dram_tensor("xr", [NT, D], F32, kind="Internal").ap()
    wg_s = nc.dram_tensor("wg_s", [NIT, 128, 16, 512], BF16, kind="Internal").ap()
    wu_s = nc.dram_tensor("wu_s", [NIT, 128, 16, 512], BF16, kind="Internal").ap()
    wd_s = nc.dram_tensor("wd_s", [NC16, 128, 16, 128], BF16,
                          kind="ExternalOutput" if "wdout" in parts else "Internal").ap()
    wq_s = nc.dram_tensor("wq_s", [D, D], BF16, kind="Internal").ap()
    wk_s = nc.dram_tensor("wk_s", [D, D], BF16, kind="Internal").ap()
    wv_s = nc.dram_tensor("wv_s", [D, D], BF16, kind="Internal").ap()

    em = Em(nc, stack)
    sb = lambda name, shape, dt: stack.enter_context(nc.sbuf_tensor(name, shape, dt))
    ps = lambda name, shape, dt: stack.enter_context(nc.psum_tensor(name, shape, dt))

    dbg_t = {}
    def dbg_dump(name, shape, sbuf_ap, waits):
        if name not in dbg:
            return
        if name not in dbg_t:
            dbg_t[name] = nc.dram_tensor("dbg_" + name, shape, F32, kind="ExternalOutput").ap()
        em.dma("gpsimd", dbg_t[name], sbuf_ap, waits=waits)

    ready = {"consts": None, "bias": None, "wgu": [None] * NIT, "qkv": None, "wd": None}

    pSh = ps("pSh", [128, 2048], F32)
    paf = lambda i, kc=512: pSh[:, i * 512: i * 512 + kc]
    pB = [ps(f"pB{i}", [128, 512], F32) for i in range(2)]
    pT = [ps(f"pT{i}", [128, 1024], BF16) for i in range(2)]

    idb = sb("idb", [128, 128], BF16)
    idf = sb("idf", [128, 128], F32)
    mtri = sb("mtri", [128, 128], F32)
    mtriT = sb("mtriT", [128, 128], BF16)
    eps_t = sb("eps_t", [128, 1], F32)
    ones_r = sb("ones_r", [65, 512], BF16)
    bias3 = sb("bias3", [65, D], BF16)
    bq_r = bias3[0:1, :]
    bk_r = bias3[32:33, :]
    bv_r = bias3[64:65, :]
    stt = sb("stt", [128, 64], F32)

    def setup_consts():
        em.op("gpsimd", lambda e: e.memset(idb[:], 0.0))
        em.op("gpsimd", lambda e: e.affine_select(
            out=idb[:], in_=idb[:], compare_op=ALU.not_equal, fill=1.0,
            base=0, pattern=[[-1, 128]], channel_multiplier=1))
        em.op("gpsimd", lambda e: e.memset(idf[:], 0.0))
        em.op("gpsimd", lambda e: e.affine_select(
            out=idf[:], in_=idf[:], compare_op=ALU.not_equal, fill=1.0,
            base=0, pattern=[[-1, 128]], channel_multiplier=1))
        em.op("gpsimd", lambda e: e.memset(mtri[:], 0.0))
        em.op("gpsimd", lambda e: e.affine_select(
            out=mtri[:], in_=mtri[:], compare_op=ALU.is_ge, fill=MASKVAL,
            base=0, pattern=[[-1, 128]], channel_multiplier=1))
        em.op("gpsimd", lambda e: e.memset(mtriT[:], 0.0))
        em.op("gpsimd", lambda e: e.affine_select(
            out=mtriT[:], in_=mtriT[:], compare_op=ALU.is_ge, fill=MASKVAL,
            base=0, pattern=[[1, 128]], channel_multiplier=-1))
        em.op("gpsimd", lambda e: e.memset(eps_t[:], 1e-5))
        t = em.op("gpsimd", lambda e: e.memset(ones_r[:], 1.0), inc="pool")
        ready["consts"] = t
        em.dma("gpsimd", bq_r, bq[None, :])
        em.dma("gpsimd", bk_r, bk[None, :])
        ready["bias"] = em.dma("gpsimd", bv_r, bv[None, :])

    def preconvert():
        t = None
        for it in range(NIT):
            for src, dst in ((wg, wg_s), (wu, wu_s)):
                for ch in range(2):
                    c0 = ch * 8
                    sap = bass.AP(src.tensor, c0 * 128 * I + it * 512,
                                  [[128 * I, 8], [I, 128], [1, 512]])
                    dap = dst[it][:, c0:c0 + 8, :].rearrange("p c n -> c p n")
                    t = em.dma("gpsimd", dap, sap)
            ready["wgu"][it] = t
        pre_s = ExitStack()
        wdbuf = [pre_s.enter_context(nc.sbuf_tensor(f"wdc{i}", [128, 2048], BF16))
                 for i in range(2)]
        free_tok = [None, None]
        for cp in range(NC16):
            slot = cp % 2
            base = (cp // 4) * 512 + (cp % 4)
            sap = bass.AP(wd.tensor, base * D, [[4 * D, 128], [1, D]])
            ld = em.dma("gpsimd", wdbuf[slot][:], sap, waits=(free_tok[slot],))
            dap = bass.AP(wd_s.tensor, cp * (128 * 16 * 128), [[2048, 128], [1, 2048]])
            st_tok = em.dma("sync", dap, wdbuf[slot][:], waits=(ld,))
            free_tok[slot] = st_tok
        ready["wd"] = st_tok
        pre_s.close()
        for src_, dst in ((wq, wq_s), (wk, wk_s), (wv, wv_s)):
            for ch in range(4):
                r0 = ch * 512
                sap = bass.AP(src_.tensor, r0 * D, [[D, 512], [1, D]])
                dap = bass.AP(dst.tensor, r0 * D, [[D, 512], [1, D]])
                t = em.dma("gpsimd", dap, sap)
        ready["qkv"] = t

    pt_free = [None, None]
    pA_free = [None, None, None, None]
    pB_free = [None, None]

    def ln_chunk(src_ap, row0, tc, xnT_buf, xbuf, xnbuf, xb_free, xn_free, par, sq_scr):
        sc = stt[:, par * 16: par * 16 + 16]
        ld = em.dma("sync", xbuf[:], src_ap[row0:row0 + 128, :], waits=(xb_free,))
        t0 = em.op("vector", lambda e: e.reduce_sum(sc[:, 0:1], xbuf[:], axis=AX.X),
                   waits=(ld,), inc="dve")
        t_sq = em.op("vector", lambda e: e.scalar_tensor_tensor(
                         out=sq_scr[:], in0=xbuf[:], scalar=1.0, in1=xbuf[:],
                         op0=ALU.mult, op1=ALU.mult, accum_out=sc[:, 1:2]),
                     waits=(ld,), inc="dve")
        t1 = em.op("vector", lambda e: e.tensor_scalar_mul(sc[:, 2:3], sc[:, 0:1], 1.0 / D),
                   fwaits=(t0,), inc="dve")
        t2 = em.op("vector", lambda e: e.tensor_tensor(sc[:, 3:4], sc[:, 2:3], sc[:, 2:3], op=ALU.mult),
                   fwaits=(t1,), inc="dve")
        t3 = em.op("vector", lambda e: e.tensor_scalar_mul(sc[:, 4:5], sc[:, 1:2], 1.0 / D),
                   fwaits=(t_sq,), inc="dve")
        t_var = em.op("vector", lambda e: e.tensor_tensor(sc[:, 5:6], sc[:, 4:5], sc[:, 3:4], op=ALU.subtract),
                      fwaits=(t2, t3), inc="dve")
        t_sd = em.op("scalar", lambda e: e.activation(sc[:, 6:7], sc[:, 5:6], AF.Sqrt, bias=eps_t[:]),
                     waits=(t_var, ready["consts"]), inc="act")
        t5 = em.op("vector", lambda e: e.reciprocal(sc[:, 7:8], sc[:, 6:7]),
                   waits=(t_sd,), inc="dve")
        t6 = em.op("vector", lambda e: e.tensor_tensor(sc[:, 8:9], sc[:, 2:3], sc[:, 7:8], op=ALU.mult),
                   fwaits=(t5,), inc="dve")
        t_bias = em.op("vector", lambda e: e.tensor_scalar_mul(sc[:, 9:10], sc[:, 8:9], -1.0),
                       fwaits=(t6,), inc="dve")
        t_xn = em.op("scalar", lambda e: e.activation(xnbuf[:], xbuf[:], AF.Identity,
                                                      scale=sc[:, 7:8], bias=sc[:, 9:10]),
                     waits=(t_bias, xn_free), inc="act")
        evac_toks = []
        for bnk in range(4):
            pbk = bnk % 2
            pb = pT[pbk]
            ptok = None
            for j in range(4):
                c = bnk * 4 + j
                w = [t_xn, pt_free[pbk], ready["consts"]] if j == 0 else []
                ptok = em.op("tensor",
                             lambda e, c=c, pb=pb, j=j: e.transpose(
                                 pb[:, j * 128:(j + 1) * 128],
                                 xnbuf[:, c * 128:(c + 1) * 128], idb[:]),
                             waits=w, inc="pe" if j == 3 else None)
            dst = xnT_buf[:, tc * 2048 + bnk * 512: tc * 2048 + (bnk + 1) * 512]
            ev_eng = "vector" if bnk % 2 == 0 else "scalar"
            ev = em.op(ev_eng, lambda e, pb=pb, dst=dst: ecopy(e, dst, pb[:, 0:512]),
                       waits=(ptok,), inc="dve" if ev_eng == "vector" else "act")
            pt_free[pbk] = ev
            evac_toks.append(ev)
        return evac_toks

    def ffn(src_ap, dst_ap, tag):
        fs = ExitStack()
        fsb = lambda name, shape, dt: fs.enter_context(nc.sbuf_tensor(tag + name, shape, dt))
        xnT = fsb("xnT", [128, NCH * 2048], BF16)
        hT = fsb("hT", [128, NCH * 8192], BF16)
        wsl = [fsb(f"w{i}", [128, 4096], BF16) for i in range(6)]
        xb = [fsb(f"xb{i}", [128, 2048], F32) for i in range(2)]
        xb2 = [fsb(f"xb2{i}", [128, 512], F32) for i in range(2)]
        xnb = [fsb(f"xnb{i}", [128, 2048], BF16) for i in range(2)]
        sg = [fsb(f"sg{i}", [128, 512], BF16) for i in range(2)]
        hst = [fsb(f"hst{i}", [128, 2048], BF16) for i in range(4)]
        ab = fsb("ab", [128, 2048], BF16)
        mtmp = fsb("mtmp", [128, 2048], BF16)
        hsp = [fsb(f"hsp{i}", [128, 2048], BF16) for i in range(2)]
        ytsb = fsb("ytsb", [128, 4 * 512], F32)

        xb_free = [None, None]
        xb2_free = [None, None]
        xn_free = [None, None]
        sg_free = [None, None]
        hst_free = [None, None, None, None]
        hsp_free = [None, None]
        wfree = [None] * 6
        hT3 = hT[:].rearrange("p (tc r) -> p tc r", tc=NCH)

        def emit_ln_group(tg):
            evs_all = []
            for tc in range(NCH):
                slot = tc % 2
                evs = ln_chunk(src_ap, tg * TG + tc * 128, tc, xnT,
                               xb[slot], xnb[slot], xb_free[slot], xn_free[slot],
                               tc % 2, mtmp)
                xn_free[slot] = evs[-1]
                xb_free[slot] = evs[-1]
                evs_all.extend(evs)
            return evs_all

        xn_evacs = emit_ln_group(0)
        if "f1xnT" in [tag + "xnT"] and tag + "xnT" in dbg:
            dbg_dump(tag + "xnT", [128, NCH * 2048], xnT[:], tuple(xn_evacs))
            dbg_dump(tag + "stt", [128, 64], stt[:], tuple(xn_evacs))

        for tg in range(NTG):
            row_base = tg * TG
            hT_ready = []
            gtok = utok = None
            pending_tr = [None]

            def flush_tr():
                if pending_tr[0] is not None:
                    f = pending_tr[0]
                    pending_tr[0] = None
                    f()

            for it in range(NIT):
                gs = [(2 * it + h) % 3 for h in range(2)]
                us = [3 + (2 * it + h) % 3 for h in range(2)]
                wts = {}
                for h in range(2):
                    wts[("g", h)] = em.dma(
                        "sync", wsl[gs[h]][:],
                        bass.AP(wg_s.tensor, it * (128 * 16 * 512) + h * 4096,
                                [[8192, 128], [1, 4096]]),
                        waits=(wfree[gs[h]], ready["wgu"][it]))
                    wts[("u", h)] = em.dma(
                        "sync", wsl[us[h]][:],
                        bass.AP(wu_s.tensor, it * (128 * 16 * 512) + h * 4096,
                                [[8192, 128], [1, 4096]]),
                        waits=(wfree[us[h]], ready["wgu"][it]))
                IT, s4 = it // 4, it % 4
                gq = [None] * 2
                uq = [None] * 2
                for tc in range(NCH):
                    gb = (it * NCH + tc) % 2
                    ub = 2 + gb
                    for k in range(16):
                        h = k // 8
                        kk = k % 8
                        stat = xnT[:, tc * 2048 + k * 128: tc * 2048 + (k + 1) * 128]
                        w = []
                        if k == 0:
                            w = [pA_free[gb], pA_free[ub]]
                            w.extend(xn_evacs[tc * 4: tc * 4 + 4])
                        if kk == 0:
                            w = w + [wts[("g", h)], wts[("u", h)]]
                        ginc = "pe" if kk == 7 else None
                        gtok = em.op("tensor",
                                     lambda e, stat=stat, gb=gb, sl=gs[h], kk=kk, k=k: e.matmul(
                                         paf(gb), stat, wsl[sl][:, kk * 512:(kk + 1) * 512],
                                         start=(k == 0), stop=(k == 15)),
                                     waits=w, inc=ginc)
                        utok = em.op("tensor",
                                     lambda e, stat=stat, ub=ub, sl=us[h], kk=kk, k=k: e.matmul(
                                         paf(ub), stat, wsl[sl][:, kk * 512:(kk + 1) * 512],
                                         start=(k == 0), stop=(k == 15)),
                                     inc=ginc)
                        if kk == 7:
                            gq[h] = gtok
                            uq[h] = utok
                    if tc == NCH - 1:
                        for h in range(2):
                            wfree[gs[h]] = gq[h]
                            wfree[us[h]] = uq[h]
                    flush_tr()
                    sslot = (it * NCH + tc) % 2
                    t_silu = em.op("scalar",
                                   lambda e, gb=gb, sslot=sslot: e.activation(
                                       sg[sslot][:], paf(gb), AF.Silu),
                                   waits=(gtok, sg_free[sslot]), inc="act")
                    pA_free[gb] = t_silu
                    hdst = hst[tc][:].rearrange("p (q r) -> p q r", q=4)[:, :, s4 * 128:(s4 + 1) * 128]
                    t_h = em.op("vector",
                                lambda e, sslot=sslot, ub=ub, hdst=hdst: e.tensor_tensor(
                                    hdst,
                                    sg[sslot][:].rearrange("p (j q) -> p q j", q=4),
                                    paf(ub).rearrange("p (j q) -> p q j", q=4),
                                    op=ALU.mult),
                                waits=(t_silu, utok,
                                       hst_free[tc] if s4 == 0 else None),
                                inc="dve")
                    pA_free[ub] = t_h
                    sg_free[sslot] = t_h
                    if s4 < 3:
                        continue
                    hs = hst[tc]
                    t_abs = em.op("scalar",
                                  lambda e, hs=hs: e.activation(ab[:], hs[:], AF.Abs),
                                  waits=(t_h,), inc="act")
                    q0, q1 = ab[:, 0:512], ab[:, 512:1024]
                    q2, q3 = ab[:, 1024:1536], ab[:, 1536:2048]
                    m1, n1 = mtmp[:, 0:512], mtmp[:, 512:1024]
                    m2, n2 = mtmp[:, 1024:1536], mtmp[:, 1536:2048]
                    em.op("vector", lambda e: e.tensor_tensor(m1, q0, q1, op=ALU.max), waits=(t_abs,))
                    em.op("vector", lambda e: e.tensor_tensor(n1, q0, q1, op=ALU.min))
                    em.op("vector", lambda e: e.tensor_tensor(m2, q2, q3, op=ALU.max))
                    em.op("vector", lambda e: e.tensor_tensor(n2, q2, q3, op=ALU.min))
                    em.op("vector", lambda e: e.tensor_tensor(m2, m1, m2, op=ALU.min))
                    em.op("vector", lambda e: e.tensor_tensor(n2, n1, n2, op=ALU.max))
                    em.op("vector", lambda e: e.tensor_tensor(m1, m2, n2, op=ALU.max))
                    hslot = (it * NCH + tc) % 2
                    happly = None
                    for q in range(4):
                        aq = ab[:, q * 512:(q + 1) * 512]
                        em.op("vector", lambda e, aq=aq: e.tensor_tensor(aq, aq, m1, op=ALU.is_ge))
                        happly = em.op("vector",
                                       lambda e, q=q, aq=aq, hslot=hslot, hs=hs: e.tensor_tensor(
                                           hsp[hslot][:, q * 512:(q + 1) * 512],
                                           hs[:, q * 512:(q + 1) * 512], aq, op=ALU.mult),
                                       waits=(hsp_free[hslot] if q == 0 else None,),
                                       inc="dve" if q == 3 else None)
                    hst_free[tc] = happly

                    def do_tr(IT=IT, tc=tc, hslot=hslot, happly=happly):
                        for s in range(4):
                            pbk = (IT * 4 + s) % 2
                            pb = pT[pbk]
                            ptok = None
                            for q in range(4):
                                w = [happly, pt_free[pbk]] if q == 0 else []
                                ptok = em.op("tensor",
                                             lambda e, pb=pb, q=q, s=s, hslot=hslot: e.transpose(
                                                 pb[:, q * 128:(q + 1) * 128],
                                                 hsp[hslot][:, q * 512 + s * 128: q * 512 + s * 128 + 128],
                                                 idb[:]),
                                             waits=w, inc="pe" if q == 3 else None)
                            if s == 3:
                                hsp_free[hslot] = ptok
                            cbase = IT * 16 + s * 4
                            dst = hT[:, tc * 8192 + cbase * 128: tc * 8192 + (cbase + 4) * 128]
                            ev_eng = "vector" if s % 2 == 0 else "scalar"
                            ev = em.op(ev_eng, lambda e, pb=pb, dst=dst: ecopy(e, dst, pb[:, 0:512]),
                                       waits=(ptok,), inc="dve" if ev_eng == "vector" else "act")
                            pt_free[pbk] = ev
                            hT_ready.append(ev)
                    pending_tr[0] = do_tr
            flush_tr()

            if tg + 1 < NTG:
                next_evacs = emit_ln_group(tg + 1)
            else:
                next_evacs = None

            for dq in range(4):
                yevs = []
                for d4 in range(4):
                    gd = dq * 4 + d4
                    dsl = [(2 * gd + j) % 6 for j in range(2)]
                    wdts = {}
                    for j in range(2):
                        for q4 in range(4):
                            cp0 = j * 32 + q4 * 8
                            wdts[j] = em.dma(
                                "sync",
                                wsl[dsl[j]][:].rearrange("p (cp n) -> p cp n", cp=32)[:, q4 * 8:(q4 + 1) * 8, :],
                                bass.AP(wd_s.tensor, cp0 * (128 * 16 * 128) + gd * 128,
                                        [[2048, 128], [128 * 16 * 128, 8], [1, 128]]),
                                waits=(wfree[dsl[j]], ready["wd"]) if q4 == 0 else ())
                    ybk = gd % 2
                    ytok = None
                    ytoks = {}
                    for i in range(NC16):
                        j = i // 32
                        ii = i % 32
                        w = []
                        if i == 0:
                            w = [pA_free[ybk]]
                            if gd == 0:
                                w += hT_ready
                        if ii == 0:
                            w = w + [wdts[j]]
                        yinc = "pe" if ii == 31 else None
                        rhs = hT3[:, :, i * 128:(i + 1) * 128]
                        ytok = em.op("tensor",
                                     lambda e, i=i, ybk=ybk, sl=dsl[j], ii=ii, rhs=rhs: e.matmul(
                                         paf(ybk), wsl[sl][:, ii * 128:(ii + 1) * 128], rhs,
                                         start=(i == 0), stop=(i == NC16 - 1)),
                                     waits=w, inc=yinc)
                        if ii == 31:
                            ytoks[j] = ytok
                    for j in range(2):
                        wfree[dsl[j]] = ytoks[j]
                    ev = em.op("vector",
                               lambda e, d4=d4, ybk=ybk: e.tensor_copy(
                                   out=ytsb[:, d4 * 512:(d4 + 1) * 512], in_=paf(ybk)),
                               waits=(ytok,), inc="dve")
                    pA_free[ybk] = ev
                    yevs.append(ev)
                for tc in range(NCH):
                    xslot = tc % 2
                    rows = src_ap[row_base + tc * 128: row_base + tc * 128 + 128,
                                  dq * 512:(dq + 1) * 512]
                    xld = em.dma("scalar", xb2[xslot][:, 0:512], rows, waits=(xb2_free[xslot],))
                    pbk = tc % 2
                    ptok = None
                    for j in range(4):
                        d4 = j
                        w = [yevs[-1], pB_free[pbk], ready["consts"]] if j == 0 else []
                        ptok = em.op("tensor",
                                     lambda e, pbk=pbk, j=j, d4=d4, tc=tc: e.transpose(
                                         pB[pbk][:, j * 128:(j + 1) * 128],
                                         ytsb[:, d4 * 512 + tc * 128: d4 * 512 + tc * 128 + 128],
                                         idf[:]),
                                     waits=w, inc="pe" if j == 3 else None)
                    t_res = em.op("vector",
                                  lambda e, pbk=pbk, xslot=xslot: e.scalar_tensor_tensor(
                                      out=xb2[xslot][:, 0:512],
                                      in0=pB[pbk][:], scalar=0.5,
                                      in1=xb2[xslot][:, 0:512],
                                      op0=ALU.mult, op1=ALU.add),
                                  waits=(ptok, xld), inc="dve")
                    pB_free[pbk] = t_res
                    st = em.dma("gpsimd",
                                dst_ap[row_base + tc * 128: row_base + tc * 128 + 128,
                                       dq * 512:(dq + 1) * 512],
                                xb2[xslot][:, 0:512], waits=(t_res,))
                    xb2_free[xslot] = st
            if next_evacs is not None:
                xn_evacs = next_evacs
        fs.close()

    def attn(src_ap, dst_ap):
        As = ExitStack()
        asb = lambda name, shape, dt: As.enter_context(nc.sbuf_tensor(name, shape, dt))
        xnT = asb("a_xnT", [128, 16 * 2048], BF16)
        qt_sb = asb("a_qt", [128, 4 * 2048], BF16)
        kt_sb = asb("a_kt", [128, 4 * 2048], BF16)
        v_sb = asb("a_v", [128, 16 * 512], BF16)
        wqsl = [asb(f"a_wq{i}", [128, 2048], BF16) for i in range(2)]
        wksl = [asb(f"a_wk{i}", [128, 2048], BF16) for i in range(2)]
        wvsl = asb("a_wv", [128, 16 * 512], BF16)
        pbuf = [asb(f"a_p{i}", [128, 2048], BF16) for i in range(2)]
        ptb = [asb(f"a_pt{i}", [128, 512], BF16) for i in range(2)]
        og = [asb(f"a_og{i}", [128, 512], F32) for i in range(2)]
        xb = [asb(f"a_xb{i}", [128, 2048], F32) for i in range(2)]
        xnb = [asb(f"a_xnb{i}", [128, 2048], BF16) for i in range(2)]
        rres = asb("a_r", [128, 4], F32)
        a_sq = asb("a_sq", [128, 2048], BF16)
        xrm = [asb(f"a_xrm{i}", [128, 512], F32) for i in range(2)]

        xnT3 = xnT[:].rearrange("p (tc r) -> p tc r", tc=16)
        xb_free = [None, None]
        xn_free = [None, None]
        pbuf_free = [None, None]
        ptb_free = [None, None]
        og_free = [None, None]
        r_free = [None] * 4
        xrm_free = [None, None]
        w_free = [None]
        proj_free = [None]

        for b in range(BC):
            rb = b * S
            xn_evacs_all = []
            for tc in range(16):
                slot = tc % 2
                evs = ln_chunk(src_ap, rb + tc * 128, tc, xnT,
                               xb[slot], xnb[slot], xb_free[slot], xn_free[slot],
                               tc % 2, a_sq)
                xn_free[slot] = evs[-1]
                xb_free[slot] = evs[-1]
                xn_evacs_all.extend(evs)
            qk_free = [None, None, None, None]
            for g in range(4):
                wts = []
                for chh in range(2):
                    c0_ = chh * 8
                    wts.append(em.dma(
                        "sync",
                        wvsl[:].rearrange("p (c n) -> p c n", c=16)[:, c0_:c0_ + 8, :],
                        bass.AP(wv_s.tensor, c0_ * 128 * D + g * 512,
                                [[D, 128], [128 * D, 8], [1, 512]]),
                        waits=(w_free[0], ready["qkv"]) if chh == 0 else ()))
                ev_flip = 0
                proj_evs = []
                lastmm = None
                for mi, (mat, wslr, wsrc, br, dstb) in enumerate(
                        (("q", wqsl, wq_s, bq_r, qt_sb), ("k", wksl, wk_s, bk_r, kt_sb))):
                    for oc in range(4):
                        o = g * 4 + oc
                        slot = oc % 2
                        for chh in range(2):
                            c0_ = chh * 8
                            wt = em.dma(
                                "sync",
                                wslr[slot][:].rearrange("p (c n) -> p c n", c=16)[:, c0_:c0_ + 8, :],
                                bass.AP(wsrc.tensor, c0_ * 128 * D + o * 128,
                                        [[D, 128], [128 * D, 8], [1, 128]]),
                                waits=(qk_free[mi * 2 + slot], ready["qkv"], ready["bias"]) if chh == 0 else ())
                        lastproj = None
                        for w4 in range(4):
                            bk_ = (oc * 4 + w4) % 2
                            for c in range(16):
                                w = ([pA_free[bk_], wt] + xn_evacs_all[w4 * 16:(w4 + 1) * 16]) if c == 0 else []
                                rhs = xnT3[:, w4 * 4:(w4 + 1) * 4, c * 128:(c + 1) * 128]
                                lastproj = em.op("tensor",
                                      lambda e, wslr=wslr, slot=slot, c=c, bk_=bk_, rhs=rhs: e.matmul(
                                          paf(bk_), wslr[slot][:, c * 128:(c + 1) * 128],
                                          rhs, start=(c == 0), stop=False),
                                      waits=w)
                            mtok = em.op("tensor",
                                         lambda e, br=br, oc=oc, g=g, bk_=bk_, mi=mi: e.matmul(
                                             paf(bk_), br[0:1, (g * 4 + oc) * 128:(g * 4 + oc + 1) * 128],
                                             ones_r[mi * 32: mi * 32 + 1, :], start=False, stop=True),
                                         inc="pe")
                            lastmm = mtok
                            qk_free[mi * 2 + slot] = mtok
                            ev_eng = "vector" if ev_flip % 2 == 0 else "scalar"
                            ev_flip += 1
                            dst = dstb[:, oc * 2048 + w4 * 512: oc * 2048 + (w4 + 1) * 512]
                            scl = CSC if mat == "q" else 1.0
                            if ev_eng == "vector":
                                ev = em.op(ev_eng,
                                           lambda e, bk_=bk_, dst=dst, scl=scl: e.tensor_scalar_mul(
                                               dst, paf(bk_), scl),
                                           waits=(mtok, proj_free[0]), inc="dve")
                            else:
                                ev = em.op(ev_eng,
                                           lambda e, bk_=bk_, dst=dst, scl=scl: e.activation(
                                               dst, paf(bk_), AF.Identity, scale=scl),
                                           waits=(mtok, proj_free[0]), inc="act")
                            pA_free[bk_] = ev
                            proj_evs.append(ev)
                for tc in range(16):
                    bk_ = tc % 2
                    for c in range(16):
                        w = [pA_free[bk_]] + (xn_evacs_all[tc * 4:(tc + 1) * 4] if True else []) if c == 0 else []
                        if tc == 0 and c == 0:
                            w = w + wts
                        stat = xnT[:, tc * 2048 + c * 128: tc * 2048 + (c + 1) * 128]
                        em.op("tensor",
                              lambda e, stat=stat, c=c, bk_=bk_: e.matmul(
                                  paf(bk_), stat, wvsl[:, c * 512:(c + 1) * 512],
                                  start=(c == 0), stop=False),
                              waits=w)
                    mtok = em.op("tensor",
                                 lambda e, g=g, bk_=bk_: e.matmul(
                                     paf(bk_), ones_r[64:65, 0:128], bv_r[0:1, g * 512:(g + 1) * 512],
                                     start=False, stop=True),
                                 inc="pe")
                    lastmm = mtok
                    ev_eng = "vector" if tc % 2 == 0 else "scalar"
                    ev = em.op(ev_eng,
                               lambda e, tc=tc, bk_=bk_: ecopy(
                                   e, v_sb[:, tc * 512:(tc + 1) * 512], paf(bk_)),
                               waits=(mtok, proj_free[0]),
                               inc="dve" if ev_eng == "vector" else "act")
                    pA_free[bk_] = ev
                    proj_evs.append(ev)
                w_free[0] = lastmm

                otok = None
                for Q in range(16):
                    oQev = None
                    for oc in range(4):
                        KW = (Q + 4) // 4
                        span = (Q + 1) * 128
                        par = (Q * 4 + oc) % 2
                        mc = stt[:, 32 + par * 16: 32 + (par + 1) * 16]
                        rtoks = []
                        for kw in range(KW):
                            kc = min(512, span - kw * 512)
                            w = [pA_free[kw]]
                            if Q == 0 and oc == 0 and kw == 0:
                                w += proj_evs
                            stok = em.op("tensor",
                                         lambda e, oc=oc, Q=Q, kw=kw, kc=kc: e.matmul(
                                             paf(kw, kc),
                                             qt_sb[:, oc * 2048 + Q * 128: oc * 2048 + (Q + 1) * 128],
                                             kt_sb[:, oc * 2048 + kw * 512: oc * 2048 + kw * 512 + kc],
                                             start=True, stop=(kw != KW - 1)),
                                         waits=w, inc="pe" if kw != KW - 1 else None)
                            if kw == KW - 1:
                                stok = em.op("tensor",
                                             lambda e, Q=Q: e.matmul(
                                                 pSh[:, Q * 128:(Q + 1) * 128],
                                                 mtriT[:], idb[:], start=False, stop=True),
                                             inc="pe")
                            rtoks.append(em.op(
                                "vector",
                                lambda e, kw=kw, kc=kc, mc=mc: e.reduce_max(
                                    mc[:, kw:kw + 1], paf(kw, kc), axis=AX.X, negate=True),
                                waits=(stok,), inc="dve"))
                        tnm = em.op("vector",
                                    lambda e, mc=mc, KW=KW: e.tensor_reduce(
                                        mc[:, 8:9], mc[:, 0:KW], axis=AX.X, op=ALU.min),
                                    fwaits=tuple(rtoks), inc="dve")
                        pslot = (Q * 4 + oc) % 2
                        etoks = []
                        for kw in range(KW):
                            kc = min(512, span - kw * 512)
                            w = [tnm] + ([pbuf_free[pslot]] if kw == 0 else [])
                            et = em.op("scalar",
                                       lambda e, kw=kw, kc=kc, pslot=pslot, mc=mc: e.activation(
                                           pbuf[pslot][:, kw * 512: kw * 512 + kc],
                                           paf(kw, kc), AF.Exp,
                                           bias=mc[:, 8:9],
                                           accum_out=mc[:, 9 + kw:10 + kw]),
                                       waits=tuple(w), inc="act")
                            pA_free[kw] = et
                            etoks.append(et)
                        tsum = em.op("vector",
                                     lambda e, mc=mc, KW=KW: e.reduce_sum(
                                         mc[:, 13:14], mc[:, 9:9 + KW], axis=AX.X),
                                     waits=tuple(etoks), inc="dve")
                        rslot = (Q * 4 + oc) % 4
                        trec = em.op("vector",
                                     lambda e, mc=mc, rslot=rslot: e.reciprocal(
                                         rres[:, rslot:rslot + 1], mc[:, 13:14]),
                                     fwaits=(tsum, r_free[rslot]), inc="dve")
                        obk = (Q * 4 + oc) % 2
                        NKT = Q + 1
                        for ktg in range(ceil_div(NKT, 4)):
                            nk = min(4, NKT - ktg * 4)
                            pbk = ktg % 2
                            ptok = None
                            for j in range(nk):
                                kt = ktg * 4 + j
                                w = [etoks[ktg], pt_free[pbk]] if j == 0 else []
                                ptok = em.op("tensor",
                                             lambda e, pbk=pbk, j=j, kt=kt, pslot=pslot: e.transpose(
                                                 pT[pbk][:, j * 128:(j + 1) * 128],
                                                 pbuf[pslot][:, kt * 128:(kt + 1) * 128], idb[:]),
                                             waits=w, inc="pe" if j == nk - 1 else None)
                            ptev = em.op("vector",
                                         lambda e, pbk=pbk, nk=nk: e.tensor_copy(
                                             out=ptb[pbk][:, 0:nk * 128], in_=pT[pbk][:, 0:nk * 128]),
                                         waits=(ptok, ptb_free[pbk]), inc="dve")
                            pt_free[pbk] = ptev
                            for j in range(nk):
                                kt = ktg * 4 + j
                                w = [ptev]
                                if ktg == 0 and j == 0:
                                    w.append(pB_free[obk])
                                otok = em.op("tensor",
                                             lambda e, pbk=pbk, j=j, kt=kt, oc=oc, obk=obk: e.matmul(
                                                 pB[obk][:, 0:128],
                                                 ptb[pbk][:, j * 128:(j + 1) * 128],
                                                 v_sb[:, kt * 512 + oc * 128: kt * 512 + (oc + 1) * 128],
                                                 start=(kt == 0), stop=(kt == NKT - 1)),
                                             waits=w, inc="pe" if kt == NKT - 1 else None)
                            ptb_free[pbk] = otok
                        pbuf_free[pslot] = otok
                        oslot = Q % 2
                        oev = em.op("vector",
                                    lambda e, oc=oc, obk=obk, rslot=rslot, oslot=oslot: e.tensor_scalar(
                                        out=og[oslot][:, oc * 128:(oc + 1) * 128],
                                        in0=pB[obk][:, 0:128], scalar1=rres[:, rslot:rslot + 1],
                                        scalar2=None, op0=ALU.mult),
                                    waits=(otok, og_free[oslot] if oc == 0 else None),
                                    fwaits=(trec,), inc="dve")
                        pB_free[obk] = oev
                        r_free[rslot] = oev
                        oQev = oev
                    xs = Q % 2
                    ld = em.dma("scalar", xrm[xs][:],
                                src_ap[rb + Q * 128: rb + (Q + 1) * 128, g * 512:(g + 1) * 512],
                                waits=(xrm_free[xs],))
                    oslot = Q % 2
                    tadd = em.op("vector",
                                 lambda e, xs=xs, oslot=oslot: e.tensor_tensor(
                                     xrm[xs][:], xrm[xs][:], og[oslot][:], op=ALU.add),
                                 waits=(ld, oQev), inc="dve")
                    st = em.dma("gpsimd",
                                dst_ap[rb + Q * 128: rb + (Q + 1) * 128, g * 512:(g + 1) * 512],
                                xrm[xs][:], waits=(tadd,))
                    xrm_free[xs] = st
                    og_free[oslot] = tadd
                proj_free[0] = otok
        As.close()

    setup_consts()
    preconvert()
    if "dumpw" in parts:
        em.join()
        dump_out = nc.dram_tensor("dump_out", [16 * 128, NC16 * 128], BF16,
                                  kind="ExternalOutput").ap()
        rb = sb("dump_rb", [128, NC16 * 128], BF16)
        for blk in range(16):
            ldd = em.dma("sync", rb[:], bass.AP(wd_s.tensor, blk * (128 * NC16 * 128),
                                                [[NC16 * 128, 128], [1, NC16 * 128]]))
            std = em.dma("gpsimd", dump_out[blk * 128:(blk + 1) * 128, :], rb[:], waits=(ldd,))
            em.join()
    if "ffn1" in parts:
        ffn(x_in, xr if ("attn" in parts or "ffn2" in parts) else y_out, "f1")
        em.join()
    if "attn" in parts:
        src = xr if "ffn1" in parts else x_in
        dst = xr if "ffn2" in parts else y_out
        attn(src, dst)
        em.join()
    if "ffn2" in parts:
        src = xr if ("ffn1" in parts or "attn" in parts) else x_in
        ffn(src, y_out, "f2")
        em.join()

    for eng in ENGS:
        for q in ("qs", "qg", "qa"):
            for i in range(Em.NLANE):
                em._emit_wait(eng, f"{q}{i}", em.counts[f"{q}{i}"])

    with stack:
        with nc.Block() as block:
            em.finalize(block)
    return nc


_PROG_CACHE = {}


def _get_prog(parts):
    key = tuple(parts)
    if key not in _PROG_CACHE:
        _PROG_CACHE[key] = build_program(parts)
    return _PROG_CACHE[key]


def kernel(L0_attn_x, L0_attn_Wq, L0_attn_bq, L0_attn_Wk, L0_attn_bk,
           L0_attn_Wv, L0_attn_bv, L0_ffn_w_gate, L0_ffn_w_up, L0_ffn_w_down,
           parts=("ffn1", "attn", "ffn2"), core_ids=None):
    x = np.ascontiguousarray(np.asarray(L0_attn_x, dtype=np.float32))
    weights = {
        "wq": np.ascontiguousarray(np.asarray(L0_attn_Wq, np.float32)),
        "wk": np.ascontiguousarray(np.asarray(L0_attn_Wk, np.float32)),
        "wv": np.ascontiguousarray(np.asarray(L0_attn_Wv, np.float32)),
        "bq": np.ascontiguousarray(np.asarray(L0_attn_bq, np.float32)),
        "bk": np.ascontiguousarray(np.asarray(L0_attn_bk, np.float32)),
        "bv": np.ascontiguousarray(np.asarray(L0_attn_bv, np.float32)),
        "wg": np.ascontiguousarray(np.asarray(L0_ffn_w_gate, np.float32)),
        "wu": np.ascontiguousarray(np.asarray(L0_ffn_w_up, np.float32)),
        "wd": np.ascontiguousarray(np.asarray(L0_ffn_w_down, np.float32)),
    }
    nc = _get_prog(parts)
    ncore = NCORES
    in_maps = []
    for core in range(ncore):
        m = dict(weights)
        m["x_in"] = x[core * BC:(core + 1) * BC].reshape(NT, D)
        in_maps.append(m)
    res = run_bass_kernel_spmd(nc, in_maps, core_ids=list(range(ncore)))
    out = np.empty((B, S, D), np.float32)
    for core in range(ncore):
        out[core * BC:(core + 1) * BC] = res.results[core]["y_out"].reshape(BC, S, D)
    return out


# version 12
# speedup vs baseline: 1.1514x; 1.0233x over previous
import numpy as np
import ml_dtypes
from contextlib import ExitStack

import concourse.bass as bass
import concourse.mybir as mybir
from concourse.bass_utils import run_bass_kernel_spmd

F32 = mybir.dt.float32
BF16 = mybir.dt.bfloat16
AF = mybir.ActivationFunctionType
ALU = mybir.AluOpType
AX = mybir.AxisListType

B, S, D, I, H, DH = 16, 2048, 2048, 8192, 16, 128
NCORES = 8
BC = B // NCORES
NT = BC * S
TG = 512
NTG = NT // TG
NCH = TG // 128
NIT = I // 512
NC16 = I // 128
CSC = float(1.0 / np.sqrt(DH))
MASKVAL = -1e9

ENGS = ("tensor", "vector", "scalar", "sync", "gpsimd")
DMA_SEM = {"sync": "qs", "gpsimd": "qg", "scalar": "qa"}


class Em:

    NLANE = 8

    def __init__(self, nc, stack):
        self.nc = nc
        self.streams = {e: [] for e in ENGS}
        self.sems = {}
        self.counts = {}
        names = ["pe", "dve", "act", "pool", "jn", "qtiny", "qkvc", "wdc"]
        names += [f"wguc{i}" for i in range(16)]
        for q in ("qs", "qg", "qa"):
            names += [f"{q}{i}" for i in range(self.NLANE)]
        self.private_lanes = ["qtiny", "qkvc", "wdc"] + [f"wguc{i}" for i in range(16)]
        for name in names:
            self.sems[name] = stack.enter_context(nc.semaphore(name))
            self.counts[name] = 0
        self.qidx = {"qs": 0, "qg": 0, "qa": 0}
        self.waited = {e: {} for e in ENGS}
        self.njoin = 0

    def _emit_wait(self, eng, sem, val):
        if val is None or val <= 0:
            return
        if self.waited[eng].get(sem, 0) >= val:
            return
        self.waited[eng][sem] = val
        h = self.sems[sem]
        self.streams[eng].append(lambda e, h=h, v=val: e.wait_ge(h, v))

    def _coalesce(self, waits, skip_eng=None):
        best = {}
        for t in waits:
            if t is None:
                continue
            sem, val, prod = t
            if prod == skip_eng:
                continue
            if best.get(sem, 0) < val:
                best[sem] = val
        return best

    def op(self, eng, fn, waits=(), inc=None, fwaits=()):
        for sem, val in self._coalesce(waits, skip_eng=eng).items():
            self._emit_wait(eng, sem, val)
        for sem, val in self._coalesce(fwaits).items():
            self._emit_wait(eng, sem, val)
        tok = None
        if inc:
            self.counts[inc] += 1
            v = self.counts[inc]
            h = self.sems[inc]
            self.streams[eng].append(lambda e, fn=fn, h=h, v=v: fn(e).then_inc(h, 1))
            tok = (inc, v, eng)
        else:
            self.streams[eng].append(lambda e, fn=fn: fn(e))
        return tok

    def dma(self, eng, out, in_, waits=(), lane=None, **kw):
        for sem, val in self._coalesce(waits).items():
            self._emit_wait(eng, sem, val)
        q = DMA_SEM[eng]
        if lane is None:
            lane = f"{q}{self.qidx[q] % self.NLANE}"
            self.qidx[q] += 1
        self.counts[lane] += 16
        v = self.counts[lane]
        h = self.sems[lane]
        self.streams[eng].append(
            lambda e, o=out, i=in_, h=h, v=v, kw=kw: e.dma_start(out=o, in_=i, **kw).then_inc(h, 16))
        return (lane, v, "dma")

    def join(self):
        for eng in ("sync", "gpsimd", "scalar"):
            q = DMA_SEM[eng]
            for i in range(self.NLANE):
                self._emit_wait(eng, f"{q}{i}", self.counts[f"{q}{i}"])
        for ln in self.private_lanes:
            self._emit_wait("gpsimd", ln, self.counts[ln])
        self.njoin += 1
        target = 5 * self.njoin
        h = self.sems["jn"]
        for eng in ENGS:
            self.streams[eng].append(lambda e, h=h: e.sem_inc(h, 1))
            self.streams[eng].append(lambda e, h=h, t=target: e.wait_ge(h, t))
            self.waited[eng] = {}

    def finalize(self, block):
        for eng in ENGS:
            stream = self.streams[eng]

            def runner(e, stream=stream):
                for fn in stream:
                    fn(e)
            getattr(block, eng)(runner)


def ceil_div(a, b):
    return (a + b - 1) // b


def ecopy(e, dst, src):
    if hasattr(e, "tensor_copy"):
        return e.tensor_copy(out=dst, in_=src)
    return e.copy(dst, src)


def build_program(parts=("ffn1", "attn", "ffn2"), dbg=(), paranoid=0):
    nc = bass.Bass("TRN2", target_bir_lowering=False)
    stack = ExitStack()

    x_in = nc.dram_tensor("x_in", [NT, D], F32, kind="ExternalInput").ap()
    y_out = nc.dram_tensor("y_out", [NT, D], F32, kind="ExternalOutput").ap()
    wq = nc.dram_tensor("wq", [D, D], F32, kind="ExternalInput").ap()
    wk = nc.dram_tensor("wk", [D, D], F32, kind="ExternalInput").ap()
    wv = nc.dram_tensor("wv", [D, D], F32, kind="ExternalInput").ap()
    bq = nc.dram_tensor("bq", [D], F32, kind="ExternalInput").ap()
    bk = nc.dram_tensor("bk", [D], F32, kind="ExternalInput").ap()
    bv = nc.dram_tensor("bv", [D], F32, kind="ExternalInput").ap()
    wg = nc.dram_tensor("wg", [D, I], F32, kind="ExternalInput").ap()
    wu = nc.dram_tensor("wu", [D, I], F32, kind="ExternalInput").ap()
    wd = nc.dram_tensor("wd", [I, D], F32, kind="ExternalInput").ap()

    xr = nc.dram_tensor("xr", [NT, D], F32, kind="Internal").ap()
    wg_s = nc.dram_tensor("wg_s", [NIT, 128, 16, 512], BF16, kind="Internal").ap()
    wu_s = nc.dram_tensor("wu_s", [NIT, 128, 16, 512], BF16, kind="Internal").ap()
    wd_s = nc.dram_tensor("wd_s", [NC16, 128, 16, 128], BF16,
                          kind="ExternalOutput" if "wdout" in parts else "Internal").ap()
    wq_s = nc.dram_tensor("wq_s", [D, D], BF16, kind="Internal").ap()
    wk_s = nc.dram_tensor("wk_s", [D, D], BF16, kind="Internal").ap()
    wv_s = nc.dram_tensor("wv_s", [D, D], BF16, kind="Internal").ap()

    em = Em(nc, stack)
    sb = lambda name, shape, dt: stack.enter_context(nc.sbuf_tensor(name, shape, dt))
    ps = lambda name, shape, dt: stack.enter_context(nc.psum_tensor(name, shape, dt))

    dbg_t = {}
    def dbg_dump(name, shape, sbuf_ap, waits):
        if name not in dbg:
            return
        if name not in dbg_t:
            dbg_t[name] = nc.dram_tensor("dbg_" + name, shape, F32, kind="ExternalOutput").ap()
        em.dma("gpsimd", dbg_t[name], sbuf_ap, waits=waits)

    ready = {"consts": None, "bias": None, "wgu": [None] * NIT, "qkv": None, "wd": None}

    pSh = ps("pSh", [128, 2048], F32)
    paf = lambda i, kc=512: pSh[:, i * 512: i * 512 + kc]
    pB = [ps(f"pB{i}", [128, 512], F32) for i in range(2)]
    pT = [ps(f"pT{i}", [128, 1024], BF16) for i in range(2)]

    idb = sb("idb", [128, 128], BF16)
    idf = sb("idf", [128, 128], F32)
    mtri = sb("mtri", [128, 128], F32)
    mtriT = sb("mtriT", [128, 128], BF16)
    eps_t = sb("eps_t", [128, 1], F32)
    ones_r = sb("ones_r", [65, 512], BF16)
    bias3 = sb("bias3", [65, D], BF16)
    bq_r = bias3[0:1, :]
    bk_r = bias3[32:33, :]
    bv_r = bias3[64:65, :]
    stt = sb("stt", [128, 64], F32)

    def setup_consts():
        em.op("gpsimd", lambda e: e.memset(idb[:], 0.0))
        em.op("gpsimd", lambda e: e.affine_select(
            out=idb[:], in_=idb[:], compare_op=ALU.not_equal, fill=1.0,
            base=0, pattern=[[-1, 128]], channel_multiplier=1))
        em.op("gpsimd", lambda e: e.memset(idf[:], 0.0))
        em.op("gpsimd", lambda e: e.affine_select(
            out=idf[:], in_=idf[:], compare_op=ALU.not_equal, fill=1.0,
            base=0, pattern=[[-1, 128]], channel_multiplier=1))
        em.op("gpsimd", lambda e: e.memset(mtri[:], 0.0))
        em.op("gpsimd", lambda e: e.affine_select(
            out=mtri[:], in_=mtri[:], compare_op=ALU.is_ge, fill=MASKVAL,
            base=0, pattern=[[-1, 128]], channel_multiplier=1))
        em.op("gpsimd", lambda e: e.memset(mtriT[:], 0.0))
        em.op("gpsimd", lambda e: e.affine_select(
            out=mtriT[:], in_=mtriT[:], compare_op=ALU.is_ge, fill=MASKVAL,
            base=0, pattern=[[1, 128]], channel_multiplier=-1))
        em.op("gpsimd", lambda e: e.memset(eps_t[:], 1e-5))
        t = em.op("gpsimd", lambda e: e.memset(ones_r[:], 1.0), inc="pool")
        ready["consts"] = t
        em.dma("gpsimd", bq_r, bq[None, :], lane="qtiny")
        em.dma("gpsimd", bk_r, bk[None, :], lane="qtiny")
        ready["bias"] = em.dma("gpsimd", bv_r, bv[None, :], lane="qtiny")

    def preconvert():
        t = None
        for it in range(NIT):
            for src, dst in ((wg, wg_s), (wu, wu_s)):
                for ch in range(2):
                    c0 = ch * 8
                    sap = bass.AP(src.tensor, c0 * 128 * I + it * 512,
                                  [[128 * I, 8], [I, 128], [1, 512]])
                    dap = dst[it][:, c0:c0 + 8, :].rearrange("p c n -> c p n")
                    t = em.dma("gpsimd", dap, sap, lane=f"wguc{it}")
            ready["wgu"][it] = t
        st_tok = None
        for cp in range(NC16):
            base = (cp // 4) * 512 + (cp % 4)
            sap = bass.AP(wd.tensor, base * D, [[4 * D, 128], [1, D]])
            dap = bass.AP(wd_s.tensor, cp * (128 * 16 * 128), [[2048, 128], [1, 2048]])
            st_tok = em.dma("gpsimd", dap, sap, lane="wdc")
        for src_, dst in ((wq, wq_s), (wk, wk_s), (wv, wv_s)):
            for ch in range(4):
                r0 = ch * 512
                sap = bass.AP(src_.tensor, r0 * D, [[D, 512], [1, D]])
                dap = bass.AP(dst.tensor, r0 * D, [[D, 512], [1, D]])
                t = em.dma("gpsimd", dap, sap, lane="qkvc")
        ready["qkv"] = t

    pt_free = [None, None]
    pA_free = [None, None, None, None]
    pB_free = [None, None]

    def ln_chunk(src_ap, row0, tc, xnT_buf, xbuf, xnbuf, xb_free, xn_free, par, sq_scr):
        sc = stt[:, par * 16: par * 16 + 16]
        ld = em.dma("sync", xbuf[:], src_ap[row0:row0 + 128, :], waits=(xb_free,))
        t0 = em.op("vector", lambda e: e.reduce_sum(sc[:, 0:1], xbuf[:], axis=AX.X),
                   waits=(ld,), inc="dve")
        t_sq = em.op("vector", lambda e: e.scalar_tensor_tensor(
                         out=sq_scr[:], in0=xbuf[:], scalar=1.0, in1=xbuf[:],
                         op0=ALU.mult, op1=ALU.mult, accum_out=sc[:, 1:2]),
                     waits=(ld,), inc="dve")
        t1 = em.op("vector", lambda e: e.tensor_scalar_mul(sc[:, 2:3], sc[:, 0:1], 1.0 / D),
                   fwaits=(t0,), inc="dve")
        t2 = em.op("vector", lambda e: e.tensor_tensor(sc[:, 3:4], sc[:, 2:3], sc[:, 2:3], op=ALU.mult),
                   fwaits=(t1,), inc="dve")
        t3 = em.op("vector", lambda e: e.tensor_scalar_mul(sc[:, 4:5], sc[:, 1:2], 1.0 / D),
                   fwaits=(t_sq,), inc="dve")
        t_var = em.op("vector", lambda e: e.tensor_tensor(sc[:, 5:6], sc[:, 4:5], sc[:, 3:4], op=ALU.subtract),
                      fwaits=(t2, t3), inc="dve")
        t_sd = em.op("scalar", lambda e: e.activation(sc[:, 6:7], sc[:, 5:6], AF.Sqrt, bias=eps_t[:]),
                     waits=(t_var, ready["consts"]), inc="act")
        t5 = em.op("vector", lambda e: e.reciprocal(sc[:, 7:8], sc[:, 6:7]),
                   waits=(t_sd,), inc="dve")
        t6 = em.op("vector", lambda e: e.tensor_tensor(sc[:, 8:9], sc[:, 2:3], sc[:, 7:8], op=ALU.mult),
                   fwaits=(t5,), inc="dve")
        t_bias = em.op("vector", lambda e: e.tensor_scalar_mul(sc[:, 9:10], sc[:, 8:9], -1.0),
                       fwaits=(t6,), inc="dve")
        t_xn = em.op("scalar", lambda e: e.activation(xnbuf[:], xbuf[:], AF.Identity,
                                                      scale=sc[:, 7:8], bias=sc[:, 9:10]),
                     waits=(t_bias, xn_free), inc="act")
        evac_toks = []
        for bnk in range(4):
            pbk = bnk % 2
            pb = pT[pbk]
            ptok = None
            for j in range(4):
                c = bnk * 4 + j
                w = [t_xn, pt_free[pbk], ready["consts"]] if j == 0 else []
                ptok = em.op("tensor",
                             lambda e, c=c, pb=pb, j=j: e.transpose(
                                 pb[:, j * 128:(j + 1) * 128],
                                 xnbuf[:, c * 128:(c + 1) * 128], idb[:]),
                             waits=w, inc="pe" if j == 3 else None)
            dst = xnT_buf[:, tc * 2048 + bnk * 512: tc * 2048 + (bnk + 1) * 512]
            ev_eng = "vector" if bnk % 2 == 0 else "scalar"
            ev = em.op(ev_eng, lambda e, pb=pb, dst=dst: ecopy(e, dst, pb[:, 0:512]),
                       waits=(ptok,), inc="dve" if ev_eng == "vector" else "act")
            pt_free[pbk] = ev
            evac_toks.append(ev)
        return evac_toks

    def ffn(src_ap, dst_ap, tag):
        fs = ExitStack()
        fsb = lambda name, shape, dt: fs.enter_context(nc.sbuf_tensor(tag + name, shape, dt))
        xnT = fsb("xnT", [128, NCH * 2048], BF16)
        hT = fsb("hT", [128, NCH * 8192], BF16)
        wsl = [fsb(f"w{i}", [128, 4096], BF16) for i in range(6)]
        xb = [fsb(f"xb{i}", [128, 2048], F32) for i in range(2)]
        xb2 = [fsb(f"xb2{i}", [128, 512], F32) for i in range(2)]
        xnb = [fsb(f"xnb{i}", [128, 2048], BF16) for i in range(2)]
        sg = [fsb(f"sg{i}", [128, 512], BF16) for i in range(2)]
        hst = [fsb(f"hst{i}", [128, 2048], BF16) for i in range(4)]
        ab = fsb("ab", [128, 2048], BF16)
        mtmp = fsb("mtmp", [128, 2048], BF16)
        hsp = [fsb(f"hsp{i}", [128, 2048], BF16) for i in range(2)]
        ytsb = fsb("ytsb", [128, 4 * 512], F32)

        xb_free = [None, None]
        xb2_free = [None, None]
        xn_free = [None, None]
        sg_free = [None, None]
        hst_free = [None, None, None, None]
        hsp_free = [None, None]
        wfree = [None] * 6
        hT3 = hT[:].rearrange("p (tc r) -> p tc r", tc=NCH)

        def emit_ln_group(tg):
            evs_all = []
            for tc in range(NCH):
                slot = tc % 2
                evs = ln_chunk(src_ap, tg * TG + tc * 128, tc, xnT,
                               xb[slot], xnb[slot], xb_free[slot], xn_free[slot],
                               tc % 2, mtmp)
                xn_free[slot] = evs[-1]
                xb_free[slot] = evs[-1]
                evs_all.extend(evs)
            return evs_all

        xn_evacs = emit_ln_group(0)
        if "f1xnT" in [tag + "xnT"] and tag + "xnT" in dbg:
            dbg_dump(tag + "xnT", [128, NCH * 2048], xnT[:], tuple(xn_evacs))
            dbg_dump(tag + "stt", [128, 64], stt[:], tuple(xn_evacs))

        for tg in range(NTG):
            row_base = tg * TG
            hT_ready = []
            gtok = utok = None
            pending_tr = [None]

            def flush_tr():
                if pending_tr[0] is not None:
                    f = pending_tr[0]
                    pending_tr[0] = None
                    f()

            for it in range(NIT):
                gs = [(2 * it + h) % 3 for h in range(2)]
                us = [3 + (2 * it + h) % 3 for h in range(2)]
                wts = {}
                for h in range(2):
                    wts[("g", h)] = em.dma(
                        "sync", wsl[gs[h]][:],
                        bass.AP(wg_s.tensor, it * (128 * 16 * 512) + h * 4096,
                                [[8192, 128], [1, 4096]]),
                        waits=(wfree[gs[h]], ready["wgu"][it]))
                    wts[("u", h)] = em.dma(
                        "sync", wsl[us[h]][:],
                        bass.AP(wu_s.tensor, it * (128 * 16 * 512) + h * 4096,
                                [[8192, 128], [1, 4096]]),
                        waits=(wfree[us[h]], ready["wgu"][it]))
                IT, s4 = it // 4, it % 4
                gq = [None] * 2
                uq = [None] * 2
                for tc in range(NCH):
                    gb = (it * NCH + tc) % 2
                    ub = 2 + gb
                    for k in range(16):
                        h = k // 8
                        kk = k % 8
                        stat = xnT[:, tc * 2048 + k * 128: tc * 2048 + (k + 1) * 128]
                        w = []
                        if k == 0:
                            w = [pA_free[gb], pA_free[ub]]
                            w.extend(xn_evacs[tc * 4: tc * 4 + 4])
                        if kk == 0:
                            w = w + [wts[("g", h)], wts[("u", h)]]
                        ginc = "pe" if kk == 7 else None
                        gtok = em.op("tensor",
                                     lambda e, stat=stat, gb=gb, sl=gs[h], kk=kk, k=k: e.matmul(
                                         paf(gb), stat, wsl[sl][:, kk * 512:(kk + 1) * 512],
                                         start=(k == 0), stop=(k == 15)),
                                     waits=w, inc=ginc)
                        utok = em.op("tensor",
                                     lambda e, stat=stat, ub=ub, sl=us[h], kk=kk, k=k: e.matmul(
                                         paf(ub), stat, wsl[sl][:, kk * 512:(kk + 1) * 512],
                                         start=(k == 0), stop=(k == 15)),
                                     inc=ginc)
                        if kk == 7:
                            gq[h] = gtok
                            uq[h] = utok
                    if tc == NCH - 1:
                        for h in range(2):
                            wfree[gs[h]] = gq[h]
                            wfree[us[h]] = uq[h]
                    flush_tr()
                    sslot = (it * NCH + tc) % 2
                    t_silu = em.op("scalar",
                                   lambda e, gb=gb, sslot=sslot: e.activation(
                                       sg[sslot][:], paf(gb), AF.Silu),
                                   waits=(gtok, sg_free[sslot]), inc="act")
                    pA_free[gb] = t_silu
                    hdst = hst[tc][:].rearrange("p (q r) -> p q r", q=4)[:, :, s4 * 128:(s4 + 1) * 128]
                    t_h = em.op("vector",
                                lambda e, sslot=sslot, ub=ub, hdst=hdst: e.tensor_tensor(
                                    hdst,
                                    sg[sslot][:].rearrange("p (j q) -> p q j", q=4),
                                    paf(ub).rearrange("p (j q) -> p q j", q=4),
                                    op=ALU.mult),
                                waits=(t_silu, utok,
                                       hst_free[tc] if s4 == 0 else None),
                                inc="dve")
                    pA_free[ub] = t_h
                    sg_free[sslot] = t_h
                    if s4 < 3:
                        continue
                    hs = hst[tc]
                    t_abs = em.op("scalar",
                                  lambda e, hs=hs: e.activation(ab[:], hs[:], AF.Abs),
                                  waits=(t_h,), inc="act")
                    q0, q1 = ab[:, 0:512], ab[:, 512:1024]
                    q2, q3 = ab[:, 1024:1536], ab[:, 1536:2048]
                    m1, n1 = mtmp[:, 0:512], mtmp[:, 512:1024]
                    m2, n2 = mtmp[:, 1024:1536], mtmp[:, 1536:2048]
                    em.op("vector", lambda e: e.tensor_tensor(m1, q0, q1, op=ALU.max), waits=(t_abs,))
                    em.op("vector", lambda e: e.tensor_tensor(n1, q0, q1, op=ALU.min))
                    em.op("vector", lambda e: e.tensor_tensor(m2, q2, q3, op=ALU.max))
                    em.op("vector", lambda e: e.tensor_tensor(n2, q2, q3, op=ALU.min))
                    em.op("vector", lambda e: e.tensor_tensor(m2, m1, m2, op=ALU.min))
                    em.op("vector", lambda e: e.tensor_tensor(n2, n1, n2, op=ALU.max))
                    em.op("vector", lambda e: e.tensor_tensor(m1, m2, n2, op=ALU.max))
                    hslot = (it * NCH + tc) % 2
                    happly = None
                    for q in range(4):
                        aq = ab[:, q * 512:(q + 1) * 512]
                        em.op("vector", lambda e, aq=aq: e.tensor_tensor(aq, aq, m1, op=ALU.is_ge))
                        happly = em.op("vector",
                                       lambda e, q=q, aq=aq, hslot=hslot, hs=hs: e.tensor_tensor(
                                           hsp[hslot][:, q * 512:(q + 1) * 512],
                                           hs[:, q * 512:(q + 1) * 512], aq, op=ALU.mult),
                                       waits=(hsp_free[hslot] if q == 0 else None,),
                                       inc="dve" if q == 3 else None)
                    hst_free[tc] = happly

                    def do_tr(IT=IT, tc=tc, hslot=hslot, happly=happly):
                        for s in range(4):
                            pbk = (IT * 4 + s) % 2
                            pb = pT[pbk]
                            ptok = None
                            for q in range(4):
                                w = [happly, pt_free[pbk]] if q == 0 else []
                                ptok = em.op("tensor",
                                             lambda e, pb=pb, q=q, s=s, hslot=hslot: e.transpose(
                                                 pb[:, q * 128:(q + 1) * 128],
                                                 hsp[hslot][:, q * 512 + s * 128: q * 512 + s * 128 + 128],
                                                 idb[:]),
                                             waits=w, inc="pe" if q == 3 else None)
                            if s == 3:
                                hsp_free[hslot] = ptok
                            cbase = IT * 16 + s * 4
                            dst = hT[:, tc * 8192 + cbase * 128: tc * 8192 + (cbase + 4) * 128]
                            ev_eng = "vector" if s % 2 == 0 else "scalar"
                            ev = em.op(ev_eng, lambda e, pb=pb, dst=dst: ecopy(e, dst, pb[:, 0:512]),
                                       waits=(ptok,), inc="dve" if ev_eng == "vector" else "act")
                            pt_free[pbk] = ev
                            hT_ready.append(ev)
                    pending_tr[0] = do_tr
            flush_tr()

            if tg + 1 < NTG:
                next_evacs = emit_ln_group(tg + 1)
            else:
                next_evacs = None

            for dq in range(4):
                yevs = []
                for d4 in range(4):
                    gd = dq * 4 + d4
                    dsl = [(2 * gd + j) % 6 for j in range(2)]
                    wdts = {}
                    for j in range(2):
                        for q4 in range(4):
                            cp0 = j * 32 + q4 * 8
                            wdts[j] = em.dma(
                                "scalar",
                                wsl[dsl[j]][:].rearrange("p (cp n) -> p cp n", cp=32)[:, q4 * 8:(q4 + 1) * 8, :],
                                bass.AP(wd_s.tensor, cp0 * (128 * 16 * 128) + gd * 128,
                                        [[2048, 128], [128 * 16 * 128, 8], [1, 128]]),
                                waits=(wfree[dsl[j]], ready["wd"]) if q4 == 0 else ())
                    ybk = gd % 2
                    ytok = None
                    ytoks = {}
                    for i in range(NC16):
                        j = i // 32
                        ii = i % 32
                        w = []
                        if i == 0:
                            w = [pA_free[ybk]]
                            if gd == 0:
                                w += hT_ready
                        if ii == 0:
                            w = w + [wdts[j]]
                        yinc = "pe" if ii == 31 else None
                        rhs = hT3[:, :, i * 128:(i + 1) * 128]
                        ytok = em.op("tensor",
                                     lambda e, i=i, ybk=ybk, sl=dsl[j], ii=ii, rhs=rhs: e.matmul(
                                         paf(ybk), wsl[sl][:, ii * 128:(ii + 1) * 128], rhs,
                                         start=(i == 0), stop=(i == NC16 - 1)),
                                     waits=w, inc=yinc)
                        if ii == 31:
                            ytoks[j] = ytok
                    for j in range(2):
                        wfree[dsl[j]] = ytoks[j]
                    ev = em.op("vector",
                               lambda e, d4=d4, ybk=ybk: e.tensor_copy(
                                   out=ytsb[:, d4 * 512:(d4 + 1) * 512], in_=paf(ybk)),
                               waits=(ytok,), inc="dve")
                    pA_free[ybk] = ev
                    yevs.append(ev)
                for tc in range(NCH):
                    xslot = tc % 2
                    rows = src_ap[row_base + tc * 128: row_base + tc * 128 + 128,
                                  dq * 512:(dq + 1) * 512]
                    xld = em.dma("scalar", xb2[xslot][:, 0:512], rows, waits=(xb2_free[xslot],))
                    pbk = tc % 2
                    ptok = None
                    for j in range(4):
                        d4 = j
                        w = [yevs[-1], pB_free[pbk], ready["consts"]] if j == 0 else []
                        ptok = em.op("tensor",
                                     lambda e, pbk=pbk, j=j, d4=d4, tc=tc: e.transpose(
                                         pB[pbk][:, j * 128:(j + 1) * 128],
                                         ytsb[:, d4 * 512 + tc * 128: d4 * 512 + tc * 128 + 128],
                                         idf[:]),
                                     waits=w, inc="pe" if j == 3 else None)
                    t_res = em.op("vector",
                                  lambda e, pbk=pbk, xslot=xslot: e.scalar_tensor_tensor(
                                      out=xb2[xslot][:, 0:512],
                                      in0=pB[pbk][:], scalar=0.5,
                                      in1=xb2[xslot][:, 0:512],
                                      op0=ALU.mult, op1=ALU.add),
                                  waits=(ptok, xld), inc="dve")
                    pB_free[pbk] = t_res
                    st = em.dma("gpsimd",
                                dst_ap[row_base + tc * 128: row_base + tc * 128 + 128,
                                       dq * 512:(dq + 1) * 512],
                                xb2[xslot][:, 0:512], waits=(t_res,))
                    xb2_free[xslot] = st
            if next_evacs is not None:
                xn_evacs = next_evacs
        fs.close()

    def attn(src_ap, dst_ap):
        As = ExitStack()
        asb = lambda name, shape, dt: As.enter_context(nc.sbuf_tensor(name, shape, dt))
        xnT = asb("a_xnT", [128, 16 * 2048], BF16)
        qt_sb = asb("a_qt", [128, 4 * 2048], BF16)
        kt_sb = asb("a_kt", [128, 4 * 2048], BF16)
        v_sb = asb("a_v", [128, 16 * 512], BF16)
        wqsl = [asb(f"a_wq{i}", [128, 2048], BF16) for i in range(2)]
        wksl = [asb(f"a_wk{i}", [128, 2048], BF16) for i in range(2)]
        wvsl = asb("a_wv", [128, 16 * 512], BF16)
        pbuf = [asb(f"a_p{i}", [128, 2048], BF16) for i in range(2)]
        ptb = [asb(f"a_pt{i}", [128, 512], BF16) for i in range(2)]
        og = [asb(f"a_og{i}", [128, 512], F32) for i in range(2)]
        xb = [asb(f"a_xb{i}", [128, 2048], F32) for i in range(2)]
        xnb = [asb(f"a_xnb{i}", [128, 2048], BF16) for i in range(2)]
        rres = asb("a_r", [128, 4], F32)
        a_sq = asb("a_sq", [128, 2048], BF16)
        xrm = [asb(f"a_xrm{i}", [128, 512], F32) for i in range(2)]

        xnT3 = xnT[:].rearrange("p (tc r) -> p tc r", tc=16)
        xb_free = [None, None]
        xn_free = [None, None]
        pbuf_free = [None, None]
        ptb_free = [None, None]
        og_free = [None, None]
        r_free = [None] * 4
        xrm_free = [None, None]
        w_free = [None]
        proj_free = [None]

        for b in range(BC):
            rb = b * S
            xn_evacs_all = []
            for tc in range(16):
                slot = tc % 2
                evs = ln_chunk(src_ap, rb + tc * 128, tc, xnT,
                               xb[slot], xnb[slot], xb_free[slot], xn_free[slot],
                               tc % 2, a_sq)
                xn_free[slot] = evs[-1]
                xb_free[slot] = evs[-1]
                xn_evacs_all.extend(evs)
            qk_free = [None, None, None, None]
            for g in range(4):
                wts = []
                for chh in range(2):
                    c0_ = chh * 8
                    wts.append(em.dma(
                        "sync",
                        wvsl[:].rearrange("p (c n) -> p c n", c=16)[:, c0_:c0_ + 8, :],
                        bass.AP(wv_s.tensor, c0_ * 128 * D + g * 512,
                                [[D, 128], [128 * D, 8], [1, 512]]),
                        waits=(w_free[0], ready["qkv"]) if chh == 0 else ()))
                ev_flip = 0
                proj_evs = []
                lastmm = None
                for mi, (mat, wslr, wsrc, br, dstb) in enumerate(
                        (("q", wqsl, wq_s, bq_r, qt_sb), ("k", wksl, wk_s, bk_r, kt_sb))):
                    for oc in range(4):
                        o = g * 4 + oc
                        slot = oc % 2
                        for chh in range(2):
                            c0_ = chh * 8
                            wt = em.dma(
                                "sync",
                                wslr[slot][:].rearrange("p (c n) -> p c n", c=16)[:, c0_:c0_ + 8, :],
                                bass.AP(wsrc.tensor, c0_ * 128 * D + o * 128,
                                        [[D, 128], [128 * D, 8], [1, 128]]),
                                waits=(qk_free[mi * 2 + slot], ready["qkv"], ready["bias"]) if chh == 0 else ())
                        lastproj = None
                        for w4 in range(4):
                            bk_ = (oc * 4 + w4) % 2
                            for c in range(16):
                                w = ([pA_free[bk_], wt] + xn_evacs_all[w4 * 16:(w4 + 1) * 16]) if c == 0 else []
                                rhs = xnT3[:, w4 * 4:(w4 + 1) * 4, c * 128:(c + 1) * 128]
                                lastproj = em.op("tensor",
                                      lambda e, wslr=wslr, slot=slot, c=c, bk_=bk_, rhs=rhs: e.matmul(
                                          paf(bk_), wslr[slot][:, c * 128:(c + 1) * 128],
                                          rhs, start=(c == 0), stop=False),
                                      waits=w)
                            mtok = em.op("tensor",
                                         lambda e, br=br, oc=oc, g=g, bk_=bk_, mi=mi: e.matmul(
                                             paf(bk_), br[0:1, (g * 4 + oc) * 128:(g * 4 + oc + 1) * 128],
                                             ones_r[mi * 32: mi * 32 + 1, :], start=False, stop=True),
                                         inc="pe")
                            lastmm = mtok
                            qk_free[mi * 2 + slot] = mtok
                            ev_eng = "vector" if ev_flip % 2 == 0 else "scalar"
                            ev_flip += 1
                            dst = dstb[:, oc * 2048 + w4 * 512: oc * 2048 + (w4 + 1) * 512]
                            scl = CSC if mat == "q" else 1.0
                            if ev_eng == "vector":
                                ev = em.op(ev_eng,
                                           lambda e, bk_=bk_, dst=dst, scl=scl: e.tensor_scalar_mul(
                                               dst, paf(bk_), scl),
                                           waits=(mtok, proj_free[0]), inc="dve")
                            else:
                                ev = em.op(ev_eng,
                                           lambda e, bk_=bk_, dst=dst, scl=scl: e.activation(
                                               dst, paf(bk_), AF.Identity, scale=scl),
                                           waits=(mtok, proj_free[0]), inc="act")
                            pA_free[bk_] = ev
                            proj_evs.append(ev)
                for tc in range(16):
                    bk_ = tc % 2
                    for c in range(16):
                        w = ([pA_free[bk_]] + xn_evacs_all[tc * 4:(tc + 1) * 4]) if c == 0 else []
                        if tc == 0 and c == 0:
                            w = w + wts
                        stat = xnT[:, tc * 2048 + c * 128: tc * 2048 + (c + 1) * 128]
                        em.op("tensor",
                              lambda e, stat=stat, c=c, bk_=bk_: e.matmul(
                                  paf(bk_), stat, wvsl[:, c * 512:(c + 1) * 512],
                                  start=(c == 0), stop=False),
                              waits=w)
                    mtok = em.op("tensor",
                                 lambda e, g=g, bk_=bk_: e.matmul(
                                     paf(bk_), ones_r[64:65, 0:128], bv_r[0:1, g * 512:(g + 1) * 512],
                                     start=False, stop=True),
                                 inc="pe")
                    lastmm = mtok
                    ev_eng = "vector" if tc % 2 == 0 else "scalar"
                    ev = em.op(ev_eng,
                               lambda e, tc=tc, bk_=bk_: ecopy(
                                   e, v_sb[:, tc * 512:(tc + 1) * 512], paf(bk_)),
                               waits=(mtok, proj_free[0]),
                               inc="dve" if ev_eng == "vector" else "act")
                    pA_free[bk_] = ev
                    proj_evs.append(ev)
                w_free[0] = lastmm
                if b == 0 and g == 0:
                    dbg_dump("aqt", [128, 4 * 2048], qt_sb[:], tuple(proj_evs))
                    dbg_dump("akt", [128, 4 * 2048], kt_sb[:], tuple(proj_evs))
                    dbg_dump("av", [128, 16 * 512], v_sb[:], tuple(proj_evs))
                    dbg_dump("axnT", [128, 16 * 2048], xnT[:], tuple(proj_evs))

                otok = None
                last_otok = [None]
                pending = [None]
                oev_by_Q = {}

                def flush_pending():
                    if pending[0] is not None:
                        f = pending[0]
                        pending[0] = None
                        f()

                for Q in range(16):
                    for oc in range(4):
                        KW = (Q + 4) // 4
                        span = (Q + 1) * 128
                        par = (Q * 4 + oc) % 2
                        mc = stt[:, 32 + par * 16: 32 + (par + 1) * 16]
                        rtoks = []
                        for kw in range(KW):
                            kc = min(512, span - kw * 512)
                            w = [pA_free[kw]]
                            if Q == 0 and oc == 0 and kw == 0:
                                w += proj_evs
                            stok = em.op("tensor",
                                         lambda e, oc=oc, Q=Q, kw=kw, kc=kc: e.matmul(
                                             paf(kw, kc),
                                             qt_sb[:, oc * 2048 + Q * 128: oc * 2048 + (Q + 1) * 128],
                                             kt_sb[:, oc * 2048 + kw * 512: oc * 2048 + kw * 512 + kc],
                                             start=True, stop=(kw != KW - 1)),
                                         waits=w, inc="pe" if kw != KW - 1 else None)
                            if kw == KW - 1:
                                stok = em.op("tensor",
                                             lambda e, Q=Q: e.matmul(
                                                 pSh[:, Q * 128:(Q + 1) * 128],
                                                 mtriT[:], idb[:], start=False, stop=True),
                                             inc="pe")
                            rtoks.append(em.op(
                                "vector",
                                lambda e, kw=kw, kc=kc, mc=mc: e.reduce_max(
                                    mc[:, kw:kw + 1], paf(kw, kc), axis=AX.X, negate=True),
                                waits=(stok,), inc="dve"))
                        flush_pending()
                        tnm = em.op("vector",
                                    lambda e, mc=mc, KW=KW: e.tensor_reduce(
                                        mc[:, 8:9], mc[:, 0:KW], axis=AX.X, op=ALU.min),
                                    fwaits=tuple(rtoks), inc="dve")
                        pslot = (Q * 4 + oc) % 2
                        etoks = []
                        for kw in range(KW):
                            kc = min(512, span - kw * 512)
                            w = [tnm] + ([pbuf_free[pslot]] if kw == 0 else [])
                            et = em.op("scalar",
                                       lambda e, kw=kw, kc=kc, pslot=pslot, mc=mc: e.activation(
                                           pbuf[pslot][:, kw * 512: kw * 512 + kc],
                                           paf(kw, kc), AF.Exp,
                                           bias=mc[:, 8:9],
                                           accum_out=mc[:, 9 + kw:10 + kw]),
                                       waits=tuple(w), inc="act")
                            pA_free[kw] = et
                            etoks.append(et)
                        tsum = em.op("vector",
                                     lambda e, mc=mc, KW=KW: e.reduce_sum(
                                         mc[:, 13:14], mc[:, 9:9 + KW], axis=AX.X),
                                     waits=tuple(etoks), inc="dve")
                        rslot = (Q * 4 + oc) % 4
                        trec = em.op("vector",
                                     lambda e, mc=mc, rslot=rslot: e.reciprocal(
                                         rres[:, rslot:rslot + 1], mc[:, 13:14]),
                                     fwaits=(tsum, r_free[rslot]), inc="dve")

                        def tail(Q=Q, oc=oc, etoks=etoks, trec=trec,
                                 pslot=pslot, rslot=rslot):
                            obk = (Q * 4 + oc) % 2
                            NKT = Q + 1
                            NG = ceil_div(NKT, 4)
                            otok = None
                            ptevs = {}

                            def emit_pt(ktg):
                                nk = min(4, NKT - ktg * 4)
                                pbk = ktg % 2
                                ptok = None
                                for j in range(nk):
                                    kt = ktg * 4 + j
                                    w = [etoks[ktg], pt_free[pbk]] if j == 0 else []
                                    ptok = em.op("tensor",
                                                 lambda e, pbk=pbk, j=j, kt=kt, pslot=pslot: e.transpose(
                                                     pT[pbk][:, j * 128:(j + 1) * 128],
                                                     pbuf[pslot][:, kt * 128:(kt + 1) * 128], idb[:]),
                                                 waits=w, inc="pe" if j == nk - 1 else None)
                                ev_eng = "vector" if ktg % 2 == 0 else "scalar"
                                ptevs[ktg] = em.op(ev_eng,
                                                   lambda e, pbk=pbk, nk=nk: ecopy(
                                                       e, ptb[pbk][:, 0:nk * 128], pT[pbk][:, 0:nk * 128]),
                                                   waits=(ptok, ptb_free[pbk]),
                                                   inc="dve" if ev_eng == "vector" else "act")
                                pt_free[pbk] = ptevs[ktg]

                            def emit_o(ktg):
                                nonlocal otok
                                nk = min(4, NKT - ktg * 4)
                                pbk = ktg % 2
                                for j in range(nk):
                                    kt = ktg * 4 + j
                                    w = [ptevs[ktg]]
                                    if ktg == 0 and j == 0:
                                        w.append(pB_free[obk])
                                    otok = em.op("tensor",
                                                 lambda e, pbk=pbk, j=j, kt=kt, oc=oc, obk=obk: e.matmul(
                                                     pB[obk][:, 0:128],
                                                     ptb[pbk][:, j * 128:(j + 1) * 128],
                                                     v_sb[:, kt * 512 + oc * 128: kt * 512 + (oc + 1) * 128],
                                                     start=(kt == 0), stop=(kt == NKT - 1)),
                                                 waits=w, inc="pe" if kt == NKT - 1 else None)
                                ptb_free[pbk] = otok

                            emit_pt(0)
                            for ktg in range(1, NG):
                                emit_pt(ktg)
                                emit_o(ktg - 1)
                            emit_o(NG - 1)
                            pbuf_free[pslot] = otok
                            last_otok[0] = otok
                            oslot = Q % 2
                            oev = em.op("vector",
                                        lambda e, oc=oc, obk=obk, rslot=rslot, oslot=oslot: e.tensor_scalar(
                                            out=og[oslot][:, oc * 128:(oc + 1) * 128],
                                            in0=pB[obk][:, 0:128], scalar1=rres[:, rslot:rslot + 1],
                                            scalar2=None, op0=ALU.mult),
                                        waits=(otok, og_free[oslot] if oc == 0 else None),
                                        fwaits=(trec,), inc="dve")
                            pB_free[obk] = oev
                            r_free[rslot] = oev
                            oev_by_Q.setdefault(Q, []).append(oev)
                            if oc == 3:
                                xs = Q % 2
                                ld = em.dma("scalar", xrm[xs][:],
                                            src_ap[rb + Q * 128: rb + (Q + 1) * 128,
                                                   g * 512:(g + 1) * 512],
                                            waits=(xrm_free[xs],))
                                tadd = em.op("vector",
                                             lambda e, xs=xs, oslot=oslot: e.tensor_tensor(
                                                 xrm[xs][:], xrm[xs][:], og[oslot][:], op=ALU.add),
                                             waits=(ld, oev), inc="dve")
                                st = em.dma("gpsimd",
                                            dst_ap[rb + Q * 128: rb + (Q + 1) * 128,
                                                   g * 512:(g + 1) * 512],
                                            xrm[xs][:], waits=(tadd,))
                                xrm_free[xs] = st
                                og_free[oslot] = tadd
                        pending[0] = tail
                flush_pending()
                otok = last_otok[0]
                proj_free[0] = otok
        As.close()

    setup_consts()
    preconvert()
    if "dumpw" in parts:
        em.join()
        dump_out = nc.dram_tensor("dump_out", [16 * 128, NC16 * 128], BF16,
                                  kind="ExternalOutput").ap()
        rb = sb("dump_rb", [128, NC16 * 128], BF16)
        for blk in range(16):
            ldd = em.dma("sync", rb[:], bass.AP(wd_s.tensor, blk * (128 * NC16 * 128),
                                                [[NC16 * 128, 128], [1, NC16 * 128]]))
            std = em.dma("gpsimd", dump_out[blk * 128:(blk + 1) * 128, :], rb[:], waits=(ldd,))
            em.join()
    if "ffn1" in parts:
        ffn(x_in, xr if ("attn" in parts or "ffn2" in parts) else y_out, "f1")
        em.join()
    if "attn" in parts:
        src = xr if "ffn1" in parts else x_in
        dst = xr if "ffn2" in parts else y_out
        attn(src, dst)
        em.join()
    if "ffn2" in parts:
        src = xr if ("ffn1" in parts or "attn" in parts) else x_in
        ffn(src, y_out, "f2")
        em.join()

    for eng in ENGS:
        for q in ("qs", "qg", "qa"):
            for i in range(Em.NLANE):
                em._emit_wait(eng, f"{q}{i}", em.counts[f"{q}{i}"])
        for ln in em.private_lanes:
            em._emit_wait(eng, ln, em.counts[ln])

    with stack:
        with nc.Block() as block:
            em.finalize(block)
    return nc


_PROG_CACHE = {}


def _get_prog(parts):
    key = tuple(parts)
    if key not in _PROG_CACHE:
        _PROG_CACHE[key] = build_program(parts)
    return _PROG_CACHE[key]


def kernel(L0_attn_x, L0_attn_Wq, L0_attn_bq, L0_attn_Wk, L0_attn_bk,
           L0_attn_Wv, L0_attn_bv, L0_ffn_w_gate, L0_ffn_w_up, L0_ffn_w_down,
           parts=("ffn1", "attn", "ffn2"), core_ids=None):
    x = np.ascontiguousarray(np.asarray(L0_attn_x, dtype=np.float32))
    weights = {
        "wq": np.ascontiguousarray(np.asarray(L0_attn_Wq, np.float32)),
        "wk": np.ascontiguousarray(np.asarray(L0_attn_Wk, np.float32)),
        "wv": np.ascontiguousarray(np.asarray(L0_attn_Wv, np.float32)),
        "bq": np.ascontiguousarray(np.asarray(L0_attn_bq, np.float32)),
        "bk": np.ascontiguousarray(np.asarray(L0_attn_bk, np.float32)),
        "bv": np.ascontiguousarray(np.asarray(L0_attn_bv, np.float32)),
        "wg": np.ascontiguousarray(np.asarray(L0_ffn_w_gate, np.float32)),
        "wu": np.ascontiguousarray(np.asarray(L0_ffn_w_up, np.float32)),
        "wd": np.ascontiguousarray(np.asarray(L0_ffn_w_down, np.float32)),
    }
    nc = _get_prog(parts)
    ncore = NCORES
    in_maps = []
    for core in range(ncore):
        m = dict(weights)
        m["x_in"] = x[core * BC:(core + 1) * BC].reshape(NT, D)
        in_maps.append(m)
    res = run_bass_kernel_spmd(nc, in_maps, core_ids=list(range(ncore)))
    out = np.empty((B, S, D), np.float32)
    for core in range(ncore):
        out[core * BC:(core + 1) * BC] = res.results[core]["y_out"].reshape(BC, S, D)
    return out


# version 13
# speedup vs baseline: 1.2034x; 1.0452x over previous
import numpy as np
import ml_dtypes
from contextlib import ExitStack

import concourse.bass as bass
import concourse.mybir as mybir
from concourse.bass_utils import run_bass_kernel_spmd

F32 = mybir.dt.float32
BF16 = mybir.dt.bfloat16
AF = mybir.ActivationFunctionType
ALU = mybir.AluOpType
AX = mybir.AxisListType

B, S, D, I, H, DH = 16, 2048, 2048, 8192, 16, 128
NCORES = 8
BC = B // NCORES
NT = BC * S
TG = 512
NTG = NT // TG
NCH = TG // 128
NIT = I // 512
NC16 = I // 128
CSC = float(1.0 / np.sqrt(DH))
MASKVAL = -1e9

ENGS = ("tensor", "vector", "scalar", "sync", "gpsimd")
DMA_SEM = {"sync": "qs", "gpsimd": "qg", "scalar": "qa"}


class Em:

    NLANE = 8

    def __init__(self, nc, stack):
        self.nc = nc
        self.streams = {e: [] for e in ENGS}
        self.sems = {}
        self.counts = {}
        names = ["pe", "dve", "act", "pool", "jn", "qtiny", "qkvc", "wdc"]
        names += [f"wguc{i}" for i in range(16)]
        for q in ("qs", "qg", "qa"):
            names += [f"{q}{i}" for i in range(self.NLANE)]
        self.private_lanes = ["qtiny", "qkvc", "wdc"] + [f"wguc{i}" for i in range(16)]
        for name in names:
            self.sems[name] = stack.enter_context(nc.semaphore(name))
            self.counts[name] = 0
        self.qidx = {"qs": 0, "qg": 0, "qa": 0}
        self.waited = {e: {} for e in ENGS}
        self.njoin = 0

    def _emit_wait(self, eng, sem, val):
        if val is None or val <= 0:
            return
        if self.waited[eng].get(sem, 0) >= val:
            return
        self.waited[eng][sem] = val
        h = self.sems[sem]
        self.streams[eng].append(lambda e, h=h, v=val: e.wait_ge(h, v))

    def _coalesce(self, waits, skip_eng=None):
        best = {}
        for t in waits:
            if t is None:
                continue
            sem, val, prod = t
            if prod == skip_eng:
                continue
            if best.get(sem, 0) < val:
                best[sem] = val
        return best

    def op(self, eng, fn, waits=(), inc=None, fwaits=()):
        for sem, val in self._coalesce(waits, skip_eng=eng).items():
            self._emit_wait(eng, sem, val)
        for sem, val in self._coalesce(fwaits).items():
            self._emit_wait(eng, sem, val)
        tok = None
        if inc:
            self.counts[inc] += 1
            v = self.counts[inc]
            h = self.sems[inc]
            self.streams[eng].append(lambda e, fn=fn, h=h, v=v: fn(e).then_inc(h, 1))
            tok = (inc, v, eng)
        else:
            self.streams[eng].append(lambda e, fn=fn: fn(e))
        return tok

    def dma(self, eng, out, in_, waits=(), lane=None, **kw):
        for sem, val in self._coalesce(waits).items():
            self._emit_wait(eng, sem, val)
        q = DMA_SEM[eng]
        if lane is None:
            lane = f"{q}{self.qidx[q] % self.NLANE}"
            self.qidx[q] += 1
        self.counts[lane] += 16
        v = self.counts[lane]
        h = self.sems[lane]
        self.streams[eng].append(
            lambda e, o=out, i=in_, h=h, v=v, kw=kw: e.dma_start(out=o, in_=i, **kw).then_inc(h, 16))
        return (lane, v, "dma")

    def join(self):
        for eng in ("sync", "gpsimd", "scalar"):
            q = DMA_SEM[eng]
            for i in range(self.NLANE):
                self._emit_wait(eng, f"{q}{i}", self.counts[f"{q}{i}"])
        for ln in self.private_lanes:
            self._emit_wait("gpsimd", ln, self.counts[ln])
        self.njoin += 1
        target = 5 * self.njoin
        h = self.sems["jn"]
        for eng in ENGS:
            self.streams[eng].append(lambda e, h=h: e.sem_inc(h, 1))
            self.streams[eng].append(lambda e, h=h, t=target: e.wait_ge(h, t))
            self.waited[eng] = {}

    def finalize(self, block):
        for eng in ENGS:
            stream = self.streams[eng]

            def runner(e, stream=stream):
                for fn in stream:
                    fn(e)
            getattr(block, eng)(runner)


def ceil_div(a, b):
    return (a + b - 1) // b


def ecopy(e, dst, src):
    if hasattr(e, "tensor_copy"):
        return e.tensor_copy(out=dst, in_=src)
    return e.copy(dst, src)


def build_program(parts=("ffn1", "attn", "ffn2"), dbg=(), paranoid=0):
    nc = bass.Bass("TRN2", target_bir_lowering=False)
    stack = ExitStack()

    x_in = nc.dram_tensor("x_in", [NT, D], F32, kind="ExternalInput").ap()
    y_out = nc.dram_tensor("y_out", [NT, D], F32, kind="ExternalOutput").ap()
    wq = nc.dram_tensor("wq", [D, D], F32, kind="ExternalInput").ap()
    wk = nc.dram_tensor("wk", [D, D], F32, kind="ExternalInput").ap()
    wv = nc.dram_tensor("wv", [D, D], F32, kind="ExternalInput").ap()
    bq = nc.dram_tensor("bq", [D], F32, kind="ExternalInput").ap()
    bk = nc.dram_tensor("bk", [D], F32, kind="ExternalInput").ap()
    bv = nc.dram_tensor("bv", [D], F32, kind="ExternalInput").ap()
    wg = nc.dram_tensor("wg", [D, I], F32, kind="ExternalInput").ap()
    wu = nc.dram_tensor("wu", [D, I], F32, kind="ExternalInput").ap()
    wd = nc.dram_tensor("wd", [I, D], F32, kind="ExternalInput").ap()

    xr = nc.dram_tensor("xr", [NT, D], F32, kind="Internal").ap()
    wg_s = nc.dram_tensor("wg_s", [NIT, 128, 16, 512], BF16, kind="Internal").ap()
    wu_s = nc.dram_tensor("wu_s", [NIT, 128, 16, 512], BF16, kind="Internal").ap()
    wd_s = nc.dram_tensor("wd_s", [NC16, 128, 16, 128], BF16,
                          kind="ExternalOutput" if "wdout" in parts else "Internal").ap()
    wq_s = nc.dram_tensor("wq_s", [D, D], BF16, kind="Internal").ap()
    wk_s = nc.dram_tensor("wk_s", [D, D], BF16, kind="Internal").ap()
    wv_s = nc.dram_tensor("wv_s", [D, D], BF16, kind="Internal").ap()

    em = Em(nc, stack)
    sb = lambda name, shape, dt: stack.enter_context(nc.sbuf_tensor(name, shape, dt))
    ps = lambda name, shape, dt: stack.enter_context(nc.psum_tensor(name, shape, dt))

    dbg_t = {}
    def dbg_dump(name, shape, sbuf_ap, waits):
        if name not in dbg:
            return
        if name not in dbg_t:
            dbg_t[name] = nc.dram_tensor("dbg_" + name, shape, F32, kind="ExternalOutput").ap()
        em.dma("gpsimd", dbg_t[name], sbuf_ap, waits=waits)

    ready = {"consts": None, "bias": None, "wgu": [None] * NIT, "qkv": None, "wd": None}

    pSh = ps("pSh", [128, 2048], F32)
    paf = lambda i, kc=512: pSh[:, i * 512: i * 512 + kc]
    pB = [ps(f"pB{i}", [128, 512], F32) for i in range(2)]
    pT = [ps(f"pT{i}", [128, 1024], BF16) for i in range(2)]

    idb = sb("idb", [128, 128], BF16)
    idf = sb("idf", [128, 128], F32)
    mtri = sb("mtri", [128, 128], F32)
    mtriT = sb("mtriT", [128, 128], BF16)
    eps_t = sb("eps_t", [128, 1], F32)
    ones_r = sb("ones_r", [65, 512], BF16)
    bias3 = sb("bias3", [65, D], BF16)
    bq_r = bias3[0:1, :]
    bk_r = bias3[32:33, :]
    bv_r = bias3[64:65, :]
    stt = sb("stt", [128, 64], F32)

    def setup_consts():
        em.op("gpsimd", lambda e: e.memset(idb[:], 0.0))
        em.op("gpsimd", lambda e: e.affine_select(
            out=idb[:], in_=idb[:], compare_op=ALU.not_equal, fill=1.0,
            base=0, pattern=[[-1, 128]], channel_multiplier=1))
        em.op("gpsimd", lambda e: e.memset(idf[:], 0.0))
        em.op("gpsimd", lambda e: e.affine_select(
            out=idf[:], in_=idf[:], compare_op=ALU.not_equal, fill=1.0,
            base=0, pattern=[[-1, 128]], channel_multiplier=1))
        em.op("gpsimd", lambda e: e.memset(mtri[:], 0.0))
        em.op("gpsimd", lambda e: e.affine_select(
            out=mtri[:], in_=mtri[:], compare_op=ALU.is_ge, fill=MASKVAL,
            base=0, pattern=[[-1, 128]], channel_multiplier=1))
        em.op("gpsimd", lambda e: e.memset(mtriT[:], 0.0))
        em.op("gpsimd", lambda e: e.affine_select(
            out=mtriT[:], in_=mtriT[:], compare_op=ALU.is_ge, fill=MASKVAL,
            base=0, pattern=[[1, 128]], channel_multiplier=-1))
        em.op("gpsimd", lambda e: e.memset(eps_t[:], 1e-5))
        t = em.op("gpsimd", lambda e: e.memset(ones_r[:], 1.0), inc="pool")
        ready["consts"] = t
        em.dma("gpsimd", bq_r, bq[None, :], lane="qtiny")
        em.dma("gpsimd", bk_r, bk[None, :], lane="qtiny")
        ready["bias"] = em.dma("gpsimd", bv_r, bv[None, :], lane="qtiny")

    def preconvert():
        t = None
        for it in range(NIT):
            for src, dst in ((wg, wg_s), (wu, wu_s)):
                for ch in range(2):
                    c0 = ch * 8
                    sap = bass.AP(src.tensor, c0 * 128 * I + it * 512,
                                  [[128 * I, 8], [I, 128], [1, 512]])
                    dap = dst[it][:, c0:c0 + 8, :].rearrange("p c n -> c p n")
                    t = em.dma("gpsimd", dap, sap, lane=f"wguc{it}")
            ready["wgu"][it] = t
        st_tok = None
        for cp in range(NC16):
            base = (cp // 4) * 512 + (cp % 4)
            sap = bass.AP(wd.tensor, base * D, [[4 * D, 128], [1, D]])
            dap = bass.AP(wd_s.tensor, cp * (128 * 16 * 128), [[2048, 128], [1, 2048]])
            st_tok = em.dma("gpsimd", dap, sap, lane="wdc")
        for src_, dst in ((wq, wq_s), (wk, wk_s), (wv, wv_s)):
            for ch in range(4):
                r0 = ch * 512
                sap = bass.AP(src_.tensor, r0 * D, [[D, 512], [1, D]])
                dap = bass.AP(dst.tensor, r0 * D, [[D, 512], [1, D]])
                t = em.dma("gpsimd", dap, sap, lane="qkvc")
        ready["qkv"] = t

    pt_free = [None, None]
    pA_free = [None, None, None, None]
    pB_free = [None, None]

    def ln_chunk(src_ap, row0, tc, xnT_buf, xbuf, xnbuf, xb_free, xn_free, par, sq_scr):
        sc = stt[:, par * 16: par * 16 + 16]
        ld = em.dma("sync", xbuf[:], src_ap[row0:row0 + 128, :], waits=(xb_free,))
        t0 = em.op("vector", lambda e: e.reduce_sum(sc[:, 0:1], xbuf[:], axis=AX.X),
                   waits=(ld,), inc="dve")
        t_sq = em.op("vector", lambda e: e.scalar_tensor_tensor(
                         out=sq_scr[:], in0=xbuf[:], scalar=1.0, in1=xbuf[:],
                         op0=ALU.mult, op1=ALU.mult, accum_out=sc[:, 1:2]),
                     waits=(ld,), inc="dve")
        t1 = em.op("vector", lambda e: e.tensor_scalar_mul(sc[:, 2:3], sc[:, 0:1], 1.0 / D),
                   fwaits=(t0,), inc="dve")
        t2 = em.op("vector", lambda e: e.tensor_tensor(sc[:, 3:4], sc[:, 2:3], sc[:, 2:3], op=ALU.mult),
                   fwaits=(t1,), inc="dve")
        t3 = em.op("vector", lambda e: e.tensor_scalar_mul(sc[:, 4:5], sc[:, 1:2], 1.0 / D),
                   fwaits=(t_sq,), inc="dve")
        t_var = em.op("vector", lambda e: e.tensor_tensor(sc[:, 5:6], sc[:, 4:5], sc[:, 3:4], op=ALU.subtract),
                      fwaits=(t2, t3), inc="dve")
        t_sd = em.op("scalar", lambda e: e.activation(sc[:, 6:7], sc[:, 5:6], AF.Sqrt, bias=eps_t[:]),
                     waits=(t_var, ready["consts"]), inc="act")
        t5 = em.op("vector", lambda e: e.reciprocal(sc[:, 7:8], sc[:, 6:7]),
                   waits=(t_sd,), inc="dve")
        t6 = em.op("vector", lambda e: e.tensor_tensor(sc[:, 8:9], sc[:, 2:3], sc[:, 7:8], op=ALU.mult),
                   fwaits=(t5,), inc="dve")
        t_bias = em.op("vector", lambda e: e.tensor_scalar_mul(sc[:, 9:10], sc[:, 8:9], -1.0),
                       fwaits=(t6,), inc="dve")
        t_xn = em.op("scalar", lambda e: e.activation(xnbuf[:], xbuf[:], AF.Identity,
                                                      scale=sc[:, 7:8], bias=sc[:, 9:10]),
                     waits=(t_bias, xn_free), inc="act")
        evac_toks = []
        for bnk in range(4):
            pbk = bnk % 2
            pb = pT[pbk]
            ptok = None
            for j in range(4):
                c = bnk * 4 + j
                w = [t_xn, pt_free[pbk], ready["consts"]] if j == 0 else []
                ptok = em.op("tensor",
                             lambda e, c=c, pb=pb, j=j: e.transpose(
                                 pb[:, j * 128:(j + 1) * 128],
                                 xnbuf[:, c * 128:(c + 1) * 128], idb[:]),
                             waits=w, inc="pe" if j == 3 else None)
            dst = xnT_buf[:, tc * 2048 + bnk * 512: tc * 2048 + (bnk + 1) * 512]
            ev_eng = "vector" if bnk % 2 == 0 else "scalar"
            ev = em.op(ev_eng, lambda e, pb=pb, dst=dst: ecopy(e, dst, pb[:, 0:512]),
                       waits=(ptok,), inc="dve" if ev_eng == "vector" else "act")
            pt_free[pbk] = ev
            evac_toks.append(ev)
        return evac_toks

    def ffn(src_ap, dst_ap, tag):
        fs = ExitStack()
        fsb = lambda name, shape, dt: fs.enter_context(nc.sbuf_tensor(tag + name, shape, dt))
        xnT = fsb("xnT", [128, NCH * 2048], BF16)
        hT = fsb("hT", [128, NCH * 8192], BF16)
        wsl = [fsb(f"w{i}", [128, 4096], BF16) for i in range(6)]
        xb = [fsb(f"xb{i}", [128, 2048], F32) for i in range(2)]
        xb2 = [fsb(f"xb2{i}", [128, 512], F32) for i in range(2)]
        xnb = [fsb(f"xnb{i}", [128, 2048], BF16) for i in range(2)]
        sg = [fsb(f"sg{i}", [128, 512], BF16) for i in range(2)]
        hst = [fsb(f"hst{i}", [128, 2048], BF16) for i in range(4)]
        ab = fsb("ab", [128, 2048], BF16)
        mtmp = fsb("mtmp", [128, 2048], BF16)
        hsp = [fsb(f"hsp{i}", [128, 2048], BF16) for i in range(2)]
        ytsb = fsb("ytsb", [128, 4 * 512], F32)

        xb_free = [None, None]
        xb2_free = [None, None]
        xn_free = [None, None]
        sg_free = [None, None]
        hst_free = [None, None, None, None]
        hsp_free = [None, None]
        wfree = [None] * 6
        hT3 = hT[:].rearrange("p (tc r) -> p tc r", tc=NCH)

        def emit_ln_group(tg):
            evs_all = []
            for tc in range(NCH):
                slot = tc % 2
                evs = ln_chunk(src_ap, tg * TG + tc * 128, tc, xnT,
                               xb[slot], xnb[slot], xb_free[slot], xn_free[slot],
                               tc % 2, mtmp)
                xn_free[slot] = evs[-1]
                xb_free[slot] = evs[-1]
                evs_all.extend(evs)
            return evs_all

        xn_evacs = emit_ln_group(0)
        if "f1xnT" in [tag + "xnT"] and tag + "xnT" in dbg:
            dbg_dump(tag + "xnT", [128, NCH * 2048], xnT[:], tuple(xn_evacs))
            dbg_dump(tag + "stt", [128, 64], stt[:], tuple(xn_evacs))

        for tg in range(NTG):
            row_base = tg * TG
            hT_ready = []
            gtok = utok = None
            pending_tr = [None]

            def flush_tr():
                if pending_tr[0] is not None:
                    f = pending_tr[0]
                    pending_tr[0] = None
                    f()

            for it in range(NIT):
                gs = [(2 * it + h) % 3 for h in range(2)]
                us = [3 + (2 * it + h) % 3 for h in range(2)]
                wts = {}
                for h in range(2):
                    wts[("g", h)] = em.dma(
                        "sync", wsl[gs[h]][:],
                        bass.AP(wg_s.tensor, it * (128 * 16 * 512) + h * 4096,
                                [[8192, 128], [1, 4096]]),
                        waits=(wfree[gs[h]], ready["wgu"][it]))
                    wts[("u", h)] = em.dma(
                        "sync", wsl[us[h]][:],
                        bass.AP(wu_s.tensor, it * (128 * 16 * 512) + h * 4096,
                                [[8192, 128], [1, 4096]]),
                        waits=(wfree[us[h]], ready["wgu"][it]))
                IT, s4 = it // 4, it % 4
                gq = [None] * 2
                uq = [None] * 2
                for tc in range(NCH):
                    gb = (it * NCH + tc) % 2
                    ub = 2 + gb
                    for k in range(16):
                        h = k // 8
                        kk = k % 8
                        stat = xnT[:, tc * 2048 + k * 128: tc * 2048 + (k + 1) * 128]
                        w = []
                        if k == 0:
                            w = [pA_free[gb], pA_free[ub]]
                            w.extend(xn_evacs[tc * 4: tc * 4 + 4])
                        if kk == 0:
                            w = w + [wts[("g", h)], wts[("u", h)]]
                        ginc = "pe" if kk == 7 else None
                        gtok = em.op("tensor",
                                     lambda e, stat=stat, gb=gb, sl=gs[h], kk=kk, k=k: e.matmul(
                                         paf(gb), stat, wsl[sl][:, kk * 512:(kk + 1) * 512],
                                         start=(k == 0), stop=(k == 15)),
                                     waits=w, inc=ginc)
                        utok = em.op("tensor",
                                     lambda e, stat=stat, ub=ub, sl=us[h], kk=kk, k=k: e.matmul(
                                         paf(ub), stat, wsl[sl][:, kk * 512:(kk + 1) * 512],
                                         start=(k == 0), stop=(k == 15)),
                                     inc=ginc)
                        if kk == 7:
                            gq[h] = gtok
                            uq[h] = utok
                    if tc == NCH - 1:
                        for h in range(2):
                            wfree[gs[h]] = gq[h]
                            wfree[us[h]] = uq[h]
                    flush_tr()
                    sslot = (it * NCH + tc) % 2
                    t_silu = em.op("scalar",
                                   lambda e, gb=gb, sslot=sslot: e.activation(
                                       sg[sslot][:], paf(gb), AF.Silu),
                                   waits=(gtok, sg_free[sslot]), inc="act")
                    pA_free[gb] = t_silu
                    hdst = hst[tc][:].rearrange("p (q r) -> p q r", q=4)[:, :, s4 * 128:(s4 + 1) * 128]
                    t_h = em.op("vector",
                                lambda e, sslot=sslot, ub=ub, hdst=hdst: e.tensor_tensor(
                                    hdst,
                                    sg[sslot][:].rearrange("p (j q) -> p q j", q=4),
                                    paf(ub).rearrange("p (j q) -> p q j", q=4),
                                    op=ALU.mult),
                                waits=(t_silu, utok,
                                       hst_free[tc] if s4 == 0 else None),
                                inc="dve")
                    pA_free[ub] = t_h
                    sg_free[sslot] = t_h
                    if s4 < 3:
                        continue
                    hs = hst[tc]
                    t_abs = em.op("scalar",
                                  lambda e, hs=hs: e.activation(ab[:], hs[:], AF.Abs),
                                  waits=(t_h,), inc="act")
                    q0, q1 = ab[:, 0:512], ab[:, 512:1024]
                    q2, q3 = ab[:, 1024:1536], ab[:, 1536:2048]
                    m1, n1 = mtmp[:, 0:512], mtmp[:, 512:1024]
                    m2, n2 = mtmp[:, 1024:1536], mtmp[:, 1536:2048]
                    em.op("vector", lambda e: e.tensor_tensor(m1, q0, q1, op=ALU.max), waits=(t_abs,))
                    em.op("vector", lambda e: e.tensor_tensor(n1, q0, q1, op=ALU.min))
                    em.op("vector", lambda e: e.tensor_tensor(m2, q2, q3, op=ALU.max))
                    em.op("vector", lambda e: e.tensor_tensor(n2, q2, q3, op=ALU.min))
                    em.op("vector", lambda e: e.tensor_tensor(m2, m1, m2, op=ALU.min))
                    em.op("vector", lambda e: e.tensor_tensor(n2, n1, n2, op=ALU.max))
                    em.op("vector", lambda e: e.tensor_tensor(m1, m2, n2, op=ALU.max))
                    hslot = (it * NCH + tc) % 2
                    happly = None
                    for q in range(4):
                        aq = ab[:, q * 512:(q + 1) * 512]
                        em.op("vector", lambda e, aq=aq: e.tensor_tensor(aq, aq, m1, op=ALU.is_ge))
                        happly = em.op("vector",
                                       lambda e, q=q, aq=aq, hslot=hslot, hs=hs: e.tensor_tensor(
                                           hsp[hslot][:, q * 512:(q + 1) * 512],
                                           hs[:, q * 512:(q + 1) * 512], aq, op=ALU.mult),
                                       waits=(hsp_free[hslot] if q == 0 else None,),
                                       inc="dve" if q == 3 else None)
                    hst_free[tc] = happly

                    def do_tr(IT=IT, tc=tc, hslot=hslot, happly=happly):
                        for s in range(4):
                            pbk = (IT * 4 + s) % 2
                            pb = pT[pbk]
                            ptok = None
                            for q in range(4):
                                w = [happly, pt_free[pbk]] if q == 0 else []
                                ptok = em.op("tensor",
                                             lambda e, pb=pb, q=q, s=s, hslot=hslot: e.transpose(
                                                 pb[:, q * 128:(q + 1) * 128],
                                                 hsp[hslot][:, q * 512 + s * 128: q * 512 + s * 128 + 128],
                                                 idb[:]),
                                             waits=w, inc="pe" if q == 3 else None)
                            if s == 3:
                                hsp_free[hslot] = ptok
                            cbase = IT * 16 + s * 4
                            dst = hT[:, tc * 8192 + cbase * 128: tc * 8192 + (cbase + 4) * 128]
                            ev_eng = "vector" if s % 2 == 0 else "scalar"
                            ev = em.op(ev_eng, lambda e, pb=pb, dst=dst: ecopy(e, dst, pb[:, 0:512]),
                                       waits=(ptok,), inc="dve" if ev_eng == "vector" else "act")
                            pt_free[pbk] = ev
                            hT_ready.append(ev)
                    pending_tr[0] = do_tr
            flush_tr()

            if tg + 1 < NTG:
                next_evacs = emit_ln_group(tg + 1)
            else:
                next_evacs = None

            for dq in range(4):
                yevs = []
                for d4 in range(4):
                    gd = dq * 4 + d4
                    dsl = [(2 * gd + j) % 6 for j in range(2)]
                    wdts = {}
                    for j in range(2):
                        for q4 in range(4):
                            cp0 = j * 32 + q4 * 8
                            wdts[j] = em.dma(
                                "sync",
                                wsl[dsl[j]][:].rearrange("p (cp n) -> p cp n", cp=32)[:, q4 * 8:(q4 + 1) * 8, :],
                                bass.AP(wd_s.tensor, cp0 * (128 * 16 * 128) + gd * 128,
                                        [[2048, 128], [128 * 16 * 128, 8], [1, 128]]),
                                waits=(wfree[dsl[j]], ready["wd"]) if q4 == 0 else ())
                    ybk = gd % 2
                    ytok = None
                    ytoks = {}
                    for i in range(NC16):
                        j = i // 32
                        ii = i % 32
                        w = []
                        if i == 0:
                            w = [pA_free[ybk]]
                            if gd == 0:
                                w += hT_ready
                        if ii == 0:
                            w = w + [wdts[j]]
                        yinc = "pe" if ii == 31 else None
                        rhs = hT3[:, :, i * 128:(i + 1) * 128]
                        ytok = em.op("tensor",
                                     lambda e, i=i, ybk=ybk, sl=dsl[j], ii=ii, rhs=rhs: e.matmul(
                                         paf(ybk), wsl[sl][:, ii * 128:(ii + 1) * 128], rhs,
                                         start=(i == 0), stop=(i == NC16 - 1)),
                                     waits=w, inc=yinc)
                        if ii == 31:
                            ytoks[j] = ytok
                    for j in range(2):
                        wfree[dsl[j]] = ytoks[j]
                    ev = em.op("vector",
                               lambda e, d4=d4, ybk=ybk: e.tensor_copy(
                                   out=ytsb[:, d4 * 512:(d4 + 1) * 512], in_=paf(ybk)),
                               waits=(ytok,), inc="dve")
                    pA_free[ybk] = ev
                    yevs.append(ev)
                for tc in range(NCH):
                    xslot = tc % 2
                    rows = src_ap[row_base + tc * 128: row_base + tc * 128 + 128,
                                  dq * 512:(dq + 1) * 512]
                    xld = em.dma("scalar", xb2[xslot][:, 0:512], rows, waits=(xb2_free[xslot],))
                    pbk = tc % 2
                    ptok = None
                    for j in range(4):
                        d4 = j
                        w = [yevs[-1], pB_free[pbk], ready["consts"]] if j == 0 else []
                        ptok = em.op("tensor",
                                     lambda e, pbk=pbk, j=j, d4=d4, tc=tc: e.transpose(
                                         pB[pbk][:, j * 128:(j + 1) * 128],
                                         ytsb[:, d4 * 512 + tc * 128: d4 * 512 + tc * 128 + 128],
                                         idf[:]),
                                     waits=w, inc="pe" if j == 3 else None)
                    t_res = em.op("vector",
                                  lambda e, pbk=pbk, xslot=xslot: e.scalar_tensor_tensor(
                                      out=xb2[xslot][:, 0:512],
                                      in0=pB[pbk][:], scalar=0.5,
                                      in1=xb2[xslot][:, 0:512],
                                      op0=ALU.mult, op1=ALU.add),
                                  waits=(ptok, xld), inc="dve")
                    pB_free[pbk] = t_res
                    st = em.dma("gpsimd",
                                dst_ap[row_base + tc * 128: row_base + tc * 128 + 128,
                                       dq * 512:(dq + 1) * 512],
                                xb2[xslot][:, 0:512], waits=(t_res,))
                    xb2_free[xslot] = st
            if next_evacs is not None:
                xn_evacs = next_evacs
        fs.close()

    def attn(src_ap, dst_ap):
        As = ExitStack()
        asb = lambda name, shape, dt: As.enter_context(nc.sbuf_tensor(name, shape, dt))
        xnT = asb("a_xnT", [128, 16 * 2048], BF16)
        qt_sb = asb("a_qt", [128, 4 * 2048], BF16)
        kt_sb = asb("a_kt", [128, 4 * 2048], BF16)
        v_sb = asb("a_v", [128, 16 * 512], BF16)
        wqsl = [asb(f"a_wq{i}", [128, 2048], BF16) for i in range(2)]
        wksl = [asb(f"a_wk{i}", [128, 2048], BF16) for i in range(2)]
        wvsl = asb("a_wv", [128, 16 * 512], BF16)
        pbuf = [asb(f"a_p{i}", [128, 2048], BF16) for i in range(2)]
        ptb = [asb(f"a_pt{i}", [128, 512], BF16) for i in range(2)]
        og = [asb(f"a_og{i}", [128, 512], F32) for i in range(2)]
        xb = [asb(f"a_xb{i}", [128, 2048], F32) for i in range(2)]
        xnb = [asb(f"a_xnb{i}", [128, 2048], BF16) for i in range(2)]
        rres = asb("a_r", [128, 4], F32)
        a_sq = asb("a_sq", [128, 2048], BF16)
        xrm = [asb(f"a_xrm{i}", [128, 512], F32) for i in range(2)]

        xnT3 = xnT[:].rearrange("p (tc r) -> p tc r", tc=16)
        xb_free = [None, None]
        xn_free = [None, None]
        pbuf_free = [None, None]
        ptb_free = [None, None]
        og_free = [None, None]
        r_free = [None] * 4
        xrm_free = [None, None]
        w_free = [None]
        proj_free = [None]

        for b in range(BC):
            rb = b * S
            xn_evacs_all = []
            for tc in range(16):
                slot = tc % 2
                evs = ln_chunk(src_ap, rb + tc * 128, tc, xnT,
                               xb[slot], xnb[slot], xb_free[slot], xn_free[slot],
                               tc % 2, a_sq)
                xn_free[slot] = evs[-1]
                xb_free[slot] = evs[-1]
                xn_evacs_all.extend(evs)
            qk_free = [None, None, None, None]
            for g in range(4):
                wts = []
                for chh in range(2):
                    c0_ = chh * 8
                    wts.append(em.dma(
                        "sync",
                        wvsl[:].rearrange("p (c n) -> p c n", c=16)[:, c0_:c0_ + 8, :],
                        bass.AP(wv_s.tensor, c0_ * 128 * D + g * 512,
                                [[D, 128], [128 * D, 8], [1, 512]]),
                        waits=(w_free[0], ready["qkv"]) if chh == 0 else ()))
                ev_flip = 0
                proj_evs = []
                lastmm = None
                for mi, (mat, wslr, wsrc, br, dstb) in enumerate(
                        (("q", wqsl, wq_s, bq_r, qt_sb), ("k", wksl, wk_s, bk_r, kt_sb))):
                    for oc in range(4):
                        o = g * 4 + oc
                        slot = oc % 2
                        for chh in range(2):
                            c0_ = chh * 8
                            wt = em.dma(
                                "sync",
                                wslr[slot][:].rearrange("p (c n) -> p c n", c=16)[:, c0_:c0_ + 8, :],
                                bass.AP(wsrc.tensor, c0_ * 128 * D + o * 128,
                                        [[D, 128], [128 * D, 8], [1, 128]]),
                                waits=(qk_free[mi * 2 + slot], ready["qkv"], ready["bias"]) if chh == 0 else ())
                        lastproj = None
                        for w4 in range(4):
                            bk_ = (oc * 4 + w4) % 2
                            for c in range(16):
                                w = ([pA_free[bk_], wt] + xn_evacs_all[w4 * 16:(w4 + 1) * 16]) if c == 0 else []
                                rhs = xnT3[:, w4 * 4:(w4 + 1) * 4, c * 128:(c + 1) * 128]
                                lastproj = em.op("tensor",
                                      lambda e, wslr=wslr, slot=slot, c=c, bk_=bk_, rhs=rhs: e.matmul(
                                          paf(bk_), wslr[slot][:, c * 128:(c + 1) * 128],
                                          rhs, start=(c == 0), stop=False),
                                      waits=w)
                            mtok = em.op("tensor",
                                         lambda e, br=br, oc=oc, g=g, bk_=bk_, mi=mi: e.matmul(
                                             paf(bk_), br[0:1, (g * 4 + oc) * 128:(g * 4 + oc + 1) * 128],
                                             ones_r[mi * 32: mi * 32 + 1, :], start=False, stop=True),
                                         inc="pe")
                            lastmm = mtok
                            qk_free[mi * 2 + slot] = mtok
                            ev_eng = "vector" if ev_flip % 2 == 0 else "scalar"
                            ev_flip += 1
                            dst = dstb[:, oc * 2048 + w4 * 512: oc * 2048 + (w4 + 1) * 512]
                            scl = CSC if mat == "q" else 1.0
                            if ev_eng == "vector":
                                ev = em.op(ev_eng,
                                           lambda e, bk_=bk_, dst=dst, scl=scl: e.tensor_scalar_mul(
                                               dst, paf(bk_), scl),
                                           waits=(mtok, proj_free[0]), inc="dve")
                            else:
                                ev = em.op(ev_eng,
                                           lambda e, bk_=bk_, dst=dst, scl=scl: e.activation(
                                               dst, paf(bk_), AF.Identity, scale=scl),
                                           waits=(mtok, proj_free[0]), inc="act")
                            pA_free[bk_] = ev
                            proj_evs.append(ev)
                for tc in range(16):
                    bk_ = tc % 2
                    for c in range(16):
                        w = ([pA_free[bk_]] + xn_evacs_all[tc * 4:(tc + 1) * 4]) if c == 0 else []
                        if tc == 0 and c == 0:
                            w = w + wts
                        stat = xnT[:, tc * 2048 + c * 128: tc * 2048 + (c + 1) * 128]
                        em.op("tensor",
                              lambda e, stat=stat, c=c, bk_=bk_: e.matmul(
                                  paf(bk_), stat, wvsl[:, c * 512:(c + 1) * 512],
                                  start=(c == 0), stop=False),
                              waits=w)
                    mtok = em.op("tensor",
                                 lambda e, g=g, bk_=bk_: e.matmul(
                                     paf(bk_), ones_r[64:65, 0:128], bv_r[0:1, g * 512:(g + 1) * 512],
                                     start=False, stop=True),
                                 inc="pe")
                    lastmm = mtok
                    ev_eng = "vector" if tc % 2 == 0 else "scalar"
                    ev = em.op(ev_eng,
                               lambda e, tc=tc, bk_=bk_: ecopy(
                                   e, v_sb[:, tc * 512:(tc + 1) * 512], paf(bk_)),
                               waits=(mtok, proj_free[0]),
                               inc="dve" if ev_eng == "vector" else "act")
                    pA_free[bk_] = ev
                    proj_evs.append(ev)
                w_free[0] = lastmm
                if b == 0 and g == 0:
                    dbg_dump("aqt", [128, 4 * 2048], qt_sb[:], tuple(proj_evs))
                    dbg_dump("akt", [128, 4 * 2048], kt_sb[:], tuple(proj_evs))
                    dbg_dump("av", [128, 16 * 512], v_sb[:], tuple(proj_evs))
                    dbg_dump("axnT", [128, 16 * 2048], xnT[:], tuple(proj_evs))

                otok = None
                last_otok = [None]
                pending = [None]
                oev_by_Q = {}

                def flush_pending():
                    if pending[0] is not None:
                        f = pending[0]
                        pending[0] = None
                        f()

                for Q in range(16):
                    for oc in range(4):
                        KW = (Q + 4) // 4
                        span = (Q + 1) * 128
                        par = (Q * 4 + oc) % 2
                        mc = stt[:, 32 + par * 16: 32 + (par + 1) * 16]
                        rtoks = []
                        for kw in range(KW):
                            kc = min(512, span - kw * 512)
                            w = [pA_free[kw]]
                            if Q == 0 and oc == 0 and kw == 0:
                                w += proj_evs
                            stok = em.op("tensor",
                                         lambda e, oc=oc, Q=Q, kw=kw, kc=kc: e.matmul(
                                             paf(kw, kc),
                                             qt_sb[:, oc * 2048 + Q * 128: oc * 2048 + (Q + 1) * 128],
                                             kt_sb[:, oc * 2048 + kw * 512: oc * 2048 + kw * 512 + kc],
                                             start=True, stop=(kw != KW - 1)),
                                         waits=w, inc="pe" if kw != KW - 1 else None)
                            if kw == KW - 1:
                                stok = em.op("tensor",
                                             lambda e, Q=Q: e.matmul(
                                                 pSh[:, Q * 128:(Q + 1) * 128],
                                                 mtriT[:], idb[:], start=False, stop=True),
                                             inc="pe")
                            rtoks.append(em.op(
                                "vector",
                                lambda e, kw=kw, kc=kc, mc=mc: e.reduce_max(
                                    mc[:, kw:kw + 1], paf(kw, kc), axis=AX.X, negate=True),
                                waits=(stok,), inc="dve"))
                        flush_pending()
                        tnm = em.op("vector",
                                    lambda e, mc=mc, KW=KW: e.tensor_reduce(
                                        mc[:, 8:9], mc[:, 0:KW], axis=AX.X, op=ALU.min),
                                    fwaits=tuple(rtoks), inc="dve")
                        pslot = (Q * 4 + oc) % 2
                        etoks = []
                        for kw in range(KW):
                            kc = min(512, span - kw * 512)
                            w = [tnm] + ([pbuf_free[pslot]] if kw == 0 else [])
                            et = em.op("scalar",
                                       lambda e, kw=kw, kc=kc, pslot=pslot, mc=mc: e.activation(
                                           pbuf[pslot][:, kw * 512: kw * 512 + kc],
                                           paf(kw, kc), AF.Exp,
                                           bias=mc[:, 8:9],
                                           accum_out=mc[:, 9 + kw:10 + kw]),
                                       waits=tuple(w), inc="act")
                            pA_free[kw] = et
                            etoks.append(et)
                        tsum = em.op("vector",
                                     lambda e, mc=mc, KW=KW: e.reduce_sum(
                                         mc[:, 13:14], mc[:, 9:9 + KW], axis=AX.X),
                                     waits=tuple(etoks), inc="dve")
                        rslot = (Q * 4 + oc) % 4
                        trec = em.op("vector",
                                     lambda e, mc=mc, rslot=rslot: e.reciprocal(
                                         rres[:, rslot:rslot + 1], mc[:, 13:14]),
                                     fwaits=(tsum, r_free[rslot]), inc="dve")

                        def tail(Q=Q, oc=oc, etoks=etoks, trec=trec,
                                 pslot=pslot, rslot=rslot):
                            obk = (Q * 4 + oc) % 2
                            NKT = Q + 1
                            NG = ceil_div(NKT, 4)
                            otok = None
                            ptevs = {}

                            def emit_pt(ktg):
                                nk = min(4, NKT - ktg * 4)
                                pbk = ktg % 2
                                ptok = None
                                for j in range(nk):
                                    kt = ktg * 4 + j
                                    w = [etoks[ktg], pt_free[pbk]] if j == 0 else []
                                    ptok = em.op("tensor",
                                                 lambda e, pbk=pbk, j=j, kt=kt, pslot=pslot: e.transpose(
                                                     pT[pbk][:, j * 128:(j + 1) * 128],
                                                     pbuf[pslot][:, kt * 128:(kt + 1) * 128], idb[:]),
                                                 waits=w, inc="pe" if j == nk - 1 else None)
                                ev_eng = "vector" if ktg % 2 == 0 else "scalar"
                                ptevs[ktg] = em.op(ev_eng,
                                                   lambda e, pbk=pbk, nk=nk: ecopy(
                                                       e, ptb[pbk][:, 0:nk * 128], pT[pbk][:, 0:nk * 128]),
                                                   waits=(ptok, ptb_free[pbk]),
                                                   inc="dve" if ev_eng == "vector" else "act")
                                pt_free[pbk] = ptevs[ktg]

                            def emit_o(ktg):
                                nonlocal otok
                                nk = min(4, NKT - ktg * 4)
                                pbk = ktg % 2
                                for j in range(nk):
                                    kt = ktg * 4 + j
                                    w = [ptevs[ktg]]
                                    if ktg == 0 and j == 0:
                                        w.append(pB_free[obk])
                                    otok = em.op("tensor",
                                                 lambda e, pbk=pbk, j=j, kt=kt, oc=oc, obk=obk: e.matmul(
                                                     pB[obk][:, 0:128],
                                                     ptb[pbk][:, j * 128:(j + 1) * 128],
                                                     v_sb[:, kt * 512 + oc * 128: kt * 512 + (oc + 1) * 128],
                                                     start=(kt == 0), stop=(kt == NKT - 1)),
                                                 waits=w, inc="pe" if kt == NKT - 1 else None)
                                ptb_free[pbk] = otok

                            emit_pt(0)
                            for ktg in range(1, NG):
                                emit_pt(ktg)
                                emit_o(ktg - 1)
                            emit_o(NG - 1)
                            pbuf_free[pslot] = otok
                            last_otok[0] = otok
                            oslot = Q % 2
                            oev = em.op("vector",
                                        lambda e, oc=oc, obk=obk, rslot=rslot, oslot=oslot: e.tensor_scalar(
                                            out=og[oslot][:, oc * 128:(oc + 1) * 128],
                                            in0=pB[obk][:, 0:128], scalar1=rres[:, rslot:rslot + 1],
                                            scalar2=None, op0=ALU.mult),
                                        waits=(otok, og_free[oslot] if oc == 0 else None),
                                        fwaits=(trec,), inc="dve")
                            pB_free[obk] = oev
                            r_free[rslot] = oev
                            oev_by_Q.setdefault(Q, []).append(oev)
                            if oc == 3:
                                xs = Q % 2
                                ld = em.dma("scalar", xrm[xs][:],
                                            src_ap[rb + Q * 128: rb + (Q + 1) * 128,
                                                   g * 512:(g + 1) * 512],
                                            waits=(xrm_free[xs],))
                                tadd = em.op("vector",
                                             lambda e, xs=xs, oslot=oslot: e.tensor_tensor(
                                                 xrm[xs][:], xrm[xs][:], og[oslot][:], op=ALU.add),
                                             waits=(ld, oev), inc="dve")
                                st = em.dma("gpsimd",
                                            dst_ap[rb + Q * 128: rb + (Q + 1) * 128,
                                                   g * 512:(g + 1) * 512],
                                            xrm[xs][:], waits=(tadd,))
                                xrm_free[xs] = st
                                og_free[oslot] = tadd
                        pending[0] = tail
                flush_pending()
                otok = last_otok[0]
                proj_free[0] = otok
        As.close()

    setup_consts()
    preconvert()
    if "dumpw" in parts:
        em.join()
        dump_out = nc.dram_tensor("dump_out", [16 * 128, NC16 * 128], BF16,
                                  kind="ExternalOutput").ap()
        rb = sb("dump_rb", [128, NC16 * 128], BF16)
        for blk in range(16):
            ldd = em.dma("sync", rb[:], bass.AP(wd_s.tensor, blk * (128 * NC16 * 128),
                                                [[NC16 * 128, 128], [1, NC16 * 128]]))
            std = em.dma("gpsimd", dump_out[blk * 128:(blk + 1) * 128, :], rb[:], waits=(ldd,))
            em.join()
    if "ffn1" in parts:
        ffn(x_in, xr if ("attn" in parts or "ffn2" in parts) else y_out, "f1")
        em.join()
    if "attn" in parts:
        src = xr if "ffn1" in parts else x_in
        dst = xr if "ffn2" in parts else y_out
        attn(src, dst)
        em.join()
    if "ffn2" in parts:
        src = xr if ("ffn1" in parts or "attn" in parts) else x_in
        ffn(src, y_out, "f2")
        em.join()

    for eng in ENGS:
        for q in ("qs", "qg", "qa"):
            for i in range(Em.NLANE):
                em._emit_wait(eng, f"{q}{i}", em.counts[f"{q}{i}"])
        for ln in em.private_lanes:
            em._emit_wait(eng, ln, em.counts[ln])

    with stack:
        with nc.Block() as block:
            em.finalize(block)
    return nc


_PROG_CACHE = {}


def _get_prog(parts):
    key = tuple(parts)
    if key not in _PROG_CACHE:
        _PROG_CACHE[key] = build_program(parts)
    return _PROG_CACHE[key]


def kernel(L0_attn_x, L0_attn_Wq, L0_attn_bq, L0_attn_Wk, L0_attn_bk,
           L0_attn_Wv, L0_attn_bv, L0_ffn_w_gate, L0_ffn_w_up, L0_ffn_w_down,
           parts=("ffn1", "attn", "ffn2"), core_ids=None):
    x = np.ascontiguousarray(np.asarray(L0_attn_x, dtype=np.float32))
    weights = {
        "wq": np.ascontiguousarray(np.asarray(L0_attn_Wq, np.float32)),
        "wk": np.ascontiguousarray(np.asarray(L0_attn_Wk, np.float32)),
        "wv": np.ascontiguousarray(np.asarray(L0_attn_Wv, np.float32)),
        "bq": np.ascontiguousarray(np.asarray(L0_attn_bq, np.float32)),
        "bk": np.ascontiguousarray(np.asarray(L0_attn_bk, np.float32)),
        "bv": np.ascontiguousarray(np.asarray(L0_attn_bv, np.float32)),
        "wg": np.ascontiguousarray(np.asarray(L0_ffn_w_gate, np.float32)),
        "wu": np.ascontiguousarray(np.asarray(L0_ffn_w_up, np.float32)),
        "wd": np.ascontiguousarray(np.asarray(L0_ffn_w_down, np.float32)),
    }
    nc = _get_prog(parts)
    ncore = NCORES
    in_maps = []
    for core in range(ncore):
        m = dict(weights)
        m["x_in"] = x[core * BC:(core + 1) * BC].reshape(NT, D)
        in_maps.append(m)
    res = run_bass_kernel_spmd(nc, in_maps, core_ids=list(range(ncore)))
    out = np.empty((B, S, D), np.float32)
    for core in range(ncore):
        out[core * BC:(core + 1) * BC] = res.results[core]["y_out"].reshape(BC, S, D)
    return out
